# version 07
# baseline (speedup 1.0000x reference)
import numpy as np
import os
os.environ.setdefault("TRN_ALLOW_ICI_ALL_TO_ALL", "1")
import concourse.bass as bass
import concourse.mybir as mybir
from concourse.bass_utils import run_bass_kernel_spmd
from contextlib import ExitStack

F32 = mybir.dt.float32
BF16 = mybir.dt.bfloat16
AF = mybir.ActivationFunctionType
ALU = mybir.AluOpType
AX = mybir.AxisListType

B, T, D, NH, HD, INNER = 2, 4096, 2048, 16, 128, 8192
HPC, DPC, TQ = 4, 512, 1024
ATT_SCALE = float(1.0 / np.sqrt(np.float64(HD)))
NEG = float(np.finfo(np.float32).min)
EPS_LN = 1e-5
EPS_Q = 1e-5
EPS_W = 1e-5
EPS_RMS = 1e-12
C32 = 3.0 * 2.0**22
C64 = 3.0 * 2.0**51
SQD = float(np.sqrt(np.float64(D)))
SQI = float(np.sqrt(np.float64(INNER)))
N_W1 = float(INNER * D)
N_W2 = float(D * INNER)

FUSED = True


class CEng:

    def __init__(self, step, key):
        self.step, self.key = step, key
        self.is_q = step == 16
        self.cnt = [0, 0]
        self.cur = 0
        self.e = None

    @property
    def n(self):
        return self.cnt[0] + self.cnt[1]

    def op(self, name, *a, **k):
        self.cnt[self.cur] += self.step
        return self.n

    def dma(self, out, in_, **kw):
        self.cnt[self.cur] += self.step
        return self.n

    def wait(self, M, key):
        pass

    def wait_n(self, sem_eng, n):
        pass

    def wait_self(self):
        pass

    def mark(self, M, key):
        if self.is_q:
            M[key] = ("Q", self.key, self.cur, self.cnt[self.cur],
                      self.cnt[self.cur ^ 1])
            self.cur ^= 1
        else:
            M[key] = self.cnt[0]


class REng:

    def __init__(self, step, sems, key):
        self.step, self.key = step, key
        self.is_q = step == 16
        self.sems = sems
        self.cnt = [0, 0]
        self.cur = 0
        self._need_drain = False
        self.e = None

    @property
    def n(self):
        return self.cnt[0] + self.cnt[1]

    @property
    def sem(self):
        return self.sems[0]

    def op(self, name, *a, **k):
        inst = getattr(self.e, name)(*a, **k)
        inst.then_inc(self.sems[self.cur] if self.is_q else self.sems[0],
                      self.step)
        self.cnt[self.cur] += self.step
        return self.n

    def dma(self, out, in_, **kw):
        if self.is_q and self._need_drain:
            self.e.wait_ge(self.sems[self.cur], self.cnt[self.cur])
            self._need_drain = False
        inst = self.e.dma_start(out=out, in_=in_, **kw)
        inst.then_inc(self.sems[self.cur] if self.is_q else self.sems[0],
                      self.step)
        self.cnt[self.cur] += self.step
        return self.n

    def wait(self, M, key):
        val = M[key]
        if isinstance(val, tuple) and val[0] == "Q":
            _, qkey, which, count, other = val
            self.e.wait_ge(M["_sems"][qkey][which], count)
            if other:
                self.e.wait_ge(M["_sems"][qkey][which ^ 1], other)
        else:
            self.e.wait_ge(M["_sems"][key[0]][0], val)

    def wait_n(self, sem_eng, n):
        self.e.wait_ge(sem_eng.sems[0], n)

    def wait_self(self):
        self.e.wait_ge(self.sems[0], self.cnt[0])

    def wait_total(self):
        for i in (0, 1):
            self.e.wait_ge(self.sems[i], self.cnt[i])

    def mark(self, M, key):
        if self.is_q:
            exp = M[key]
            got = ("Q", self.key, self.cur, self.cnt[self.cur],
                   self.cnt[self.cur ^ 1])
            assert exp == got, f"mark {key}: planned {exp} emitted {got}"
            self.cur ^= 1
            self._need_drain = True
        else:
            assert M[key] == self.cnt[0], (
                f"mark {key}: planned {M[key]} emitted {self.cnt[0]}")


def emit_round(v, out_bf, y, tA, tB, tS):
    v.op("tensor_scalar", out=tS, in0=y, scalar1=1e-30, scalar2=1e-30,
         op0=ALU.mult, op1=ALU.mult)
    v.wait_self()
    v.op("tensor_scalar", out=tS, in0=tS, scalar1=0.0, scalar2=None,
         op0=ALU.not_equal)
    v.op("tensor_scalar", out=tA, in0=y, scalar1=C32, scalar2=C32,
         op0=ALU.add, op1=ALU.subtract)
    v.wait_self()
    v.op("tensor_scalar", out=tB, in0=tA, scalar1=C64, scalar2=C64,
         op0=ALU.add, op1=ALU.subtract)
    v.wait_self()
    v.op("tensor_tensor", out=tB, in0=tB, in1=tA, op=ALU.subtract)
    v.wait_self()
    v.op("tensor_tensor", out=tB, in0=tB, in1=tS, op=ALU.mult)
    v.wait_self()
    v.op("tensor_tensor", out=out_bf, in0=tB, in1=tA, op=ALU.add)


def build(part):
    wantA = part in ("A", "ALL")
    wantB = part in ("B", "ALL")
    nc = bass.Bass("TRN2", target_bir_lowering=False, debug=False)

    x = nc.dram_tensor("x", [T, D], F32, kind="ExternalInput").ap()
    identf = nc.dram_tensor("identf", [128, 128], F32, kind="ExternalInput").ap()
    if wantA:
        wqT = nc.dram_tensor("wqT", [D, DPC], F32, kind="ExternalInput").ap()
        wkT = nc.dram_tensor("wkT", [D, DPC], F32, kind="ExternalInput").ap()
        wvT = nc.dram_tensor("wvT", [D, DPC], F32, kind="ExternalInput").ap()
        maskd = nc.dram_tensor("maskd", [128, 128], F32, kind="ExternalInput").ap()
        mask01 = nc.dram_tensor("mask01", [128, 128], F32, kind="ExternalInput").ap()
        qT_d = nc.dram_tensor("qT_d", [16, 128, 1024], F32, kind="Internal").ap()
        kT_d = nc.dram_tensor("kT_d", [16, 128, 1024], F32, kind="Internal").ap()
        v_d = nc.dram_tensor("v_d", [T, DPC], F32, kind="Internal").ap()
        zpk = "Internal" if part == "ALL" else "ExternalOutput"
        zparts = nc.dram_tensor("zparts", [8, DPC, TQ], F32, kind=zpk).ap()
    if wantB:
        xres = nc.dram_tensor("xres", [TQ, D], F32, kind="ExternalInput").ap()
        bmask = nc.dram_tensor("bmask", [128, 2], F32, kind="ExternalInput").ap()
        woT = nc.dram_tensor("woT", [D, D], F32, kind="ExternalInput").ap()
        w1T = nc.dram_tensor("w1T", [D, INNER], F32, kind="ExternalInput").ap()
        w2T = nc.dram_tensor("w2T", [INNER, D], F32, kind="ExternalInput").ap()
        w1q_d = nc.dram_tensor("w1q_d", [8, 16, 128, 1024], BF16, kind="Internal").ap()
        w2q_d = nc.dram_tensor("w2q_d", [4, 64, 128, 512], BF16, kind="Internal").ap()
        h_d = nc.dram_tensor("h_d", [8, 128, INNER], F32, kind="Internal").ap()
        x1_d = nc.dram_tensor("x1_d", [8, 128, D], F32, kind="Internal").ap()
        y_out = nc.dram_tensor("y", [TQ, D], F32, kind="ExternalOutput").ap()
    if part == "ALL":
        za_out = nc.dram_tensor("za_out", [8, DPC, TQ], F32, kind="Internal").ap()
    if part == "B":
        zq_in = nc.dram_tensor("zq_in", [8, DPC, TQ], F32, kind="ExternalInput").ap()
        za_out = zq_in

    ctx = ExitStack()
    with ctx:
        sidf = ctx.enter_context(nc.sbuf_tensor("sidf", [128, 128], F32))
        sidb = ctx.enter_context(nc.sbuf_tensor("sidb", [128, 128], BF16))
        smk = ctx.enter_context(nc.sbuf_tensor("smk", [128, 128], F32))
        tA = ctx.enter_context(nc.sbuf_tensor("tA", [128, 1024], F32))
        tB = ctx.enter_context(nc.sbuf_tensor("tB", [128, 1024], F32))
        tS = ctx.enter_context(nc.sbuf_tensor("tS", [128, 1024], F32))
        ybuf = ctx.enter_context(nc.sbuf_tensor("ybuf", [128, 1024], F32))
        tq1 = ctx.enter_context(nc.sbuf_tensor("tq1", [128, 1024], BF16))
        tq2 = ctx.enter_context(nc.sbuf_tensor("tq2", [128, 1024], BF16))
        st_ = ctx.enter_context(nc.sbuf_tensor("st_", [128, 128], F32))
        smb = ctx.enter_context(nc.sbuf_tensor("smb", [128, 2], F32))
        sones = ctx.enter_context(nc.sbuf_tensor("sones", [128, 1], F32))
        sones_r = ctx.enter_context(nc.sbuf_tensor("sones_r", [1, 128], F32))
        ps = [ctx.enter_context(nc.psum_tensor(f"ps{i}", [128, 512], F32))
              for i in range(6)]
        pbf = [ctx.enter_context(nc.psum_tensor(f"pbf{i}", [128, 512], BF16))
               for i in range(2)]

        SC = {k: i for i, k in enumerate(
            ["rsum", "ssq", "m", "ex2", "m2", "var", "sd", "rs", "nb",
             "nmax", "nmaxs", "psum_r", "prec",
             "sw1m", "thr1", "nthr1", "sw2m", "thr2", "nthr2",
             "n1", "rc", "rr", "mab", "mx", "mxc", "si"])}

        def sc(k):
            return st_[:, SC[k]:SC[k] + 1]

        MU1, CS1, SS2, MX2, MU2, CS2 = 26, 34, 42, 50, 58, 66

        sA1 = ExitStack()
        wq_sb = sA1.enter_context(nc.sbuf_tensor("wq_sb", [128, 16 * 512], F32))
        wk_sb = sA1.enter_context(nc.sbuf_tensor("wk_sb", [128, 16 * 512], F32))
        wv_sb = sA1.enter_context(nc.sbuf_tensor("wv_sb", [128, 16 * 512], F32))
        xc = sA1.enter_context(nc.sbuf_tensor("xc", [128, 2 * D], F32))
        xnc = sA1.enter_context(nc.sbuf_tensor("xnc", [128, 2 * D], F32))
        xnT = sA1.enter_context(nc.sbuf_tensor("xnT", [128, 16 * 256], F32))
        qev = sA1.enter_context(nc.sbuf_tensor("qev", [128, 1024], F32))
        kev = sA1.enter_context(nc.sbuf_tensor("kev", [128, 1024], F32))
        vev = sA1.enter_context(nc.sbuf_tensor("vev", [128, 1024], F32))
        sA1.close()

        sA2 = ExitStack()
        ksb = sA2.enter_context(nc.sbuf_tensor("ksb", [128, HPC * T], F32))
        vsb = sA2.enter_context(nc.sbuf_tensor("vsb", [128, 32 * 512], F32))
        qblk = sA2.enter_context(nc.sbuf_tensor("qblk", [128, 4 * 512], F32))
        ptring = sA2.enter_context(nc.sbuf_tensor("ptring", [128, 4 * 512], F32))
        sexpb = sA2.enter_context(nc.sbuf_tensor("sexpb", [128, 512], F32))
        rsums = sA2.enter_context(nc.sbuf_tensor("rsums", [128, 128], F32))
        rrow = sA2.enter_context(nc.sbuf_tensor("rrow", [1, 512], F32))
        rssb = sA2.enter_context(nc.sbuf_tensor("rssb", [128, 512], F32))
        zev = sA2.enter_context(nc.sbuf_tensor("zev", [128, 512], F32))
        sm01 = sA2.enter_context(nc.sbuf_tensor("sm01", [128, 128], F32))
        sA2.close()

        sB1 = ExitStack()
        xqT = sB1.enter_context(nc.sbuf_tensor("xqT", [128, 16 * TQ], BF16))
        sB1a = ExitStack()
        zqsb = sB1a.enter_context(nc.sbuf_tensor("zqsb", [128, 16 * TQ], F32))
        ztmp = sB1a.enter_context(nc.sbuf_tensor("ztmp", [128, 2 * TQ], F32))
        wosb = sB1a.enter_context(nc.sbuf_tensor("wosb", [128, 16 * 256], F32))
        xrb = sB1a.enter_context(nc.sbuf_tensor("xrb", [128, 2 * 256], F32))
        x1sb = sB1a.enter_context(nc.sbuf_tensor("x1sb", [128, 2 * 256], F32))
        x1row = sB1a.enter_context(nc.sbuf_tensor("x1row", [128, 2 * D], F32))
        xn2 = sB1a.enter_context(nc.sbuf_tensor("xn2", [128, D], F32))
        xqb = sB1a.enter_context(nc.sbuf_tensor("xqb", [128, D], BF16))
        sB1a.close()
        w1sb = sB1.enter_context(nc.sbuf_tensor("w1sb", [128, 16 * 1024], BF16))
        hb = sB1.enter_context(nc.sbuf_tensor("hb", [128, 4 * 512], F32))
        sB1.close()

        sB2 = ExitStack()
        xq2T = sB2.enter_context(nc.sbuf_tensor("xq2T", [128, 64 * 512], BF16))
        w2sb = sB2.enter_context(nc.sbuf_tensor("w2sb", [128, 64 * 512], BF16))
        hrow = sB2.enter_context(nc.sbuf_tensor("hrow", [128, 2 * 1024], F32))
        xq2b = sB2.enter_context(nc.sbuf_tensor("xq2b", [128, 1024], BF16))
        xr2 = sB2.enter_context(nc.sbuf_tensor("xr2", [128, 2 * 512], F32))
        yo = sB2.enter_context(nc.sbuf_tensor("yo", [128, 2 * 512], F32))
        tmp5 = sB2.enter_context(nc.sbuf_tensor("tmp5", [128, 512], F32))
        sB2.close()

        p0A, p0B = [], []
        if wantB:
            for ci in range(16):
                for cg in range(8):
                    p0A.append(("w1", w1T[ci * 128:(ci + 1) * 128,
                                          cg * 1024:(cg + 1) * 1024]))
            for ci in range(64):
                for cg in range(2):
                    p0A.append(("w2", w2T[ci * 128:(ci + 1) * 128,
                                          cg * 1024:(cg + 1) * 1024]))
            for ci in range(16):
                for cg in range(8):
                    p0B.append(("w1", w1T[ci * 128:(ci + 1) * 128,
                                          cg * 1024:(cg + 1) * 1024],
                                ("w1", cg, ci)))
            for ci in range(64):
                for cg in range(2):
                    p0B.append(("w2", w2T[ci * 128:(ci + 1) * 128,
                                          cg * 1024:(cg + 1) * 1024],
                                ("w2", ci, cg)))

        def program(EN, M, mode):
            q, g, t, v, s = EN["q"], EN["g"], EN["t"], EN["v"], EN["s"]

            p0_state = {"gA": 0, "vA": 0, "gB": 0, "vB": 0}

            def p0_g_passA(nunits):
                p_ = EN["p"]
                for _ in range(nunits):
                    i = p0_state["gA"]
                    if i >= len(p0A):
                        return
                    p0_state["gA"] += 1
                    if i >= 2:
                        p_.wait(M, ("v", "p0A", i - 2))
                    p_.dma((tA, tB)[i % 2][:], p0A[i][1])
                    p_.mark(M, ("g", "p0A", i))

            def p0_v_passA(nunits):
                for _ in range(nunits):
                    i = p0_state["vA"]
                    if i >= len(p0A):
                        return
                    p0_state["vA"] += 1
                    col = 74 if p0A[i][0] == "w1" else 75
                    v.wait(M, ("g", "p0A", i))
                    v.op("reduce_sum", out=sc("rsum"), in_=(tA, tB)[i % 2][:],
                         axis=AX.X, apply_absolute_value=True)
                    v.wait_self()
                    v.op("tensor_tensor", out=st_[:, col:col + 1],
                         in0=st_[:, col:col + 1], in1=sc("rsum"), op=ALU.add)
                    v.wait_self()
                    v.mark(M, ("v", "p0A", i))

            def p0_g_passB(nunits):
                p_ = EN["p"]
                for _ in range(nunits):
                    i = p0_state["gB"]
                    if i >= len(p0B):
                        return
                    p0_state["gB"] += 1
                    if i >= 2:
                        p_.wait(M, ("v", "p0B", i - 2))
                    else:
                        p_.wait(M, ("v", "p0A", len(p0A) - 1))
                    p_.dma((tA, tB)[i % 2][:], p0B[i][1])
                    p_.mark(M, ("g", "p0Bld", i))
                    j = i - 1
                    if j >= 0:
                        p0_g_store(j)

            def p0_g_store(j):
                p_ = EN["p"]
                which, _, dst = p0B[j]
                p_.wait(M, ("v", "p0B", j))
                tb = (tq1, tq2)[j % 2]
                if which == "w1":
                    _, cg, ci = dst
                    p_.dma(w1q_d[cg, ci], tb[:])
                else:
                    _, ci, cg = dst
                    for dd in range(2):
                        p_.dma(w2q_d[cg * 2 + dd, ci], tb[:, dd * 512:(dd + 1) * 512])
                p_.mark(M, ("g", "p0Bst", j))

            def p0_v_passB(nunits):
                for _ in range(nunits):
                    i = p0_state["vB"]
                    if i >= len(p0B):
                        return
                    p0_state["vB"] += 1
                    which = p0B[i][0]
                    thr = sc("thr1") if which == "w1" else sc("thr2")
                    nthr = sc("nthr1") if which == "w1" else sc("nthr2")
                    v.wait(M, ("g", "p0Bld", i))
                    if i >= 2:
                        v.wait(M, ("g", "p0Bst", i - 2))
                    buf = (tA, tB)[i % 2]
                    v.op("tensor_scalar", out=ybuf[:], in0=buf[:], scalar1=thr,
                         scalar2=None, op0=ALU.is_gt)
                    v.op("tensor_scalar", out=tS[:], in0=buf[:], scalar1=nthr,
                         scalar2=None, op0=ALU.is_lt)
                    v.wait_self()
                    v.op("tensor_tensor", out=(tq1, tq2)[i % 2][:], in0=tS[:],
                         in1=ybuf[:], op=ALU.subtract)
                    v.wait_self()
                    v.mark(M, ("v", "p0B", i))

            def sec(engkey, fn):
                if mode == "plan":
                    fn(EN[engkey])
                else:
                    getattr(block, {"q": "sync", "g": "gpsimd", "t": "tensor",
                                    "v": "vector", "s": "scalar"}[engkey])(
                        lambda e: fn(EN[engkey].__setattr__("e", e) or EN[engkey]))

            def init_q(q):
                q.dma(sidf[:], identf)
                if wantA:
                    q.dma(smk[:], maskd)
                if wantB:
                    q.dma(smb[:], bmask)
                q.mark(M, ("q", "init"))
            sec("q", init_q)

            def init_v(v):
                v.wait(M, ("q", "init"))
                v.op("tensor_copy", out=sidb[:], in_=sidf[:])
                v.op("memset", sones[:], 1.0)
                v.op("memset", sones_r[:], 1.0)
                if wantB:
                    v.op("memset", st_[:, 74:76], 0.0)
                v.wait_self()
                v.mark(M, ("v", "init"))
            sec("v", init_v)

            NCH = 16
            if wantA:
                def p1_q(q):
                    for dc in range(16):
                        q.dma(wq_sb[:, dc * 512:(dc + 1) * 512],
                              wqT[dc * 128:(dc + 1) * 128, :])
                        q.dma(wk_sb[:, dc * 512:(dc + 1) * 512],
                              wkT[dc * 128:(dc + 1) * 128, :])
                        q.dma(wv_sb[:, dc * 512:(dc + 1) * 512],
                              wvT[dc * 128:(dc + 1) * 128, :])
                    q.mark(M, ("q", "ldw"))
                    for ci in range(NCH):
                        if ci >= 1:
                            q.wait(M, ("s", "p1apply", ci - 1))
                        for tt in range(2):
                            q.dma(xc[:, tt * D:(tt + 1) * D],
                                  x[ci * 256 + tt * 128: ci * 256 + (tt + 1) * 128])
                            q.mark(M, ("q", "p1x", ci, tt))
                sec("q", p1_q)

                def p1_s(s):
                    for ci in range(NCH):
                        for tt in range(2):
                            xt = xc[:, tt * D:(tt + 1) * D]
                            s.wait(M, ("v", "p1var", ci, tt))
                            s.op("activation", out=sc("sd"), in_=sc("var"),
                                 func=AF.Sqrt, bias=0.0, scale=1.0)
                            s.mark(M, ("s", "p1sd", ci, tt))
                            s.wait(M, ("v", "p1nb", ci, tt))
                            if ci >= 1 and tt == 0:
                                s.wait(M, ("t", "p1tr", ci - 1, 7))
                            s.op("activation", out=xnc[:, tt * D:(tt + 1) * D],
                                 in_=xt, func=AF.Identity, scale=sc("rs"),
                                 bias=sc("nb"))
                            s.mark(M, ("s", "p1applyt", ci, tt))
                        s.mark(M, ("s", "p1apply", ci))
                        s.wait(M, ("t", "p1mm", ci))
                        if ci >= 1:
                            s.wait(M, ("g", "p1sp", ci - 1))
                        for h in range(4):
                            bank, half = ps[h // 2], (h % 2) * 256
                            s.op("activation",
                                 out=qev[:, h * 256:(h + 1) * 256],
                                 in_=bank[:, half:half + 256], func=AF.Copy,
                                 bias=0.0, scale=ATT_SCALE)
                        for h in range(4):
                            bank, half = ps[2 + h // 2], (h % 2) * 256
                            s.op("activation", out=kev[:, h * 256:(h + 1) * 256],
                                 in_=bank[:, half:half + 256], func=AF.Copy,
                                 bias=0.0, scale=1.0)
                        for tt in range(2):
                            s.op("activation", out=vev[:, tt * 512:(tt + 1) * 512],
                                 in_=ps[4 + tt][:], func=AF.Copy,
                                 bias=0.0, scale=1.0)
                        s.wait_self()
                        s.mark(M, ("s", "p1evs", ci))
                        if wantB:
                            EN["p"].e = s.e
                            p0_g_passA(16)
                sec("s", p1_s)

                def p1_v(v):
                    v.wait(M, ("v", "init"))
                    for ci in range(NCH):
                        for tt in range(2):
                            v.wait(M, ("q", "p1x", ci, tt))
                            xt = xc[:, tt * D:(tt + 1) * D]
                            for bq in range(4):
                                v.op("bn_stats", out=st_[:, 96 + 6 * bq:
                                                         96 + 6 * bq + 6],
                                     in_=xt[:, bq * 512:(bq + 1) * 512])
                            v.wait_self()
                            v.op("bn_aggr", out=st_[:, 86:88],
                                 in_=st_[:, 96:120])
                            v.wait_self()
                            v.op("tensor_scalar", out=sc("var"),
                                 in0=st_[:, 87:88], scalar1=EPS_LN,
                                 scalar2=None, op0=ALU.add)
                            v.op("tensor_copy", out=sc("m"), in_=st_[:, 86:87])
                            v.wait_self()
                            v.mark(M, ("v", "p1var", ci, tt))
                            v.wait(M, ("s", "p1sd", ci, tt))
                            v.op("reciprocal", out=sc("rs"), in_=sc("sd"))
                            v.wait_self()
                            v.op("scalar_tensor_tensor", out=sc("nb"), in0=sc("m"),
                                 scalar=-1.0, in1=sc("rs"), op0=ALU.mult,
                                 op1=ALU.mult)
                            v.wait_self()
                            v.mark(M, ("v", "p1nb", ci, tt))
                        if ci >= 1:
                            v.wait(M, ("t", "p1mm", ci - 1))
                        for grp in range(8):
                            v.wait(M, ("t", "p1tr", ci, grp))
                            tt0 = grp // 4
                            for k in range(4):
                                dc = (grp % 4) * 4 + k
                                v.op("tensor_copy",
                                     out=xnT[:, dc * 256 + tt0 * 128:
                                             dc * 256 + tt0 * 128 + 128],
                                     in_=ps[4][:, k * 128:(k + 1) * 128])
                            v.mark(M, ("v", "p1trev", ci, grp))
                        if wantB:
                            p0_v_passA(16)
                sec("v", p1_v)

                def p1_t(t):
                    t.wait(M, ("q", "ldw"))
                    for ci in range(NCH):
                        for grp in range(8):
                            tt0 = grp // 4
                            t.wait(M, ("s", "p1applyt", ci, tt0))
                            if (ci, grp) != (0, 0):
                                pci, pgrp = (ci, grp - 1) if grp else (ci - 1, 7)
                                t.wait(M, ("v", "p1trev", pci, pgrp))
                            if grp == 0 and ci >= 1:
                                t.wait(M, ("s", "p1evs", ci - 1))
                            for k in range(4):
                                dc = (grp % 4) * 4 + k
                                t.op("transpose",
                                     out=ps[4][:, k * 128:(k + 1) * 128],
                                     in_=xnc[:, tt0 * D + dc * 128:
                                             tt0 * D + (dc + 1) * 128],
                                     identity=sidf[:])
                            t.mark(M, ("t", "p1tr", ci, grp))
                        t.wait(M, ("v", "p1trev", ci, 7))
                        if ci >= 1:
                            t.wait(M, ("s", "p1evs", ci - 1))
                        for wsb, b0 in ((wq_sb, 0), (wk_sb, 2)):
                            for h in range(4):
                                bank, half = ps[b0 + h // 2], (h % 2) * 256
                                for dc in range(16):
                                    t.op("matmul", out=bank[:, half:half + 256],
                                         lhsT=wsb[:, dc * 512 + h * 128:
                                                  dc * 512 + (h + 1) * 128],
                                         rhs=xnT[:, dc * 256:(dc + 1) * 256],
                                         start=(dc == 0), stop=(dc == 15))
                        for tt in range(2):
                            for dc in range(16):
                                t.op("matmul", out=ps[4 + tt][:],
                                     lhsT=xnT[:, dc * 256 + tt * 128:
                                              dc * 256 + tt * 128 + 128],
                                     rhs=wv_sb[:, dc * 512:(dc + 1) * 512],
                                     start=(dc == 0), stop=(dc == 15))
                        t.mark(M, ("t", "p1mm", ci))
                sec("t", p1_t)

                def p1_g(g):
                    for ci in range(NCH):
                        g.wait(M, ("s", "p1evs", ci))
                        g.dma(qT_d[ci], qev[:])
                        g.dma(kT_d[ci], kev[:])
                        for tt in range(2):
                            g.dma(v_d[ci * 256 + tt * 128:ci * 256 + (tt + 1) * 128, :],
                                  vev[:, tt * 512:(tt + 1) * 512])
                        g.mark(M, ("g", "p1sp", ci))
                sec("g", p1_g)

                def units2():
                    for qb in range(8):
                        for h in range(4):
                            for i in range(4):
                                yield qb, h, i

                def p2_q(q):
                    q.wait(M, ("g", "p1sp", NCH - 1))
                    q.dma(sm01[:], mask01)
                    for ci in range(16):
                        for h in range(4):
                            q.dma(ksb[:, h * T + ci * 256:h * T + (ci + 1) * 256],
                                  kT_d[ci][:, h * 256:(h + 1) * 256])
                    for j in range(32):
                        q.dma(vsb[:, j * 512:(j + 1) * 512],
                              v_d[j * 128:(j + 1) * 128, :])
                    q.mark(M, ("q", "p2kv"))
                    for qb in range(8):
                        if qb >= 1:
                            q.wait(M, ("t", "p2send", qb - 1))
                        for h in range(4):
                            for hf in range(2):
                                q.dma(qblk[:, h * 512 + hf * 256:
                                           h * 512 + hf * 256 + 256],
                                      qT_d[2 * qb + hf][:, h * 256:(h + 1) * 256])
                        q.mark(M, ("q", "p2q", qb))
                sec("q", p2_q)

                if wantB:
                    def p2r_t(t):
                        t.wait(M, ("v", "p0A", len(p0A) - 1))
                        t.op("matmul", out=ps[2][0:1, 0:2], lhsT=sones[:],
                             rhs=st_[:, 74:76], start=True, stop=True)
                        t.mark(M, ("t", "redA1"))
                        t.wait(M, ("v", "redAsc"))
                        t.op("matmul", out=ps[2][:, 4:6], lhsT=sones_r[:],
                             rhs=st_[0:1, 76:78], start=True, stop=True)
                        t.mark(M, ("t", "redA2"))
                    sec("t", p2r_t)

                    def p2r_v(v):
                        v.wait(M, ("t", "redA1"))
                        v.op("tensor_scalar", out=st_[0:1, 76:77],
                             in0=ps[2][0:1, 0:1], scalar1=1.0 / N_W1,
                             scalar2=EPS_W, op0=ALU.mult, op1=ALU.max)
                        v.op("tensor_scalar", out=st_[0:1, 77:78],
                             in0=ps[2][0:1, 1:2], scalar1=1.0 / N_W2,
                             scalar2=EPS_W, op0=ALU.mult, op1=ALU.max)
                        v.wait_self()
                        v.mark(M, ("v", "redAsc"))
                        v.wait(M, ("t", "redA2"))
                        v.op("tensor_copy", out=sc("sw1m"), in_=ps[2][:, 4:5])
                        v.op("tensor_copy", out=sc("sw2m"), in_=ps[2][:, 5:6])
                        v.wait_self()
                        v.op("tensor_scalar", out=sc("thr1"), in0=sc("sw1m"),
                             scalar1=0.5, scalar2=None, op0=ALU.mult)
                        v.op("tensor_scalar", out=sc("thr2"), in0=sc("sw2m"),
                             scalar1=0.5, scalar2=None, op0=ALU.mult)
                        v.wait_self()
                        v.op("tensor_scalar", out=sc("nthr1"), in0=sc("thr1"),
                             scalar1=-1.0, scalar2=None, op0=ALU.mult)
                        v.op("tensor_scalar", out=sc("nthr2"), in0=sc("thr2"),
                             scalar1=-1.0, scalar2=None, op0=ALU.mult)
                        v.wait_self()
                        v.mark(M, ("v", "redA"))
                    sec("v", p2r_v)

                rlist = M.setdefault("_rlist", [])

                def _prev_rchunk(qb, h, i, j):
                    idx = rindex.get((qb, h, i, j), 0)
                    return rlist[idx - 1] if idx >= 1 else None

                rindex = M.setdefault("_rindex", {})

                def p2_t(t):
                    t.wait(M, ("q", "p2kv"))
                    rch = 0
                    for qb in range(8):
                        t.wait(M, ("q", "p2q", qb))
                        NT = 4 * qb + 4
                        for h in range(4):
                            u = qb * 4 + h
                            zbank = ps[4 + (u % 2)]
                            if u >= 1:
                                t.wait(M, ("v", "r2T", (u - 1) // 4, (u - 1) % 4))
                                t.wait(M, ("v", "r2B", (u - 1) // 4, (u - 1) % 4))
                            for i in range(4):
                                qt = qb * 4 + i
                                L = qt + 1
                                nch = (L + 3) // 4
                                for j in range(nch):
                                    wdt = min(512, L * 128 - j * 512)
                                    if rch >= 2:
                                        t.wait(M, ("s", "r2e") + rlist[rch - 2])
                                    t.op("matmul", out=ps[rch % 2][:, 0:wdt],
                                         lhsT=qblk[:, h * 512 + i * 128:
                                                   h * 512 + (i + 1) * 128],
                                         rhs=ksb[:, h * T + j * 512:
                                                 h * T + j * 512 + wdt],
                                         start=True, stop=True)
                                    t.mark(M, ("t", "r2s", qb, h, i, j))
                                    if mode == "plan":
                                        rlist.append((qb, h, i, j))
                                        rindex[(qb, h, i, j)] = len(rlist) - 1
                                    rch += 1
                            if u >= 2:
                                t.wait(M, ("v", "p2zev", (u - 2) // 4, (u - 2) % 4))

                            def _pv(j):
                                if j >= 4 * qb:
                                    t.wait(M, ("v", "x2m", qb, h, j))
                                else:
                                    t.wait(M, ("s", "x2e", qb, h, j))
                                t.op("matmul", out=zbank[:, 0:512],
                                     lhsT=vsb[:, j * 512 + h * 128:
                                             j * 512 + (h + 1) * 128],
                                     rhs=ptring[:, (j % 4) * 512:
                                               (j % 4) * 512 + 512],
                                     start=(j == 0), stop=(j == NT - 1))
                                t.mark(M, ("t", "x2pv", qb, h, j))

                            for j in range(NT):
                                if j >= 2:
                                    t.wait(M, ("s", "x2e", qb, h, j - 2))
                                t.op("matmul", out=ps[2 + j % 2][:, 0:512],
                                     lhsT=ksb[:, h * T + j * 128:
                                             h * T + (j + 1) * 128],
                                     rhs=qblk[:, h * 512:(h + 1) * 512],
                                     start=True, stop=True)
                                t.mark(M, ("t", "x2s", qb, h, j))
                                if j >= 1:
                                    _pv(j - 1)
                            _pv(NT - 1)
                            t.wait(M, ("v", "r2r", qb, h))
                            t.wait(M, ("s", "r2e") + rlist[rch - 1])
                            for i in range(4):
                                t.op("matmul",
                                     out=ps[0][0:1, i * 128:(i + 1) * 128],
                                     lhsT=rsums[:, 8 + i:9 + i], rhs=sidf[:],
                                     start=True, stop=True)
                            t.mark(M, ("t", "r2t", qb, h))
                            t.wait(M, ("v", "r2T", qb, h))
                            t.op("matmul", out=ps[1][:, 0:512],
                                 lhsT=sones_r[:], rhs=rrow[0:1, 0:512],
                                 start=True, stop=True)
                            t.mark(M, ("t", "r2b", qb, h))
                        t.mark(M, ("t", "p2send", qb))
                sec("t", p2_t)

                def p2_v(v):
                    v.wait(M, ("g", "p1sp", NCH - 1))
                    v.op("memset", rsums[:], 0.0)
                    v.wait_self()
                    for qb in range(8):
                        NT = 4 * qb + 4
                        for h in range(4):
                            u = qb * 4 + h
                            for i in range(4):
                                qt = qb * 4 + i
                                L = qt + 1
                                nch = (L + 3) // 4
                                wdt = L * 128 - (nch - 1) * 512
                                gp = rindex[(qb, h, i, nch - 1)] % 2
                                for j in range(nch):
                                    if j == nch - 1:
                                        v.wait(M, ("t", "r2s", qb, h, i, j))
                                        v.op("tensor_tensor",
                                             out=ps[gp][:, wdt - 128:wdt],
                                             in0=ps[gp][:, wdt - 128:wdt],
                                             in1=smk[:], op=ALU.add)
                                        v.wait_self()
                                        v.mark(M, ("v", "r2m", qb, h, i))
                                    v.wait(M, ("s", "r2e", qb, h, i, j))
                                    if j == 0:
                                        v.op("tensor_copy",
                                             out=rsums[:, i:i + 1],
                                             in_=sc("psum_r"))
                                    else:
                                        v.op("tensor_tensor",
                                             out=rsums[:, i:i + 1],
                                             in0=rsums[:, i:i + 1],
                                             in1=sc("psum_r"), op=ALU.add)
                                    v.wait_self()
                                    v.mark(M, ("v", "r2a", qb, h, i, j))
                            for j in range(4 * qb, NT):
                                ji = j - 4 * qb
                                v.wait(M, ("s", "x2e", qb, h, j))
                                for ci in range(4):
                                    dst = ptring[:, (j % 4) * 512 + ci * 128:
                                                 (j % 4) * 512 + (ci + 1) * 128]
                                    if ci < ji:
                                        v.op("memset", dst, 0.0)
                                    elif ci == ji:
                                        v.op("tensor_tensor", out=dst, in0=dst,
                                             in1=sm01[:], op=ALU.mult)
                                v.wait_self()
                                v.mark(M, ("v", "x2m", qb, h, j))
                            if u >= 1:
                                v.wait(M, ("t", "r2t", (u - 1) // 4, (u - 1) % 4))
                            v.op("reciprocal", out=rsums[:, 8:12],
                                 in_=rsums[:, 0:4])
                            v.wait_self()
                            v.mark(M, ("v", "r2r", qb, h))
                            v.wait(M, ("t", "r2t", qb, h))
                            v.op("tensor_copy", out=rrow[0:1, 0:512],
                                 in_=ps[0][0:1, 0:512])
                            v.wait_self()
                            v.mark(M, ("v", "r2T", qb, h))
                            v.wait(M, ("t", "r2b", qb, h))
                            v.op("tensor_copy", out=rssb[:], in_=ps[1][:, 0:512])
                            v.wait_self()
                            v.mark(M, ("v", "r2B", qb, h))
                            v.wait(M, ("t", "x2pv", qb, h, NT - 1))
                            if u >= 1:
                                pu = u - 1
                                v.wait(M, ("g", "p2z", pu // 4, pu % 4))
                            v.op("tensor_tensor", out=zev[:],
                                 in0=ps[4 + (u % 2)][:, 0:512], in1=rssb[:],
                                 op=ALU.mult)
                            v.wait_self()
                            v.mark(M, ("v", "p2zev", qb, h))
                        if wantB:
                            p0_v_passB(32)
                sec("v", p2_v)

                def p2_s(s):
                    for qb in range(8):
                        NT = 4 * qb + 4
                        for h in range(4):
                            u = qb * 4 + h
                            for i in range(4):
                                qt = qb * 4 + i
                                L = qt + 1
                                nch = (L + 3) // 4
                                for j in range(nch):
                                    wdt = min(512, L * 128 - j * 512)
                                    t_key = ("t", "r2s", qb, h, i, j)
                                    s.wait(M, t_key)
                                    if j == nch - 1:
                                        s.wait(M, ("v", "r2m", qb, h, i))
                                    pk = _prev_rchunk(qb, h, i, j)
                                    if pk is not None:
                                        s.wait(M, ("v", "r2a", *pk))
                                    gp = rindex[(qb, h, i, j)] % 2
                                    s.op("activation", out=sexpb[:, 0:wdt],
                                         in_=ps[gp][:, 0:wdt], func=AF.Exp,
                                         bias=0.0, scale=1.0,
                                         accum_out=sc("psum_r"))
                                    s.mark(M, ("s", "r2e", qb, h, i, j))
                            for j in range(NT):
                                s.wait(M, ("t", "x2s", qb, h, j))
                                if j >= 4:
                                    s.wait(M, ("t", "x2pv", qb, h, j - 4))
                                elif u >= 1:
                                    pu = u - 1
                                    pNT = 4 * (pu // 4) + 4
                                    s.wait(M, ("t", "x2pv", pu // 4, pu % 4,
                                               pNT - 1))
                                s.op("activation",
                                     out=ptring[:, (j % 4) * 512:
                                               (j % 4) * 512 + 512],
                                     in_=ps[2 + j % 2][:, 0:512], func=AF.Exp,
                                     bias=0.0, scale=1.0)
                                s.mark(M, ("s", "x2e", qb, h, j))
                        if wantB:
                            EN["p"].e = s.e
                            p0_g_passB(32)
                sec("s", p2_s)

                def p2_g(g):
                    for qb in range(8):
                        d1, cofs = qb // 2, (qb % 2) * 512
                        for h in range(4):
                            g.wait(M, ("v", "p2zev", qb, h))
                            g.dma(zparts[d1][h * 128:(h + 1) * 128,
                                             cofs:cofs + 512], zev[:])
                            g.dma(zparts[4 + d1][h * 128:(h + 1) * 128,
                                                 cofs:cofs + 512], zev[:])
                            g.mark(M, ("g", "p2z", qb, h))
                sec("g", p2_g)

                if wantB:
                    def p0_drain_v(v):
                        p0_v_passB(len(p0B))
                    def p0_drain_p(s):
                        EN["p"].e = s.e
                        p0_g_passB(len(p0B))
                        p0_g_store(len(p0B) - 1)
                        EN["p"].mark(M, ("g", "p0done"))
                    sec("v", p0_drain_v)
                    sec("s", p0_drain_p)
            elif wantB:
                def p0_all_g(s):
                    EN["p"].e = s.e
                    EN["p"].wait(M, ("v", "init"))
                    p0_g_passA(len(p0A))
                sec("s", p0_all_g)

                def p0_all_v(v):
                    p0_v_passA(len(p0A))
                sec("v", p0_all_v)

                def pBr_t(t):
                    t.wait(M, ("v", "p0A", len(p0A) - 1))
                    t.op("matmul", out=ps[2][0:1, 0:2], lhsT=sones[:],
                         rhs=st_[:, 74:76], start=True, stop=True)
                    t.mark(M, ("t", "redA1"))
                    t.wait(M, ("v", "redAsc"))
                    t.op("matmul", out=ps[2][:, 4:6], lhsT=sones_r[:],
                         rhs=st_[0:1, 76:78], start=True, stop=True)
                    t.mark(M, ("t", "redA2"))
                sec("t", pBr_t)

                def pBr_v(v):
                    v.wait(M, ("t", "redA1"))
                    v.op("tensor_scalar", out=st_[0:1, 76:77],
                         in0=ps[2][0:1, 0:1], scalar1=1.0 / N_W1,
                         scalar2=EPS_W, op0=ALU.mult, op1=ALU.max)
                    v.op("tensor_scalar", out=st_[0:1, 77:78],
                         in0=ps[2][0:1, 1:2], scalar1=1.0 / N_W2,
                         scalar2=EPS_W, op0=ALU.mult, op1=ALU.max)
                    v.wait_self()
                    v.mark(M, ("v", "redAsc"))
                    v.wait(M, ("t", "redA2"))
                    v.op("tensor_copy", out=sc("sw1m"), in_=ps[2][:, 4:5])
                    v.op("tensor_copy", out=sc("sw2m"), in_=ps[2][:, 5:6])
                    v.wait_self()
                    v.op("tensor_scalar", out=sc("thr1"), in0=sc("sw1m"),
                         scalar1=0.5, scalar2=None, op0=ALU.mult)
                    v.op("tensor_scalar", out=sc("thr2"), in0=sc("sw2m"),
                         scalar1=0.5, scalar2=None, op0=ALU.mult)
                    v.wait_self()
                    v.op("tensor_scalar", out=sc("nthr1"), in0=sc("thr1"),
                         scalar1=-1.0, scalar2=None, op0=ALU.mult)
                    v.op("tensor_scalar", out=sc("nthr2"), in0=sc("thr2"),
                         scalar1=-1.0, scalar2=None, op0=ALU.mult)
                    v.wait_self()
                    v.mark(M, ("v", "redA"))
                    p0_v_passB(len(p0B))
                sec("v", pBr_v)

                def p0B_g(s):
                    EN["p"].e = s.e
                    p0_g_passB(len(p0B))
                    p0_g_store(len(p0B) - 1)
                    EN["p"].mark(M, ("g", "p0done"))
                sec("s", p0B_g)

            if part == "ALL":
                def ag_g(g):
                    g.wait(M, ("g", "p2z", 7, 3))
                    cc_ = EN["cc"]
                    if mode == "plan":
                        cc_.cnt[0] += 1
                    else:
                        cc_.e = g.e
                        inst = g.e.collective_compute(
                            "AllToAll", ALU.bypass,
                            replica_groups=[list(range(8))],
                            ins=[zparts.opt()], outs=[za_out.opt()])
                        inst.then_inc(cc_.sems[0], 1)
                        cc_.cnt[0] += 1
                    M[("cc", "a2a")] = EN["cc"].cnt[0]
                sec("g", ag_g)

            if wantB:
                def p3_q(q):
                    if wantA:
                        q.wait(M, ("t", "p2send", 7))
                        q.wait(M, ("v", "p2zev", 7, 3))
                        q.wait(M, ("s", "x2e", 7, 3, 31))
                    if part == "ALL":
                        q.wait_n(EN["cc"], M[("cc", "a2a")])
                    for rb in range(4):
                        for pt in range(4):
                            c = rb * 4 + pt
                            if c >= 1:
                                q.wait(M, ("v", "p3cmb", c - 1))
                            q.dma(ztmp[:, 0:TQ],
                                  za_out[rb][pt * 128:(pt + 1) * 128, :])
                            q.dma(ztmp[:, TQ:2 * TQ],
                                  za_out[4 + rb][pt * 128:(pt + 1) * 128, :])
                            q.mark(M, ("q", "p3z", c))
                    for d in range(8):
                        if d >= 1:
                            q.wait(M, ("t", "p3mm", d - 1, 7))
                        for c in range(16):
                            q.dma(wosb[:, c * 256:(c + 1) * 256],
                                  woT[c * 128:(c + 1) * 128,
                                      d * 256:(d + 1) * 256])
                        q.mark(M, ("q", "p3wo", d))
                        for ti in range(8):
                            u = d * 8 + ti
                            if u >= 2:
                                q.wait(M, ("v", "p3ev", (u - 2) // 8, (u - 2) % 8))
                            q.dma(xrb[:, (ti % 2) * 256:(ti % 2) * 256 + 256],
                                  xres[ti * 128:(ti + 1) * 128,
                                       d * 256:(d + 1) * 256])
                            q.mark(M, ("q", "p3xr", d, ti))
                sec("q", p3_q)

                def p3comb_v(v):
                    for rb in range(4):
                        for pt in range(4):
                            c = rb * 4 + pt
                            v.wait(M, ("q", "p3z", c))
                            v.op("tensor_scalar", out=zqsb[:, c * TQ:(c + 1) * TQ],
                                 in0=ztmp[:, 0:TQ], scalar1=smb[:, 0:1],
                                 scalar2=None, op0=ALU.mult)
                            v.op("tensor_scalar", out=ztmp[:, TQ:2 * TQ],
                                 in0=ztmp[:, TQ:2 * TQ], scalar1=smb[:, 1:2],
                                 scalar2=None, op0=ALU.mult)
                            v.wait_self()
                            v.op("tensor_tensor",
                                 out=zqsb[:, c * TQ:(c + 1) * TQ],
                                 in0=zqsb[:, c * TQ:(c + 1) * TQ],
                                 in1=ztmp[:, TQ:2 * TQ], op=ALU.add)
                            v.wait_self()
                            v.mark(M, ("v", "p3cmb", c))
                sec("v", p3comb_v)

                def p3_t(t):
                    t.wait(M, ("v", "p3cmb", 15))
                    for d in range(8):
                        t.wait(M, ("q", "p3wo", d))
                        for ti in range(8):
                            u = d * 8 + ti
                            bank = ps[u % 6]
                            if u >= 6:
                                pu = u - 6
                                t.wait(M, ("v", "p3ev", pu // 8, pu % 8))
                            for c in range(16):
                                t.op("matmul", out=bank[:, 0:256],
                                     lhsT=zqsb[:, c * TQ + ti * 128:
                                              c * TQ + (ti + 1) * 128],
                                     rhs=wosb[:, c * 256:(c + 1) * 256],
                                     start=(c == 0), stop=(c == 15))
                            t.mark(M, ("t", "p3mm", d, ti))
                sec("t", p3_t)

                def p3_v(v):
                    for d in range(8):
                        for ti in range(8):
                            u = d * 8 + ti
                            bank = ps[u % 6]
                            v.wait(M, ("t", "p3mm", d, ti))
                            v.wait(M, ("q", "p3xr", d, ti))
                            if u >= 2:
                                v.wait(M, ("g", "p3st", (u - 2) // 8, (u - 2) % 8))
                            v.op("tensor_tensor",
                                 out=x1sb[:, (ti % 2) * 256:(ti % 2) * 256 + 256],
                                 in0=bank[:, 0:256],
                                 in1=xrb[:, (ti % 2) * 256:(ti % 2) * 256 + 256],
                                 op=ALU.add)
                            v.wait_self()
                            v.mark(M, ("v", "p3ev", d, ti))
                sec("v", p3_v)

                def p3_g(g):
                    for d in range(8):
                        for ti in range(8):
                            g.wait(M, ("v", "p3ev", d, ti))
                            g.dma(x1_d[ti][:, d * 256:(d + 1) * 256],
                                  x1sb[:, (ti % 2) * 256:(ti % 2) * 256 + 256])
                            g.mark(M, ("g", "p3st", d, ti))
                sec("g", p3_g)

                def p3b_q(q):
                    for ti in range(8):
                        q.wait(M, ("g", "p3st", 7, ti))
                        if ti >= 2:
                            q.wait(M, ("s", "p3bymul", ti - 2, 1))
                        q.dma(x1row[:, (ti % 2) * D:(ti % 2) * D + D], x1_d[ti])
                        q.mark(M, ("q", "p3bx", ti))
                sec("q", p3b_q)

                def p3b_s(s):
                    s.wait(M, ("v", "p0B", len(p0B) - 1))
                    s.wait(M, ("g", "p0Bst", len(p0B) - 1))
                    for ti in range(8):
                        xt = x1row[:, (ti % 2) * D:(ti % 2) * D + D]
                        s.wait(M, ("v", "p3bvar", ti))
                        s.op("activation", out=sc("sd"), in_=sc("var"),
                             func=AF.Sqrt, bias=0.0, scale=1.0)
                        s.mark(M, ("s", "p3bsd", ti))
                        s.wait(M, ("v", "p3bnb", ti))
                        if ti >= 1:
                            s.wait(M, ("v", "p3bround", ti - 1, 1))
                        s.op("activation", out=xn2[:], in_=xt, func=AF.Identity,
                             scale=sc("rs"), bias=sc("nb"))
                        s.mark(M, ("s", "p3bapply", ti))
                        s.wait(M, ("v", "p3bss2v", ti))
                        s.op("activation", out=sc("n1"), in_=sc("n1"),
                             func=AF.Sqrt, bias=0.0, scale=1.0)
                        s.mark(M, ("s", "p3bsqrt2", ti))
                        s.wait(M, ("v", "p3bmu", ti))
                        for hh in range(2):
                            s.op("activation", out=ybuf[:],
                                 in_=xn2[:, hh * 1024:(hh + 1) * 1024],
                                 func=AF.Copy,
                                 scale=st_[:, MU1 + ti:MU1 + ti + 1], bias=0.0)
                            s.mark(M, ("s", "p3bymul", ti, hh))
                            if hh == 0:
                                s.wait(M, ("v", "p3bround", ti, 0))
                sec("s", p3b_s)

                def p3b_v(v):
                    v.wait(M, ("g", "p0Bst", len(p0B) - 1))
                    for ti in range(8):
                        xt = x1row[:, (ti % 2) * D:(ti % 2) * D + D]
                        v.wait(M, ("q", "p3bx", ti))
                        for bq in range(4):
                            v.op("bn_stats", out=st_[:, 96 + 6 * bq:
                                                     96 + 6 * bq + 6],
                                 in_=xt[:, bq * 512:(bq + 1) * 512])
                        v.wait_self()
                        v.op("bn_aggr", out=st_[:, 86:88],
                             in_=st_[:, 96:120])
                        v.wait_self()
                        v.op("tensor_scalar", out=sc("var"),
                             in0=st_[:, 87:88], scalar1=EPS_LN,
                             scalar2=None, op0=ALU.add)
                        v.op("tensor_copy", out=sc("m"), in_=st_[:, 86:87])
                        v.wait_self()
                        v.mark(M, ("v", "p3bvar", ti))
                        v.wait(M, ("s", "p3bsd", ti))
                        v.op("reciprocal", out=sc("rs"), in_=sc("sd"))
                        v.wait_self()
                        v.op("scalar_tensor_tensor", out=sc("nb"), in0=sc("m"),
                             scalar=-1.0, in1=sc("rs"), op0=ALU.mult,
                             op1=ALU.mult)
                        v.wait_self()
                        v.mark(M, ("v", "p3bnb", ti))
                        v.wait(M, ("s", "p3bapply", ti))
                        v.op("reduce_max", out=sc("mab"), in_=xn2[:], axis=AX.X,
                             apply_absolute_value=True)
                        v.wait_self()
                        v.op("tensor_tensor", out=sc("n1"), in0=st_[:, 87:88],
                             in1=sc("rs"), op=ALU.mult)
                        v.wait_self()
                        v.op("tensor_tensor", out=sc("n1"), in0=sc("n1"),
                             in1=sc("rs"), op=ALU.mult)
                        v.wait_self()
                        v.op("tensor_scalar", out=sc("n1"), in0=sc("n1"),
                             scalar1=float(D), scalar2=None, op0=ALU.mult)
                        v.wait_self()
                        v.mark(M, ("v", "p3bss2v", ti))
                        v.wait(M, ("s", "p3bsqrt2", ti))
                        v.op("tensor_scalar", out=sc("n1"), in0=sc("n1"),
                             scalar1=EPS_RMS, scalar2=None, op0=ALU.max)
                        v.wait_self()
                        v.op("reciprocal", out=sc("rc"), in_=sc("n1"))
                        v.wait_self()
                        v.op("tensor_scalar", out=sc("rr"), in0=sc("rc"),
                             scalar1=SQD, scalar2=None, op0=ALU.mult)
                        v.wait_self()
                        v.op("tensor_tensor", out=sc("mx"), in0=sc("mab"),
                             in1=sc("rr"), op=ALU.mult)
                        v.wait_self()
                        v.op("tensor_scalar", out=sc("mxc"), in0=sc("mx"),
                             scalar1=EPS_Q, scalar2=None, op0=ALU.max)
                        v.wait_self()
                        v.op("reciprocal", out=sc("si"), in_=sc("mxc"))
                        v.wait_self()
                        v.op("scalar_tensor_tensor",
                             out=st_[:, MU1 + ti:MU1 + ti + 1], in0=sc("rr"),
                             scalar=127.0, in1=sc("si"), op0=ALU.mult,
                             op1=ALU.mult)
                        v.op("scalar_tensor_tensor",
                             out=st_[:, CS1 + ti:CS1 + ti + 1], in0=sc("mxc"),
                             scalar=-1.0 / 127.0, in1=sc("sw1m"), op0=ALU.mult,
                             op1=ALU.mult)
                        v.wait_self()
                        v.mark(M, ("v", "p3bmu", ti))
                        for hh in range(2):
                            v.wait(M, ("s", "p3bymul", ti, hh))
                            emit_round(v, xqb[:, hh * 1024:(hh + 1) * 1024],
                                       ybuf[:], tA[:], tB[:], tS[:])
                            v.wait_self()
                            v.mark(M, ("v", "p3bround", ti, hh))
                        for k in range(16):
                            v.wait(M, ("t", "p3btr", ti, k))
                            v.op("tensor_copy",
                                 out=xqT[:, k * TQ + ti * 128:
                                         k * TQ + (ti + 1) * 128],
                                 in_=pbf[k % 2][:, 0:128])
                            v.wait_self()
                            v.mark(M, ("v", "p3bxqev", ti, k))
                sec("v", p3b_v)

                def p3b_t(t):
                    for ti in range(8):
                        t.wait(M, ("v", "p3bround", ti, 1))
                        for k in range(16):
                            if k >= 2:
                                t.wait(M, ("v", "p3bxqev", ti, k - 2))
                            elif ti >= 1:
                                t.wait(M, ("v", "p3bxqev", ti - 1, 14 + k))
                            t.op("transpose", out=pbf[k % 2][:, 0:128],
                                 in_=xqb[:, k * 128:(k + 1) * 128],
                                 identity=sidb[:])
                            t.mark(M, ("t", "p3btr", ti, k))
                sec("t", p3b_t)

                def p4a_q(q):
                    q.wait(M, ("g", "p0done"))
                    q.wait(M, ("t", "p3mm", 7, 7))
                    q.wait(M, ("g", "p3st", 7, 7))
                    q.wait(M, ("s", "p3bymul", 7, 1))
                    for icg in range(8):
                        if icg >= 1:
                            q.wait(M, ("t", "p4amm", icg - 1, 1, 7))
                        for c in range(16):
                            q.dma(w1sb[:, c * 1024:(c + 1) * 1024],
                                  w1q_d[icg, c])
                        q.mark(M, ("q", "p4aw", icg))
                sec("q", p4a_q)

                def p4a_t(t):
                    t.wait(M, ("v", "p3bxqev", 7, 15))
                    for icg in range(8):
                        t.wait(M, ("q", "p4aw", icg))
                        for sub in range(2):
                            for ti in range(8):
                                u = icg * 16 + sub * 8 + ti
                                bank = ps[u % 6]
                                if u >= 6:
                                    pu = u - 6
                                    t.wait(M, ("s", "p4agel1",
                                               pu // 16, (pu // 8) % 2, pu % 8))
                                for c in range(16):
                                    t.op("matmul", out=bank[:],
                                         lhsT=xqT[:, c * TQ + ti * 128:
                                                  c * TQ + (ti + 1) * 128],
                                         rhs=w1sb[:, c * 1024 + sub * 512:
                                                  c * 1024 + sub * 512 + 512],
                                         start=(c == 0), stop=(c == 15))
                                t.mark(M, ("t", "p4amm", icg, sub, ti))
                sec("t", p4a_t)

                def p4a_s(s):
                    for icg in range(8):
                        for sub in range(2):
                            for ti in range(8):
                                u = icg * 16 + sub * 8 + ti
                                bank = ps[u % 6]
                                slot = (u % 2) * 2
                                s.wait(M, ("t", "p4amm", icg, sub, ti))
                                if u >= 2:
                                    pu = u - 2
                                    s.wait(M, ("g", "p4ast",
                                               pu // 16, (pu // 8) % 2, pu % 8))
                                s.op("activation",
                                     out=hb[:, slot * 512:slot * 512 + 512],
                                     in_=bank[:], func=AF.Gelu,
                                     scale=st_[:, CS1 + ti:CS1 + ti + 1],
                                     bias=0.0)
                                s.mark(M, ("s", "p4agel1", icg, sub, ti))
                                s.wait_self()
                                s.op("activation",
                                     out=hb[:, (slot + 1) * 512:
                                             (slot + 1) * 512 + 512],
                                     in_=hb[:, slot * 512:slot * 512 + 512],
                                     func=AF.Gelu, bias=0.0, scale=1.0)
                                s.wait_self()
                                if u >= 1:
                                    pv = u - 1
                                    s.wait(M, ("v", "p4av", pv // 16,
                                               (pv // 8) % 2, pv % 8))
                                s.op("activation",
                                     out=hb[:, slot * 512:slot * 512 + 512],
                                     in_=hb[:, (slot + 1) * 512:
                                             (slot + 1) * 512 + 512],
                                     func=AF.Square, bias=0.0, scale=1.0,
                                     accum_out=sc("ssq"))
                                s.mark(M, ("s", "p4asq", icg, sub, ti))
                sec("s", p4a_s)

                def p4a_v(v):
                    v.op("memset", st_[:, SS2:SS2 + 8], 0.0)
                    v.op("memset", st_[:, MX2:MX2 + 8], 0.0)
                    v.wait_self()
                    for icg in range(8):
                        for sub in range(2):
                            for ti in range(8):
                                u = icg * 16 + sub * 8 + ti
                                slot = (u % 2) * 2
                                v.wait(M, ("s", "p4asq", icg, sub, ti))
                                v.op("tensor_tensor",
                                     out=st_[:, SS2 + ti:SS2 + ti + 1],
                                     in0=st_[:, SS2 + ti:SS2 + ti + 1],
                                     in1=sc("ssq"), op=ALU.add)
                                v.op("reduce_max", out=sc("mab"),
                                     in_=hb[:, (slot + 1) * 512:
                                             (slot + 1) * 512 + 512],
                                     axis=AX.X, apply_absolute_value=True)
                                v.wait_self()
                                v.op("tensor_tensor",
                                     out=st_[:, MX2 + ti:MX2 + ti + 1],
                                     in0=st_[:, MX2 + ti:MX2 + ti + 1],
                                     in1=sc("mab"), op=ALU.max)
                                v.wait_self()
                                v.mark(M, ("v", "p4av", icg, sub, ti))
                sec("v", p4a_v)

                def p4a_g(g):
                    for icg in range(8):
                        for sub in range(2):
                            ic = icg * 2 + sub
                            for ti in range(8):
                                u = icg * 16 + sub * 8 + ti
                                slot = (u % 2) * 2
                                g.wait(M, ("v", "p4av", icg, sub, ti))
                                g.dma(h_d[ti][:, ic * 512:(ic + 1) * 512],
                                      hb[:, (slot + 1) * 512:
                                         (slot + 1) * 512 + 512])
                                g.mark(M, ("g", "p4ast", icg, sub, ti))
                sec("g", p4a_g)

                def p4b_q(q):
                    q.wait(M, ("g", "p4ast", 7, 1, 7))
                    q.wait(M, ("t", "p4amm", 7, 1, 7))
                    q.wait(M, ("s", "p4asq", 7, 1, 7))
                    q.wait(M, ("v", "p4av", 7, 1, 7))
                    for th in range(2):
                        for tl in range(4):
                            ti = th * 4 + tl
                            for ch in range(8):
                                u = (th * 4 + tl) * 8 + ch
                                if u >= 2:
                                    pu = u - 2
                                    q.wait(M, ("s", "p4bymul",
                                               pu // 32, (pu // 8) % 4, pu % 8))
                                q.wait(M, ("g", "p4ast", 7, 1, ti))
                                q.dma(hrow[:, (ch % 2) * 1024:
                                           (ch % 2) * 1024 + 1024],
                                      h_d[ti][:, ch * 1024:(ch + 1) * 1024])
                                q.mark(M, ("q", "p4bh", th, tl, ch))
                        for d in range(4):
                            if (th, d) != (0, 0):
                                pth, pd = (th, d - 1) if d else (th - 1, 3)
                                q.wait(M, ("t", "p4bmm", pth, pd, 3))
                            for c in range(64):
                                q.dma(w2sb[:, c * 512:(c + 1) * 512],
                                      w2q_d[d, c])
                            q.mark(M, ("q", "p4bw", th, d))
                            for tl in range(4):
                                ti = th * 4 + tl
                                u = d * 4 + tl
                                if u >= 2 or th > 0:
                                    pu = u - 2 if u >= 2 else 14 + u
                                    pth = th if u >= 2 else th - 1
                                    q.wait(M, ("v", "p4badd",
                                               pth, pu // 4, pu % 4))
                                q.dma(xr2[:, (tl % 2) * 512:(tl % 2) * 512 + 512],
                                      x1_d[ti][:, d * 512:(d + 1) * 512])
                                q.mark(M, ("q", "p4bxr", th, d, tl))
                sec("q", p4b_q)

                def p4b_s(s):
                    for th in range(2):
                        for tl in range(4):
                            ti = th * 4 + tl
                            s.wait(M, ("v", "p4av", 7, 1, 7))
                            s.op("activation", out=sc("sd"),
                                 in_=st_[:, SS2 + ti:SS2 + ti + 1],
                                 func=AF.Sqrt, bias=0.0, scale=1.0)
                            s.mark(M, ("s", "p4bsqrt", th, tl))
                            for ch in range(8):
                                s.wait(M, ("q", "p4bh", th, tl, ch))
                                s.wait(M, ("v", "p4bmu", th, tl))
                                u = (th * 4 + tl) * 8 + ch
                                if u >= 1:
                                    pu = u - 1
                                    s.wait(M, ("v", "p4bround",
                                               pu // 32, (pu // 8) % 4, pu % 8))
                                s.op("activation", out=ybuf[:],
                                     in_=hrow[:, (ch % 2) * 1024:
                                             (ch % 2) * 1024 + 1024],
                                     func=AF.Copy,
                                     scale=st_[:, MU2 + ti:MU2 + ti + 1],
                                     bias=0.0)
                                s.mark(M, ("s", "p4bymul", th, tl, ch))
                        for d in range(4):
                            for tl in range(4):
                                ti = th * 4 + tl
                                bank = ps[(d * 4 + tl) % 6]
                                s.wait(M, ("t", "p4bmm", th, d, tl))
                                u = d * 4 + tl
                                if u >= 1 or th > 0:
                                    pu = u - 1 if u >= 1 else 15
                                    pth = th if u >= 1 else th - 1
                                    s.wait(M, ("v", "p4badd",
                                               pth, pu // 4, pu % 4))
                                s.op("activation", out=tmp5[:], in_=bank[:],
                                     func=AF.Copy,
                                     scale=st_[:, CS2 + ti:CS2 + ti + 1],
                                     bias=0.0)
                                s.mark(M, ("s", "p4bcp", th, d, tl))
                sec("s", p4b_s)

                def p4b_v(v):
                    for th in range(2):
                        for tl in range(4):
                            ti = th * 4 + tl
                            v.wait(M, ("s", "p4bsqrt", th, tl))
                            v.op("tensor_scalar", out=sc("n1"), in0=sc("sd"),
                                 scalar1=EPS_RMS, scalar2=None, op0=ALU.max)
                            v.wait_self()
                            v.op("reciprocal", out=sc("rc"), in_=sc("n1"))
                            v.wait_self()
                            v.op("tensor_scalar", out=sc("rr"), in0=sc("rc"),
                                 scalar1=SQI, scalar2=None, op0=ALU.mult)
                            v.wait_self()
                            v.op("tensor_tensor", out=sc("mx"),
                                 in0=st_[:, MX2 + ti:MX2 + ti + 1],
                                 in1=sc("rr"), op=ALU.mult)
                            v.wait_self()
                            v.op("tensor_scalar", out=sc("mxc"), in0=sc("mx"),
                                 scalar1=EPS_Q, scalar2=None, op0=ALU.max)
                            v.wait_self()
                            v.op("reciprocal", out=sc("si"), in_=sc("mxc"))
                            v.wait_self()
                            v.op("scalar_tensor_tensor",
                                 out=st_[:, MU2 + ti:MU2 + ti + 1], in0=sc("rr"),
                                 scalar=127.0, in1=sc("si"), op0=ALU.mult,
                                 op1=ALU.mult)
                            v.op("scalar_tensor_tensor",
                                 out=st_[:, CS2 + ti:CS2 + ti + 1],
                                 in0=sc("mxc"), scalar=-1.0 / 127.0,
                                 in1=sc("sw2m"), op0=ALU.mult, op1=ALU.mult)
                            v.wait_self()
                            v.mark(M, ("v", "p4bmu", th, tl))
                            for ch in range(8):
                                v.wait(M, ("s", "p4bymul", th, tl, ch))
                                if th == 1:
                                    v.wait(M, ("t", "p4bmm", 0, 3, 3))
                                emit_round(v, xq2b[:], ybuf[:], tA[:], tB[:],
                                           tS[:])
                                v.wait_self()
                                v.mark(M, ("v", "p4bround", th, tl, ch))
                                for k in range(8):
                                    v.wait(M, ("t", "p4btr", th, tl, ch, k))
                                    kk = ch * 8 + k
                                    v.op("tensor_copy",
                                         out=xq2T[:, kk * 512 + tl * 128:
                                                 kk * 512 + (tl + 1) * 128],
                                         in_=pbf[k % 2][:, 128:256])
                                    v.wait_self()
                                    v.mark(M, ("v", "p4bxqev", th, tl, ch, k))
                        for d in range(4):
                            for tl in range(4):
                                v.wait(M, ("s", "p4bcp", th, d, tl))
                                v.wait(M, ("q", "p4bxr", th, d, tl))
                                u = d * 4 + tl
                                if u >= 2 or th > 0:
                                    pu = u - 2 if u >= 2 else 14 + u
                                    pth = th if u >= 2 else th - 1
                                    v.wait(M, ("g", "p4by",
                                               pth, pu // 4, pu % 4))
                                v.op("tensor_tensor",
                                     out=yo[:, (tl % 2) * 512:(tl % 2) * 512 + 512],
                                     in0=tmp5[:],
                                     in1=xr2[:, (tl % 2) * 512:(tl % 2) * 512 + 512],
                                     op=ALU.add)
                                v.wait_self()
                                v.mark(M, ("v", "p4badd", th, d, tl))
                sec("v", p4b_v)

                def p4b_t(t):
                    for th in range(2):
                        for tl in range(4):
                            for ch in range(8):
                                t.wait(M, ("v", "p4bround", th, tl, ch))
                                for k in range(8):
                                    u = ((th * 4 + tl) * 8 + ch) * 8 + k
                                    if u >= 2:
                                        pu = u - 2
                                        t.wait(M, ("v", "p4bxqev",
                                                   pu // 256, (pu // 64) % 4,
                                                   (pu // 8) % 8, pu % 8))
                                    t.op("transpose", out=pbf[k % 2][:, 128:256],
                                         in_=xq2b[:, k * 128:(k + 1) * 128],
                                         identity=sidb[:])
                                    t.mark(M, ("t", "p4btr", th, tl, ch, k))
                        for d in range(4):
                            t.wait(M, ("q", "p4bw", th, d))
                            for tl in range(4):
                                bank = ps[(d * 4 + tl) % 6]
                                u = d * 4 + tl
                                if u >= 6 or th > 0:
                                    pu = u - 6 if u >= 6 else 10 + u
                                    pth = th if u >= 6 else th - 1
                                    t.wait(M, ("s", "p4bcp", pth, pu // 4, pu % 4))
                                t.wait(M, ("v", "p4bxqev", th, 3, 7, 7))
                                for c in range(64):
                                    t.op("matmul", out=bank[:],
                                         lhsT=xq2T[:, c * 512 + tl * 128:
                                                  c * 512 + (tl + 1) * 128],
                                         rhs=w2sb[:, c * 512:(c + 1) * 512],
                                         start=(c == 0), stop=(c == 63))
                                t.mark(M, ("t", "p4bmm", th, d, tl))
                sec("t", p4b_t)

                def p4b_g(g):
                    for th in range(2):
                        for d in range(4):
                            for tl in range(4):
                                ti = th * 4 + tl
                                g.wait(M, ("v", "p4badd", th, d, tl))
                                g.dma(y_out[ti * 128:(ti + 1) * 128,
                                            d * 512:(d + 1) * 512],
                                      yo[:, (tl % 2) * 512:(tl % 2) * 512 + 512])
                                g.mark(M, ("g", "p4by", th, d, tl))
                    if mode == "emit":
                        g.wait_total()
                sec("g", p4b_g)

            if part == "A":
                def pa_fin_g(g):
                    if mode == "emit":
                        g.wait_total()
                sec("g", pa_fin_g)

        M = {}
        CE = {k: CEng(16 if k in ("q", "g", "p") else 1, k)
              for k in ("q", "g", "p", "t", "v", "s", "cc")}
        block = None
        program(CE, M, "plan")

        sems = {}
        for k in ("q", "g", "p"):
            sems[k] = [ctx.enter_context(nc.semaphore(f"sem{k}{i}"))
                       for i in (0, 1)]
        for k in ("t", "v", "s", "cc"):
            sems[k] = [ctx.enter_context(nc.semaphore(f"sem{k}"))]
        RE = {k: REng(16 if k in ("q", "g", "p") else 1, sems[k], k)
              for k in ("q", "g", "p", "t", "v", "s", "cc")}
        M["_sems"] = sems
        block = ctx.enter_context(nc.Block())
        program(RE, M, "emit")
    return nc


def kernel(L0_attn_x, L0_attn_Wq, L0_attn_Wk, L0_attn_Wv, L0_attn_Wo,
           L0_ffn_w1, L0_ffn_w2):
    x = np.ascontiguousarray(np.asarray(L0_attn_x, dtype=np.float32))
    WqT = np.asarray(L0_attn_Wq, np.float32).T
    WkT = np.asarray(L0_attn_Wk, np.float32).T
    WvT = np.asarray(L0_attn_Wv, np.float32).T
    WoT = np.ascontiguousarray(np.asarray(L0_attn_Wo, np.float32).T)
    w1T = np.ascontiguousarray(np.asarray(L0_ffn_w1, np.float32).T)
    w2T = np.ascontiguousarray(np.asarray(L0_ffn_w2, np.float32).T)
    identf = np.eye(128, dtype=np.float32)
    maskd = np.triu(np.full((128, 128), NEG, dtype=np.float32), 1)
    mask01 = np.triu(np.ones((128, 128), dtype=np.float32))

    out = np.empty((B, T, D), dtype=np.float32)
    if FUSED:
        nc = build("ALL")
        in_maps = []
        for c in range(8):
            b, gq = c // 4, c % 4
            bm = np.zeros((128, 2), np.float32)
            bm[:, b] = 1.0
            in_maps.append({
                "x": np.ascontiguousarray(x[b]),
                "identf": identf, "maskd": maskd, "mask01": mask01, "bmask": bm,
                "wqT": np.ascontiguousarray(WqT[:, gq * DPC:(gq + 1) * DPC]),
                "wkT": np.ascontiguousarray(WkT[:, gq * DPC:(gq + 1) * DPC]),
                "wvT": np.ascontiguousarray(WvT[:, gq * DPC:(gq + 1) * DPC]),
                "xres": np.ascontiguousarray(x[b, gq * TQ:(gq + 1) * TQ]),
                "woT": WoT, "w1T": w1T, "w2T": w2T,
            })
        res = run_bass_kernel_spmd(nc, in_maps, core_ids=list(range(8)))
        for c in range(8):
            b, tq = c // 4, c % 4
            out[b, tq * TQ:(tq + 1) * TQ] = res.results[c]["y"]
    else:
        ncA = build("A")
        in_mapsA = []
        for c in range(8):
            b, gq = c // 4, c % 4
            in_mapsA.append({
                "x": np.ascontiguousarray(x[b]),
                "identf": identf, "maskd": maskd, "mask01": mask01,
                "wqT": np.ascontiguousarray(WqT[:, gq * DPC:(gq + 1) * DPC]),
                "wkT": np.ascontiguousarray(WkT[:, gq * DPC:(gq + 1) * DPC]),
                "wvT": np.ascontiguousarray(WvT[:, gq * DPC:(gq + 1) * DPC]),
            })
        resA = run_bass_kernel_spmd(ncA, in_mapsA, core_ids=list(range(8)))
        ncB = build("B")
        in_mapsB = []
        for c in range(8):
            b, tq = c // 4, c % 4
            bm = np.zeros((128, 2), np.float32)
            bm[:, b] = 1.0
            zq = np.stack([np.asarray(resA.results[(s // 4) * 4 + s % 4]
                                      ["zparts"])[c]
                           for s in range(8)])
            in_mapsB.append({
                "x": np.ascontiguousarray(x[b]),
                "identf": identf, "bmask": bm,
                "xres": np.ascontiguousarray(x[b, tq * TQ:(tq + 1) * TQ]),
                "woT": WoT, "w1T": w1T, "w2T": w2T,
                "zq_in": np.ascontiguousarray(zq),
            })
        resB = run_bass_kernel_spmd(ncB, in_mapsB, core_ids=list(range(8)))
        for c in range(8):
            b, tq = c // 4, c % 4
            out[b, tq * TQ:(tq + 1) * TQ] = resB.results[c]["y"]
    return out


# version 10
# speedup vs baseline: 1.0973x; 1.0973x over previous
import numpy as np
import os
os.environ.setdefault("TRN_ALLOW_ICI_ALL_TO_ALL", "1")
import concourse.bass as bass
import concourse.mybir as mybir
from concourse.bass_utils import run_bass_kernel_spmd
from contextlib import ExitStack

F32 = mybir.dt.float32
BF16 = mybir.dt.bfloat16
AF = mybir.ActivationFunctionType
ALU = mybir.AluOpType
AX = mybir.AxisListType

B, T, D, NH, HD, INNER = 2, 4096, 2048, 16, 128, 8192
HPC, DPC, TQ = 4, 512, 1024
ATT_SCALE = float(1.0 / np.sqrt(np.float64(HD)))
NEG = float(np.finfo(np.float32).min)
EPS_LN = 1e-5
EPS_Q = 1e-5
EPS_W = 1e-5
EPS_RMS = 1e-12
C32 = 3.0 * 2.0**22
C64 = 3.0 * 2.0**51
SQD = float(np.sqrt(np.float64(D)))
SQI = float(np.sqrt(np.float64(INNER)))
N_W1 = float(INNER * D)
N_W2 = float(D * INNER)

FUSED = True


class CEng:

    def __init__(self, step, key):
        self.step, self.key = step, key
        self.is_q = step == 16
        self.cnt = [0, 0]
        self.cur = 0
        self.e = None

    @property
    def n(self):
        return self.cnt[0] + self.cnt[1]

    def op(self, name, *a, **k):
        self.cnt[self.cur] += self.step
        return self.n

    def dma(self, out, in_, **kw):
        self.cnt[self.cur] += self.step
        return self.n

    def wait(self, M, key):
        pass

    def wait_n(self, sem_eng, n):
        pass

    def wait_self(self):
        pass

    def mark(self, M, key):
        if self.is_q:
            M[key] = ("Q", self.key, self.cur, self.cnt[self.cur],
                      self.cnt[self.cur ^ 1])
            self.cur ^= 1
        else:
            M[key] = self.cnt[0]


class REng:

    def __init__(self, step, sems, key):
        self.step, self.key = step, key
        self.is_q = step == 16
        self.sems = sems
        self.cnt = [0, 0]
        self.cur = 0
        self._need_drain = False
        self.e = None

    @property
    def n(self):
        return self.cnt[0] + self.cnt[1]

    @property
    def sem(self):
        return self.sems[0]

    def op(self, name, *a, **k):
        inst = getattr(self.e, name)(*a, **k)
        inst.then_inc(self.sems[self.cur] if self.is_q else self.sems[0],
                      self.step)
        self.cnt[self.cur] += self.step
        return self.n

    def dma(self, out, in_, **kw):
        if self.is_q and self._need_drain:
            self.e.wait_ge(self.sems[self.cur], self.cnt[self.cur])
            self._need_drain = False
        inst = self.e.dma_start(out=out, in_=in_, **kw)
        inst.then_inc(self.sems[self.cur] if self.is_q else self.sems[0],
                      self.step)
        self.cnt[self.cur] += self.step
        return self.n

    def wait(self, M, key):
        val = M[key]
        if isinstance(val, tuple) and val[0] == "Q":
            _, qkey, which, count, other = val
            self.e.wait_ge(M["_sems"][qkey][which], count)
            if other:
                self.e.wait_ge(M["_sems"][qkey][which ^ 1], other)
        else:
            self.e.wait_ge(M["_sems"][key[0]][0], val)

    def wait_n(self, sem_eng, n):
        self.e.wait_ge(sem_eng.sems[0], n)

    def wait_self(self):
        self.e.wait_ge(self.sems[0], self.cnt[0])

    def wait_total(self):
        for i in (0, 1):
            self.e.wait_ge(self.sems[i], self.cnt[i])

    def mark(self, M, key):
        if self.is_q:
            exp = M[key]
            got = ("Q", self.key, self.cur, self.cnt[self.cur],
                   self.cnt[self.cur ^ 1])
            assert exp == got, f"mark {key}: planned {exp} emitted {got}"
            self.cur ^= 1
            self._need_drain = True
        else:
            assert M[key] == self.cnt[0], (
                f"mark {key}: planned {M[key]} emitted {self.cnt[0]}")


def emit_round(v, out_bf, y, tA, tB, tS):
    v.op("tensor_scalar", out=tS, in0=y, scalar1=1e-30, scalar2=1e-30,
         op0=ALU.mult, op1=ALU.mult)
    v.wait_self()
    v.op("tensor_scalar", out=tS, in0=tS, scalar1=0.0, scalar2=None,
         op0=ALU.not_equal)
    v.op("tensor_scalar", out=tA, in0=y, scalar1=C32, scalar2=C32,
         op0=ALU.add, op1=ALU.subtract)
    v.wait_self()
    v.op("tensor_scalar", out=tB, in0=tA, scalar1=C64, scalar2=C64,
         op0=ALU.add, op1=ALU.subtract)
    v.wait_self()
    v.op("tensor_tensor", out=tB, in0=tB, in1=tA, op=ALU.subtract)
    v.wait_self()
    v.op("tensor_tensor", out=tB, in0=tB, in1=tS, op=ALU.mult)
    v.wait_self()
    v.op("tensor_tensor", out=out_bf, in0=tB, in1=tA, op=ALU.add)


def build(part):
    wantA = part in ("A", "ALL")
    wantB = part in ("B", "ALL")
    nc = bass.Bass("TRN2", target_bir_lowering=False, debug=False)

    x = nc.dram_tensor("x", [T, D], F32, kind="ExternalInput").ap()
    identf = nc.dram_tensor("identf", [128, 128], F32, kind="ExternalInput").ap()
    if wantA:
        wqT = nc.dram_tensor("wqT", [D, DPC], F32, kind="ExternalInput").ap()
        wkT = nc.dram_tensor("wkT", [D, DPC], F32, kind="ExternalInput").ap()
        wvT = nc.dram_tensor("wvT", [D, DPC], F32, kind="ExternalInput").ap()
        maskd = nc.dram_tensor("maskd", [128, 128], F32, kind="ExternalInput").ap()
        mask01 = nc.dram_tensor("mask01", [128, 128], F32, kind="ExternalInput").ap()
        qT_d = nc.dram_tensor("qT_d", [16, 128, 1024], F32, kind="Internal").ap()
        kT_d = nc.dram_tensor("kT_d", [16, 128, 1024], F32, kind="Internal").ap()
        v_d = nc.dram_tensor("v_d", [T, DPC], F32, kind="Internal").ap()
        zpk = "Internal" if part == "ALL" else "ExternalOutput"
        zparts = nc.dram_tensor("zparts", [8, DPC, TQ], F32, kind=zpk).ap()
    if wantB:
        xres = nc.dram_tensor("xres", [TQ, D], F32, kind="ExternalInput").ap()
        bmask = nc.dram_tensor("bmask", [128, 2], F32, kind="ExternalInput").ap()
        woT = nc.dram_tensor("woT", [D, D], F32, kind="ExternalInput").ap()
        w1T = nc.dram_tensor("w1T", [D, INNER], F32, kind="ExternalInput").ap()
        w2T = nc.dram_tensor("w2T", [INNER, D], F32, kind="ExternalInput").ap()
        w1q_d = nc.dram_tensor("w1q_d", [8, 16, 128, 1024], BF16, kind="Internal").ap()
        w2q_d = nc.dram_tensor("w2q_d", [4, 64, 128, 512], BF16, kind="Internal").ap()
        h_d = nc.dram_tensor("h_d", [8, 128, INNER], F32, kind="Internal").ap()
        x1_d = nc.dram_tensor("x1_d", [8, 128, D], F32, kind="Internal").ap()
        y_out = nc.dram_tensor("y", [TQ, D], F32, kind="ExternalOutput").ap()
    if part == "ALL":
        za_out = nc.dram_tensor("za_out", [8, DPC, TQ], F32, kind="Internal").ap()
    if part == "B":
        zq_in = nc.dram_tensor("zq_in", [8, DPC, TQ], F32, kind="ExternalInput").ap()
        za_out = zq_in

    ctx = ExitStack()
    with ctx:
        sidf = ctx.enter_context(nc.sbuf_tensor("sidf", [128, 128], F32))
        sidb = ctx.enter_context(nc.sbuf_tensor("sidb", [128, 128], BF16))
        smk = ctx.enter_context(nc.sbuf_tensor("smk", [128, 128], F32))
        tA = ctx.enter_context(nc.sbuf_tensor("tA", [128, 1024], F32))
        tB = ctx.enter_context(nc.sbuf_tensor("tB", [128, 1024], F32))
        tS = ctx.enter_context(nc.sbuf_tensor("tS", [128, 1024], F32))
        ybuf = ctx.enter_context(nc.sbuf_tensor("ybuf", [128, 1024], F32))
        tq1 = ctx.enter_context(nc.sbuf_tensor("tq1", [128, 1024], BF16))
        tq2 = ctx.enter_context(nc.sbuf_tensor("tq2", [128, 1024], BF16))
        st_ = ctx.enter_context(nc.sbuf_tensor("st_", [128, 128], F32))
        smb = ctx.enter_context(nc.sbuf_tensor("smb", [128, 2], F32))
        sones = ctx.enter_context(nc.sbuf_tensor("sones", [128, 1], F32))
        sones_r = ctx.enter_context(nc.sbuf_tensor("sones_r", [1, 128], F32))
        ps = [ctx.enter_context(nc.psum_tensor(f"ps{i}", [128, 512], F32))
              for i in range(6)]
        pbf = [ctx.enter_context(nc.psum_tensor(f"pbf{i}", [128, 512], BF16))
               for i in range(2)]

        SC = {k: i for i, k in enumerate(
            ["rsum", "ssq", "m", "ex2", "m2", "var", "sd", "rs", "nb",
             "nmax", "nmaxs", "psum_r", "prec",
             "sw1m", "thr1", "nthr1", "sw2m", "thr2", "nthr2",
             "n1", "rc", "rr", "mab", "mx", "mxc", "si"])}

        def sc(k):
            return st_[:, SC[k]:SC[k] + 1]

        MU1, CS1, SS2, MX2, MU2, CS2 = 26, 34, 42, 50, 58, 66

        sA1 = ExitStack()
        wq_sb = sA1.enter_context(nc.sbuf_tensor("wq_sb", [128, 16 * 512], F32))
        wk_sb = sA1.enter_context(nc.sbuf_tensor("wk_sb", [128, 16 * 512], F32))
        wv_sb = sA1.enter_context(nc.sbuf_tensor("wv_sb", [128, 16 * 512], F32))
        xc = sA1.enter_context(nc.sbuf_tensor("xc", [128, 2 * D], F32))
        xnc = sA1.enter_context(nc.sbuf_tensor("xnc", [128, 2 * D], F32))
        xnT = sA1.enter_context(nc.sbuf_tensor("xnT", [128, 16 * 256], F32))
        qev = sA1.enter_context(nc.sbuf_tensor("qev", [128, 1024], F32))
        kev = sA1.enter_context(nc.sbuf_tensor("kev", [128, 1024], F32))
        vev = sA1.enter_context(nc.sbuf_tensor("vev", [128, 1024], F32))
        sA1.close()

        sA2 = ExitStack()
        ksb = sA2.enter_context(nc.sbuf_tensor("ksb", [128, HPC * T], F32))
        vsb = sA2.enter_context(nc.sbuf_tensor("vsb", [128, 32 * 512], F32))
        qblk = sA2.enter_context(nc.sbuf_tensor("qblk", [128, 2 * 4 * 512], F32))
        ptring = sA2.enter_context(nc.sbuf_tensor("ptring", [128, 4 * 512], F32))
        sexpb = sA2.enter_context(nc.sbuf_tensor("sexpb", [128, 512], F32))
        rsums = sA2.enter_context(nc.sbuf_tensor("rsums", [128, 128], F32))
        rrow = sA2.enter_context(nc.sbuf_tensor("rrow", [1, 512], F32))
        rssb = sA2.enter_context(nc.sbuf_tensor("rssb", [128, 512], F32))
        zev = sA2.enter_context(nc.sbuf_tensor("zev", [128, 512], F32))
        sm01 = sA2.enter_context(nc.sbuf_tensor("sm01", [128, 128], F32))
        sA2.close()

        sB1 = ExitStack()
        xqT = sB1.enter_context(nc.sbuf_tensor("xqT", [128, 16 * TQ], BF16))
        sB1a = ExitStack()
        zqsb = sB1a.enter_context(nc.sbuf_tensor("zqsb", [128, 16 * TQ], F32))
        ztmp = sB1a.enter_context(nc.sbuf_tensor("ztmp", [128, 2 * TQ], F32))
        wosb = sB1a.enter_context(nc.sbuf_tensor("wosb", [128, 16 * 256], F32))
        xrb = sB1a.enter_context(nc.sbuf_tensor("xrb", [128, 2 * 256], F32))
        x1sb = sB1a.enter_context(nc.sbuf_tensor("x1sb", [128, 2 * 256], F32))
        x1row = sB1a.enter_context(nc.sbuf_tensor("x1row", [128, 2 * D], F32))
        xn2 = sB1a.enter_context(nc.sbuf_tensor("xn2", [128, D], F32))
        xqb = sB1a.enter_context(nc.sbuf_tensor("xqb", [128, D], BF16))
        sB1a.close()
        w1sb = sB1.enter_context(nc.sbuf_tensor("w1sb", [128, 16 * 1024], BF16))
        hb = sB1.enter_context(nc.sbuf_tensor("hb", [128, 4 * 512], F32))
        sB1.close()

        sB2 = ExitStack()
        xq2T = sB2.enter_context(nc.sbuf_tensor("xq2T", [128, 64 * 512], BF16))
        w2sb = sB2.enter_context(nc.sbuf_tensor("w2sb", [128, 64 * 512], BF16))
        hrow = sB2.enter_context(nc.sbuf_tensor("hrow", [128, 2 * 1024], F32))
        xq2b = sB2.enter_context(nc.sbuf_tensor("xq2b", [128, 1024], BF16))
        xr2 = sB2.enter_context(nc.sbuf_tensor("xr2", [128, 2 * 512], F32))
        yo = sB2.enter_context(nc.sbuf_tensor("yo", [128, 2 * 512], F32))
        tmp5 = sB2.enter_context(nc.sbuf_tensor("tmp5", [128, 512], F32))
        sB2.close()

        p0A, p0B = [], []
        if wantB:
            for ci in range(16):
                for cg in range(8):
                    p0A.append(("w1", w1T[ci * 128:(ci + 1) * 128,
                                          cg * 1024:(cg + 1) * 1024]))
            for ci in range(64):
                for cg in range(2):
                    p0A.append(("w2", w2T[ci * 128:(ci + 1) * 128,
                                          cg * 1024:(cg + 1) * 1024]))
            for ci in range(16):
                for cg in range(8):
                    p0B.append(("w1", w1T[ci * 128:(ci + 1) * 128,
                                          cg * 1024:(cg + 1) * 1024],
                                ("w1", cg, ci)))
            for ci in range(64):
                for cg in range(2):
                    p0B.append(("w2", w2T[ci * 128:(ci + 1) * 128,
                                          cg * 1024:(cg + 1) * 1024],
                                ("w2", ci, cg)))

        def program(EN, M, mode):
            q, g, t, v, s = EN["q"], EN["g"], EN["t"], EN["v"], EN["s"]

            p0_state = {"gA": 0, "vA": 0, "gB": 0, "vB": 0}

            def p0_g_passA(qeng, nunits):
                for _ in range(nunits):
                    i = p0_state["gA"]
                    if i >= len(p0A):
                        return
                    p0_state["gA"] += 1
                    if i >= 4:
                        qeng.wait(M, ("v", "p0A", i - 4))
                    qeng.dma((tA, tB, tS, ybuf)[i % 4][:], p0A[i][1])
                    qeng.mark(M, ("g", "p0A", i))

            def p0_v_passA(nunits):
                for _ in range(nunits):
                    i = p0_state["vA"]
                    if i >= len(p0A):
                        return
                    p0_state["vA"] += 1
                    col = 74 if p0A[i][0] == "w1" else 75
                    v.wait(M, ("g", "p0A", i))
                    v.op("reduce_sum", out=sc("rsum"),
                         in_=(tA, tB, tS, ybuf)[i % 4][:],
                         axis=AX.X, apply_absolute_value=True)
                    v.wait_self()
                    v.op("tensor_tensor", out=st_[:, col:col + 1],
                         in0=st_[:, col:col + 1], in1=sc("rsum"), op=ALU.add)
                    v.wait_self()
                    v.mark(M, ("v", "p0A", i))

            def p0_g_passB(nunits):
                p_ = EN["p"]
                for _ in range(nunits):
                    i = p0_state["gB"]
                    if i >= len(p0B):
                        return
                    p0_state["gB"] += 1
                    if i >= 2:
                        p_.wait(M, ("v", "p0B", i - 2))
                    else:
                        p_.wait(M, ("v", "p0A", len(p0A) - 1))
                    p_.dma((tA, tB)[i % 2][:], p0B[i][1])
                    p_.mark(M, ("g", "p0Bld", i))
                    j = i - 1
                    if j >= 0:
                        p0_g_store(j)

            def p0_g_store(j):
                p_ = EN["p"]
                which, _, dst = p0B[j]
                p_.wait(M, ("v", "p0B", j))
                tb = (tq1, tq2)[j % 2]
                if which == "w1":
                    _, cg, ci = dst
                    p_.dma(w1q_d[cg, ci], tb[:])
                else:
                    _, ci, cg = dst
                    for dd in range(2):
                        p_.dma(w2q_d[cg * 2 + dd, ci], tb[:, dd * 512:(dd + 1) * 512])
                p_.mark(M, ("g", "p0Bst", j))

            def p0_v_passB(nunits):
                for _ in range(nunits):
                    i = p0_state["vB"]
                    if i >= len(p0B):
                        return
                    p0_state["vB"] += 1
                    which = p0B[i][0]
                    thr = sc("thr1") if which == "w1" else sc("thr2")
                    nthr = sc("nthr1") if which == "w1" else sc("nthr2")
                    if i == 0:
                        v.wait(M, ("v", "p0A", len(p0A) - 1))
                    v.wait(M, ("g", "p0Bld", i))
                    if i >= 2:
                        v.wait(M, ("g", "p0Bst", i - 2))
                    buf = (tA, tB)[i % 2]
                    v.op("tensor_scalar", out=ybuf[:], in0=buf[:], scalar1=thr,
                         scalar2=None, op0=ALU.is_gt)
                    v.op("tensor_scalar", out=tS[:], in0=buf[:], scalar1=nthr,
                         scalar2=None, op0=ALU.is_lt)
                    v.wait_self()
                    v.op("tensor_tensor", out=(tq1, tq2)[i % 2][:], in0=tS[:],
                         in1=ybuf[:], op=ALU.subtract)
                    v.wait_self()
                    v.mark(M, ("v", "p0B", i))

            def sec(engkey, fn):
                if mode == "plan":
                    fn(EN[engkey])
                else:
                    getattr(block, {"q": "sync", "g": "gpsimd", "t": "tensor",
                                    "v": "vector", "s": "scalar"}[engkey])(
                        lambda e: fn(EN[engkey].__setattr__("e", e) or EN[engkey]))

            def init_q(q):
                q.dma(sidf[:], identf)
                if wantA:
                    q.dma(smk[:], maskd)
                if wantB:
                    q.dma(smb[:], bmask)
                q.mark(M, ("q", "init"))
            sec("q", init_q)

            def init_v(v):
                v.wait(M, ("q", "init"))
                v.op("tensor_copy", out=sidb[:], in_=sidf[:])
                v.op("memset", sones[:], 1.0)
                v.op("memset", sones_r[:], 1.0)
                if wantB:
                    v.op("memset", st_[:, 74:76], 0.0)
                v.wait_self()
                v.mark(M, ("v", "init"))
            sec("v", init_v)

            NCH = 16
            if wantA:
                def p1_q(q):
                    for dc in range(16):
                        q.dma(wq_sb[:, dc * 512:(dc + 1) * 512],
                              wqT[dc * 128:(dc + 1) * 128, :])
                        q.dma(wk_sb[:, dc * 512:(dc + 1) * 512],
                              wkT[dc * 128:(dc + 1) * 128, :])
                        q.dma(wv_sb[:, dc * 512:(dc + 1) * 512],
                              wvT[dc * 128:(dc + 1) * 128, :])
                    q.mark(M, ("q", "ldw"))
                    for ci in range(NCH):
                        if ci >= 1:
                            q.wait(M, ("s", "p1apply", ci - 1))
                        for tt in range(2):
                            q.dma(xc[:, tt * D:(tt + 1) * D],
                                  x[ci * 256 + tt * 128: ci * 256 + (tt + 1) * 128])
                            q.mark(M, ("q", "p1x", ci, tt))
                sec("q", p1_q)

                def p1_s(s):
                    for ci in range(NCH):
                        for tt in range(2):
                            xt = xc[:, tt * D:(tt + 1) * D]
                            s.wait(M, ("v", "p1var", ci, tt))
                            s.op("activation", out=sc("sd"), in_=sc("var"),
                                 func=AF.Sqrt, bias=0.0, scale=1.0)
                            s.mark(M, ("s", "p1sd", ci, tt))
                            s.wait(M, ("v", "p1nb", ci, tt))
                            if ci >= 1 and tt == 0:
                                s.wait(M, ("t", "p1tr", ci - 1, 7))
                            s.op("activation", out=xnc[:, tt * D:(tt + 1) * D],
                                 in_=xt, func=AF.Identity, scale=sc("rs"),
                                 bias=sc("nb"))
                            s.mark(M, ("s", "p1applyt", ci, tt))
                        s.mark(M, ("s", "p1apply", ci))
                        s.wait(M, ("t", "p1mm", ci))
                        if ci >= 1:
                            s.wait(M, ("g", "p1sp", ci - 1))
                        for h in range(4):
                            bank, half = ps[h // 2], (h % 2) * 256
                            s.op("activation",
                                 out=qev[:, h * 256:(h + 1) * 256],
                                 in_=bank[:, half:half + 256], func=AF.Copy,
                                 bias=0.0, scale=ATT_SCALE)
                        for h in range(4):
                            bank, half = ps[2 + h // 2], (h % 2) * 256
                            s.op("activation", out=kev[:, h * 256:(h + 1) * 256],
                                 in_=bank[:, half:half + 256], func=AF.Copy,
                                 bias=0.0, scale=1.0)
                        for tt in range(2):
                            s.op("activation", out=vev[:, tt * 512:(tt + 1) * 512],
                                 in_=ps[4 + tt][:], func=AF.Copy,
                                 bias=0.0, scale=1.0)
                        s.wait_self()
                        s.mark(M, ("s", "p1evs", ci))
                sec("s", p1_s)

                def p1_v(v):
                    v.wait(M, ("v", "init"))
                    for ci in range(NCH):
                        for tt in range(2):
                            v.wait(M, ("q", "p1x", ci, tt))
                            xt = xc[:, tt * D:(tt + 1) * D]
                            for bq in range(4):
                                v.op("bn_stats", out=st_[:, 96 + 6 * bq:
                                                         96 + 6 * bq + 6],
                                     in_=xt[:, bq * 512:(bq + 1) * 512])
                            v.wait_self()
                            v.op("bn_aggr", out=st_[:, 86:88],
                                 in_=st_[:, 96:120])
                            v.wait_self()
                            v.op("tensor_scalar", out=sc("var"),
                                 in0=st_[:, 87:88], scalar1=EPS_LN,
                                 scalar2=None, op0=ALU.add)
                            v.op("tensor_copy", out=sc("m"), in_=st_[:, 86:87])
                            v.wait_self()
                            v.mark(M, ("v", "p1var", ci, tt))
                            v.wait(M, ("s", "p1sd", ci, tt))
                            v.op("reciprocal", out=sc("rs"), in_=sc("sd"))
                            v.wait_self()
                            v.op("scalar_tensor_tensor", out=sc("nb"), in0=sc("m"),
                                 scalar=-1.0, in1=sc("rs"), op0=ALU.mult,
                                 op1=ALU.mult)
                            v.wait_self()
                            v.mark(M, ("v", "p1nb", ci, tt))
                        if ci >= 1:
                            v.wait(M, ("t", "p1mm", ci - 1))
                        for grp in range(8):
                            v.wait(M, ("t", "p1tr", ci, grp))
                            tt0 = grp // 4
                            for k in range(4):
                                dc = (grp % 4) * 4 + k
                                v.op("tensor_copy",
                                     out=xnT[:, dc * 256 + tt0 * 128:
                                             dc * 256 + tt0 * 128 + 128],
                                     in_=ps[4][:, k * 128:(k + 1) * 128])
                            v.mark(M, ("v", "p1trev", ci, grp))
                        if wantB:
                            p0_v_passA(16)
                sec("v", p1_v)

                def p1_t(t):
                    t.wait(M, ("q", "ldw"))
                    for ci in range(NCH):
                        for grp in range(8):
                            tt0 = grp // 4
                            t.wait(M, ("s", "p1applyt", ci, tt0))
                            if (ci, grp) != (0, 0):
                                pci, pgrp = (ci, grp - 1) if grp else (ci - 1, 7)
                                t.wait(M, ("v", "p1trev", pci, pgrp))
                            if grp == 0 and ci >= 1:
                                t.wait(M, ("s", "p1evs", ci - 1))
                            for k in range(4):
                                dc = (grp % 4) * 4 + k
                                t.op("transpose",
                                     out=ps[4][:, k * 128:(k + 1) * 128],
                                     in_=xnc[:, tt0 * D + dc * 128:
                                             tt0 * D + (dc + 1) * 128],
                                     identity=sidf[:])
                            t.mark(M, ("t", "p1tr", ci, grp))
                        t.wait(M, ("v", "p1trev", ci, 7))
                        if ci >= 1:
                            t.wait(M, ("s", "p1evs", ci - 1))
                        for wsb, b0 in ((wq_sb, 0), (wk_sb, 2)):
                            for h in range(4):
                                bank, half = ps[b0 + h // 2], (h % 2) * 256
                                for dc in range(16):
                                    t.op("matmul", out=bank[:, half:half + 256],
                                         lhsT=wsb[:, dc * 512 + h * 128:
                                                  dc * 512 + (h + 1) * 128],
                                         rhs=xnT[:, dc * 256:(dc + 1) * 256],
                                         start=(dc == 0), stop=(dc == 15))
                        for tt in range(2):
                            for dc in range(16):
                                t.op("matmul", out=ps[4 + tt][:],
                                     lhsT=xnT[:, dc * 256 + tt * 128:
                                              dc * 256 + tt * 128 + 128],
                                     rhs=wv_sb[:, dc * 512:(dc + 1) * 512],
                                     start=(dc == 0), stop=(dc == 15))
                        t.mark(M, ("t", "p1mm", ci))
                sec("t", p1_t)

                def p1_g(g):
                    for ci in range(NCH):
                        if wantB:
                            p0_g_passA(g, 16)
                        g.wait(M, ("s", "p1evs", ci))
                        g.dma(qT_d[ci], qev[:])
                        g.dma(kT_d[ci], kev[:])
                        for tt in range(2):
                            g.dma(v_d[ci * 256 + tt * 128:ci * 256 + (tt + 1) * 128, :],
                                  vev[:, tt * 512:(tt + 1) * 512])
                        g.mark(M, ("g", "p1sp", ci))
                sec("g", p1_g)

                def units2():
                    for qb in range(8):
                        for h in range(4):
                            for i in range(4):
                                yield qb, h, i

                def p2_q(q):
                    q.wait(M, ("g", "p1sp", NCH - 1))
                    q.dma(sm01[:], mask01)
                    for ci in range(16):
                        for h in range(4):
                            q.dma(ksb[:, h * T + ci * 256:h * T + (ci + 1) * 256],
                                  kT_d[ci][:, h * 256:(h + 1) * 256])
                    for j in range(32):
                        q.dma(vsb[:, j * 512:(j + 1) * 512],
                              v_d[j * 128:(j + 1) * 128, :])
                    q.mark(M, ("q", "p2kv"))
                    for qb in range(8):
                        if qb >= 2:
                            q.wait(M, ("t", "p2send", qb - 2))
                        qo = (qb % 2) * 2048
                        for h in range(4):
                            for hf in range(2):
                                q.dma(qblk[:, qo + h * 512 + hf * 256:
                                           qo + h * 512 + hf * 256 + 256],
                                      qT_d[2 * qb + hf][:, h * 256:(h + 1) * 256])
                        q.mark(M, ("q", "p2q", qb))
                sec("q", p2_q)

                if wantB:
                    def p2r_t(t):
                        t.wait(M, ("v", "p0A", len(p0A) - 1))
                        t.wait(M, ("s", "p1evs", NCH - 1))
                        t.op("matmul", out=ps[2][0:1, 0:2], lhsT=sones[:],
                             rhs=st_[:, 74:76], start=True, stop=True)
                        t.mark(M, ("t", "redA1"))
                        t.wait(M, ("v", "redAsc"))
                        t.op("matmul", out=ps[2][:, 4:6], lhsT=sones_r[:],
                             rhs=st_[0:1, 76:78], start=True, stop=True)
                        t.mark(M, ("t", "redA2"))
                    sec("t", p2r_t)

                    def p2r_v(v):
                        v.wait(M, ("t", "redA1"))
                        v.op("tensor_scalar", out=st_[0:1, 76:77],
                             in0=ps[2][0:1, 0:1], scalar1=1.0 / N_W1,
                             scalar2=EPS_W, op0=ALU.mult, op1=ALU.max)
                        v.op("tensor_scalar", out=st_[0:1, 77:78],
                             in0=ps[2][0:1, 1:2], scalar1=1.0 / N_W2,
                             scalar2=EPS_W, op0=ALU.mult, op1=ALU.max)
                        v.wait_self()
                        v.mark(M, ("v", "redAsc"))
                        v.wait(M, ("t", "redA2"))
                        v.op("tensor_copy", out=sc("sw1m"), in_=ps[2][:, 4:5])
                        v.op("tensor_copy", out=sc("sw2m"), in_=ps[2][:, 5:6])
                        v.wait_self()
                        v.op("tensor_scalar", out=sc("thr1"), in0=sc("sw1m"),
                             scalar1=0.5, scalar2=None, op0=ALU.mult)
                        v.op("tensor_scalar", out=sc("thr2"), in0=sc("sw2m"),
                             scalar1=0.5, scalar2=None, op0=ALU.mult)
                        v.wait_self()
                        v.op("tensor_scalar", out=sc("nthr1"), in0=sc("thr1"),
                             scalar1=-1.0, scalar2=None, op0=ALU.mult)
                        v.op("tensor_scalar", out=sc("nthr2"), in0=sc("thr2"),
                             scalar1=-1.0, scalar2=None, op0=ALU.mult)
                        v.wait_self()
                        v.mark(M, ("v", "redA"))
                    sec("v", p2r_v)

                rlist = M.setdefault("_rlist", [])

                def _prev_rchunk(qb, h, i, j):
                    idx = rindex.get((qb, h, i, j), 0)
                    return rlist[idx - 1] if idx >= 1 else None

                rindex = M.setdefault("_rindex", {})

                def p2_t(t):
                    t.wait(M, ("q", "p2kv"))
                    rch = 0
                    for qb in range(8):
                        t.wait(M, ("q", "p2q", qb))
                        NT = 4 * qb + 4
                        for h in range(4):
                            u = qb * 4 + h
                            zbank = ps[4 + (u % 2)]
                            if u >= 1:
                                t.wait(M, ("v", "r2T", (u - 1) // 4, (u - 1) % 4))
                                t.wait(M, ("v", "r2B", (u - 1) // 4, (u - 1) % 4))
                            for i in range(4):
                                qt = qb * 4 + i
                                L = qt + 1
                                nch = (L + 3) // 4
                                for j in range(nch):
                                    wdt = min(512, L * 128 - j * 512)
                                    if rch >= 2:
                                        t.wait(M, ("s", "r2e") + rlist[rch - 2])
                                    t.op("matmul", out=ps[rch % 2][:, 0:wdt],
                                         lhsT=qblk[:, (qb % 2) * 2048 + h * 512
                                                   + i * 128:
                                                   (qb % 2) * 2048 + h * 512
                                                   + (i + 1) * 128],
                                         rhs=ksb[:, h * T + j * 512:
                                                 h * T + j * 512 + wdt],
                                         start=True, stop=True)
                                    t.mark(M, ("t", "r2s", qb, h, i, j))
                                    if mode == "plan":
                                        rlist.append((qb, h, i, j))
                                        rindex[(qb, h, i, j)] = len(rlist) - 1
                                    rch += 1
                            if u >= 2:
                                t.wait(M, ("v", "p2zev", (u - 2) // 4, (u - 2) % 4))

                            def _pv(j):
                                if j >= 4 * qb:
                                    t.wait(M, ("v", "x2m", qb, h, j))
                                else:
                                    t.wait(M, ("s", "x2e", qb, h, j))
                                t.op("matmul", out=zbank[:, 0:512],
                                     lhsT=vsb[:, j * 512 + h * 128:
                                             j * 512 + (h + 1) * 128],
                                     rhs=ptring[:, (j % 4) * 512:
                                               (j % 4) * 512 + 512],
                                     start=(j == 0), stop=(j == NT - 1))
                                t.mark(M, ("t", "x2pv", qb, h, j))

                            for j in range(NT):
                                if j >= 2:
                                    t.wait(M, ("s", "x2e", qb, h, j - 2))
                                t.op("matmul", out=ps[2 + j % 2][:, 0:512],
                                     lhsT=ksb[:, h * T + j * 128:
                                             h * T + (j + 1) * 128],
                                     rhs=qblk[:, (qb % 2) * 2048 + h * 512:
                                             (qb % 2) * 2048 + (h + 1) * 512],
                                     start=True, stop=True)
                                t.mark(M, ("t", "x2s", qb, h, j))
                                if j >= 1:
                                    _pv(j - 1)
                            _pv(NT - 1)
                            t.wait(M, ("v", "r2r", qb, h))
                            t.wait(M, ("s", "r2e") + rlist[rch - 1])
                            for i in range(4):
                                t.op("matmul",
                                     out=ps[0][0:1, i * 128:(i + 1) * 128],
                                     lhsT=rsums[:, 8 + i:9 + i], rhs=sidf[:],
                                     start=True, stop=True)
                            t.mark(M, ("t", "r2t", qb, h))
                            t.wait(M, ("v", "r2T", qb, h))
                            t.op("matmul", out=ps[1][:, 0:512],
                                 lhsT=sones_r[:], rhs=rrow[0:1, 0:512],
                                 start=True, stop=True)
                            t.mark(M, ("t", "r2b", qb, h))
                        t.mark(M, ("t", "p2send", qb))
                sec("t", p2_t)

                def p2_v(v):
                    v.wait(M, ("g", "p1sp", NCH - 1))
                    v.op("memset", rsums[:], 0.0)
                    v.wait_self()
                    for qb in range(8):
                        NT = 4 * qb + 4
                        for h in range(4):
                            u = qb * 4 + h
                            for i in range(4):
                                qt = qb * 4 + i
                                L = qt + 1
                                nch = (L + 3) // 4
                                wdt = L * 128 - (nch - 1) * 512
                                gp = rindex[(qb, h, i, nch - 1)] % 2
                                for j in range(nch):
                                    if j == nch - 1:
                                        v.wait(M, ("t", "r2s", qb, h, i, j))
                                        v.op("tensor_tensor",
                                             out=ps[gp][:, wdt - 128:wdt],
                                             in0=ps[gp][:, wdt - 128:wdt],
                                             in1=smk[:], op=ALU.add)
                                        v.wait_self()
                                        v.mark(M, ("v", "r2m", qb, h, i))
                                    v.wait(M, ("s", "r2e", qb, h, i, j))
                                    if j == 0:
                                        v.op("tensor_copy",
                                             out=rsums[:, i:i + 1],
                                             in_=sc("psum_r"))
                                    else:
                                        v.op("tensor_tensor",
                                             out=rsums[:, i:i + 1],
                                             in0=rsums[:, i:i + 1],
                                             in1=sc("psum_r"), op=ALU.add)
                                    v.wait_self()
                                    v.mark(M, ("v", "r2a", qb, h, i, j))
                            for j in range(4 * qb, NT):
                                ji = j - 4 * qb
                                v.wait(M, ("s", "x2e", qb, h, j))
                                for ci in range(4):
                                    dst = ptring[:, (j % 4) * 512 + ci * 128:
                                                 (j % 4) * 512 + (ci + 1) * 128]
                                    if ci < ji:
                                        v.op("memset", dst, 0.0)
                                    elif ci == ji:
                                        v.op("tensor_tensor", out=dst, in0=dst,
                                             in1=sm01[:], op=ALU.mult)
                                v.wait_self()
                                v.mark(M, ("v", "x2m", qb, h, j))
                            if u >= 1:
                                v.wait(M, ("t", "r2t", (u - 1) // 4, (u - 1) % 4))
                            v.op("reciprocal", out=rsums[:, 8:12],
                                 in_=rsums[:, 0:4])
                            v.wait_self()
                            v.mark(M, ("v", "r2r", qb, h))
                            v.wait(M, ("t", "r2t", qb, h))
                            v.op("tensor_copy", out=rrow[0:1, 0:512],
                                 in_=ps[0][0:1, 0:512])
                            v.wait_self()
                            v.mark(M, ("v", "r2T", qb, h))
                            v.wait(M, ("t", "r2b", qb, h))
                            v.op("tensor_copy", out=rssb[:], in_=ps[1][:, 0:512])
                            v.wait_self()
                            v.mark(M, ("v", "r2B", qb, h))
                            v.wait(M, ("t", "x2pv", qb, h, NT - 1))
                            if u >= 1:
                                pu = u - 1
                                v.wait(M, ("g", "p2z", pu // 4, pu % 4))
                            v.op("tensor_tensor", out=zev[:],
                                 in0=ps[4 + (u % 2)][:, 0:512], in1=rssb[:],
                                 op=ALU.mult)
                            v.wait_self()
                            v.mark(M, ("v", "p2zev", qb, h))
                        if wantB:
                            p0_v_passB(32)
                sec("v", p2_v)

                def p2_s(s):
                    for qb in range(8):
                        NT = 4 * qb + 4
                        for h in range(4):
                            u = qb * 4 + h
                            for i in range(4):
                                qt = qb * 4 + i
                                L = qt + 1
                                nch = (L + 3) // 4
                                for j in range(nch):
                                    wdt = min(512, L * 128 - j * 512)
                                    t_key = ("t", "r2s", qb, h, i, j)
                                    s.wait(M, t_key)
                                    if j == nch - 1:
                                        s.wait(M, ("v", "r2m", qb, h, i))
                                    pk = _prev_rchunk(qb, h, i, j)
                                    if pk is not None:
                                        s.wait(M, ("v", "r2a", *pk))
                                    gp = rindex[(qb, h, i, j)] % 2
                                    s.op("activation", out=sexpb[:, 0:wdt],
                                         in_=ps[gp][:, 0:wdt], func=AF.Exp,
                                         bias=0.0, scale=1.0,
                                         accum_out=sc("psum_r"))
                                    s.mark(M, ("s", "r2e", qb, h, i, j))
                            for j in range(NT):
                                s.wait(M, ("t", "x2s", qb, h, j))
                                if j >= 4:
                                    s.wait(M, ("t", "x2pv", qb, h, j - 4))
                                elif u >= 1:
                                    pu = u - 1
                                    pNT = 4 * (pu // 4) + 4
                                    s.wait(M, ("t", "x2pv", pu // 4, pu % 4,
                                               pNT - 1))
                                s.op("activation",
                                     out=ptring[:, (j % 4) * 512:
                                               (j % 4) * 512 + 512],
                                     in_=ps[2 + j % 2][:, 0:512], func=AF.Exp,
                                     bias=0.0, scale=1.0)
                                s.mark(M, ("s", "x2e", qb, h, j))
                        if wantB:
                            EN["p"].e = s.e
                            p0_g_passB(32)
                sec("s", p2_s)

                def p2_g(g):
                    for qb in range(8):
                        d1, cofs = qb // 2, (qb % 2) * 512
                        for h in range(4):
                            g.wait(M, ("v", "p2zev", qb, h))
                            g.dma(zparts[d1][h * 128:(h + 1) * 128,
                                             cofs:cofs + 512], zev[:])
                            g.dma(zparts[4 + d1][h * 128:(h + 1) * 128,
                                                 cofs:cofs + 512], zev[:])
                            g.mark(M, ("g", "p2z", qb, h))
                sec("g", p2_g)

                if wantB:
                    def p0_drain_v(v):
                        p0_v_passB(len(p0B))
                    def p0_drain_p(s):
                        EN["p"].e = s.e
                        p0_g_passB(len(p0B))
                        p0_g_store(len(p0B) - 1)
                        EN["p"].mark(M, ("g", "p0done"))
                    sec("v", p0_drain_v)
                    sec("s", p0_drain_p)
            elif wantB:
                def p0_all_g(s):
                    EN["p"].e = s.e
                    EN["p"].wait(M, ("v", "init"))
                    p0_g_passA(EN["p"], len(p0A))
                sec("s", p0_all_g)

                def p0_all_v(v):
                    p0_v_passA(len(p0A))
                sec("v", p0_all_v)

                def pBr_t(t):
                    t.wait(M, ("v", "p0A", len(p0A) - 1))
                    t.op("matmul", out=ps[2][0:1, 0:2], lhsT=sones[:],
                         rhs=st_[:, 74:76], start=True, stop=True)
                    t.mark(M, ("t", "redA1"))
                    t.wait(M, ("v", "redAsc"))
                    t.op("matmul", out=ps[2][:, 4:6], lhsT=sones_r[:],
                         rhs=st_[0:1, 76:78], start=True, stop=True)
                    t.mark(M, ("t", "redA2"))
                sec("t", pBr_t)

                def pBr_v(v):
                    v.wait(M, ("t", "redA1"))
                    v.op("tensor_scalar", out=st_[0:1, 76:77],
                         in0=ps[2][0:1, 0:1], scalar1=1.0 / N_W1,
                         scalar2=EPS_W, op0=ALU.mult, op1=ALU.max)
                    v.op("tensor_scalar", out=st_[0:1, 77:78],
                         in0=ps[2][0:1, 1:2], scalar1=1.0 / N_W2,
                         scalar2=EPS_W, op0=ALU.mult, op1=ALU.max)
                    v.wait_self()
                    v.mark(M, ("v", "redAsc"))
                    v.wait(M, ("t", "redA2"))
                    v.op("tensor_copy", out=sc("sw1m"), in_=ps[2][:, 4:5])
                    v.op("tensor_copy", out=sc("sw2m"), in_=ps[2][:, 5:6])
                    v.wait_self()
                    v.op("tensor_scalar", out=sc("thr1"), in0=sc("sw1m"),
                         scalar1=0.5, scalar2=None, op0=ALU.mult)
                    v.op("tensor_scalar", out=sc("thr2"), in0=sc("sw2m"),
                         scalar1=0.5, scalar2=None, op0=ALU.mult)
                    v.wait_self()
                    v.op("tensor_scalar", out=sc("nthr1"), in0=sc("thr1"),
                         scalar1=-1.0, scalar2=None, op0=ALU.mult)
                    v.op("tensor_scalar", out=sc("nthr2"), in0=sc("thr2"),
                         scalar1=-1.0, scalar2=None, op0=ALU.mult)
                    v.wait_self()
                    v.mark(M, ("v", "redA"))
                    p0_v_passB(len(p0B))
                sec("v", pBr_v)

                def p0B_g(s):
                    EN["p"].e = s.e
                    p0_g_passB(len(p0B))
                    p0_g_store(len(p0B) - 1)
                    EN["p"].mark(M, ("g", "p0done"))
                sec("s", p0B_g)

            if part == "ALL":
                def ag_g(g):
                    g.wait(M, ("g", "p2z", 7, 3))
                    cc_ = EN["cc"]
                    if mode == "plan":
                        cc_.cnt[0] += 1
                    else:
                        cc_.e = g.e
                        inst = g.e.collective_compute(
                            "AllToAll", ALU.bypass,
                            replica_groups=[list(range(8))],
                            ins=[zparts.opt()], outs=[za_out.opt()])
                        inst.then_inc(cc_.sems[0], 1)
                        cc_.cnt[0] += 1
                    M[("cc", "a2a")] = EN["cc"].cnt[0]
                sec("g", ag_g)

            if wantB:
                def p3_q(q):
                    if wantA:
                        q.wait(M, ("t", "p2send", 7))
                        q.wait(M, ("v", "p2zev", 7, 3))
                        q.wait(M, ("s", "x2e", 7, 3, 31))
                    if part == "ALL":
                        q.wait_n(EN["cc"], M[("cc", "a2a")])
                    for rb in range(4):
                        for pt in range(4):
                            c = rb * 4 + pt
                            if c >= 1:
                                q.wait(M, ("v", "p3cmb", c - 1))
                            q.dma(ztmp[:, 0:TQ],
                                  za_out[rb][pt * 128:(pt + 1) * 128, :])
                            q.dma(ztmp[:, TQ:2 * TQ],
                                  za_out[4 + rb][pt * 128:(pt + 1) * 128, :])
                            q.mark(M, ("q", "p3z", c))
                    for d in range(8):
                        if d >= 1:
                            q.wait(M, ("t", "p3mm", d - 1, 7))
                        for c in range(16):
                            q.dma(wosb[:, c * 256:(c + 1) * 256],
                                  woT[c * 128:(c + 1) * 128,
                                      d * 256:(d + 1) * 256])
                        q.mark(M, ("q", "p3wo", d))
                        for ti in range(8):
                            u = d * 8 + ti
                            if u >= 2:
                                q.wait(M, ("v", "p3ev", (u - 2) // 8, (u - 2) % 8))
                            q.dma(xrb[:, (ti % 2) * 256:(ti % 2) * 256 + 256],
                                  xres[ti * 128:(ti + 1) * 128,
                                       d * 256:(d + 1) * 256])
                            q.mark(M, ("q", "p3xr", d, ti))
                sec("q", p3_q)

                def p3comb_v(v):
                    for rb in range(4):
                        for pt in range(4):
                            c = rb * 4 + pt
                            v.wait(M, ("q", "p3z", c))
                            v.op("tensor_scalar", out=zqsb[:, c * TQ:(c + 1) * TQ],
                                 in0=ztmp[:, 0:TQ], scalar1=smb[:, 0:1],
                                 scalar2=None, op0=ALU.mult)
                            v.op("tensor_scalar", out=ztmp[:, TQ:2 * TQ],
                                 in0=ztmp[:, TQ:2 * TQ], scalar1=smb[:, 1:2],
                                 scalar2=None, op0=ALU.mult)
                            v.wait_self()
                            v.op("tensor_tensor",
                                 out=zqsb[:, c * TQ:(c + 1) * TQ],
                                 in0=zqsb[:, c * TQ:(c + 1) * TQ],
                                 in1=ztmp[:, TQ:2 * TQ], op=ALU.add)
                            v.wait_self()
                            v.mark(M, ("v", "p3cmb", c))
                sec("v", p3comb_v)

                def p3_t(t):
                    for d in range(8):
                        t.wait(M, ("q", "p3wo", d))
                        for ti in range(8):
                            u = d * 8 + ti
                            bank = ps[u % 6]
                            if u >= 6:
                                pu = u - 6
                                t.wait(M, ("v", "p3ev", pu // 8, pu % 8))
                            for c in range(16):
                                if d == 0 and ti == 0:
                                    t.wait(M, ("v", "p3cmb", c))
                                t.op("matmul", out=bank[:, 0:256],
                                     lhsT=zqsb[:, c * TQ + ti * 128:
                                              c * TQ + (ti + 1) * 128],
                                     rhs=wosb[:, c * 256:(c + 1) * 256],
                                     start=(c == 0), stop=(c == 15))
                            t.mark(M, ("t", "p3mm", d, ti))
                sec("t", p3_t)

                def p3_v(v):
                    for d in range(8):
                        for ti in range(8):
                            u = d * 8 + ti
                            bank = ps[u % 6]
                            v.wait(M, ("t", "p3mm", d, ti))
                            v.wait(M, ("q", "p3xr", d, ti))
                            if u >= 2:
                                v.wait(M, ("g", "p3st", (u - 2) // 8, (u - 2) % 8))
                            v.op("tensor_tensor",
                                 out=x1sb[:, (ti % 2) * 256:(ti % 2) * 256 + 256],
                                 in0=bank[:, 0:256],
                                 in1=xrb[:, (ti % 2) * 256:(ti % 2) * 256 + 256],
                                 op=ALU.add)
                            v.wait_self()
                            v.mark(M, ("v", "p3ev", d, ti))
                sec("v", p3_v)

                def p3_g(g):
                    for d in range(8):
                        for ti in range(8):
                            g.wait(M, ("v", "p3ev", d, ti))
                            g.dma(x1_d[ti][:, d * 256:(d + 1) * 256],
                                  x1sb[:, (ti % 2) * 256:(ti % 2) * 256 + 256])
                            g.mark(M, ("g", "p3st", d, ti))
                sec("g", p3_g)

                def p3b_q(q):
                    for ti in range(8):
                        q.wait(M, ("g", "p3st", 7, ti))
                        if ti >= 2:
                            q.wait(M, ("s", "p3bymul", ti - 2, 1))
                        q.dma(x1row[:, (ti % 2) * D:(ti % 2) * D + D], x1_d[ti])
                        q.mark(M, ("q", "p3bx", ti))
                sec("q", p3b_q)

                def p3b_s(s):
                    s.wait(M, ("v", "p0B", len(p0B) - 1))
                    s.wait(M, ("g", "p0Bst", len(p0B) - 1))
                    for ti in range(8):
                        xt = x1row[:, (ti % 2) * D:(ti % 2) * D + D]
                        s.wait(M, ("v", "p3bvar", ti))
                        s.op("activation", out=sc("sd"), in_=sc("var"),
                             func=AF.Sqrt, bias=0.0, scale=1.0)
                        s.mark(M, ("s", "p3bsd", ti))
                        s.wait(M, ("v", "p3bnb", ti))
                        if ti >= 1:
                            s.wait(M, ("v", "p3bround", ti - 1, 1))
                        s.op("activation", out=xn2[:], in_=xt, func=AF.Identity,
                             scale=sc("rs"), bias=sc("nb"))
                        s.mark(M, ("s", "p3bapply", ti))
                        s.wait(M, ("v", "p3bss2v", ti))
                        s.op("activation", out=sc("n1"), in_=sc("n1"),
                             func=AF.Sqrt, bias=0.0, scale=1.0)
                        s.mark(M, ("s", "p3bsqrt2", ti))
                        s.wait(M, ("v", "p3bmu", ti))
                        for hh in range(2):
                            s.op("activation", out=ybuf[:],
                                 in_=xn2[:, hh * 1024:(hh + 1) * 1024],
                                 func=AF.Copy,
                                 scale=st_[:, MU1 + ti:MU1 + ti + 1], bias=0.0)
                            s.mark(M, ("s", "p3bymul", ti, hh))
                            if hh == 0:
                                s.wait(M, ("v", "p3bround", ti, 0))
                sec("s", p3b_s)

                def p3b_v(v):
                    v.wait(M, ("g", "p0Bst", len(p0B) - 1))
                    for ti in range(8):
                        xt = x1row[:, (ti % 2) * D:(ti % 2) * D + D]
                        v.wait(M, ("q", "p3bx", ti))
                        for bq in range(4):
                            v.op("bn_stats", out=st_[:, 96 + 6 * bq:
                                                     96 + 6 * bq + 6],
                                 in_=xt[:, bq * 512:(bq + 1) * 512])
                        v.wait_self()
                        v.op("bn_aggr", out=st_[:, 86:88],
                             in_=st_[:, 96:120])
                        v.wait_self()
                        v.op("tensor_scalar", out=sc("var"),
                             in0=st_[:, 87:88], scalar1=EPS_LN,
                             scalar2=None, op0=ALU.add)
                        v.op("tensor_copy", out=sc("m"), in_=st_[:, 86:87])
                        v.wait_self()
                        v.mark(M, ("v", "p3bvar", ti))
                        v.wait(M, ("s", "p3bsd", ti))
                        v.op("reciprocal", out=sc("rs"), in_=sc("sd"))
                        v.wait_self()
                        v.op("scalar_tensor_tensor", out=sc("nb"), in0=sc("m"),
                             scalar=-1.0, in1=sc("rs"), op0=ALU.mult,
                             op1=ALU.mult)
                        v.wait_self()
                        v.mark(M, ("v", "p3bnb", ti))
                        v.wait(M, ("s", "p3bapply", ti))
                        v.op("reduce_max", out=sc("mab"), in_=xn2[:], axis=AX.X,
                             apply_absolute_value=True)
                        v.wait_self()
                        v.op("tensor_tensor", out=sc("n1"), in0=st_[:, 87:88],
                             in1=sc("rs"), op=ALU.mult)
                        v.wait_self()
                        v.op("tensor_tensor", out=sc("n1"), in0=sc("n1"),
                             in1=sc("rs"), op=ALU.mult)
                        v.wait_self()
                        v.op("tensor_scalar", out=sc("n1"), in0=sc("n1"),
                             scalar1=float(D), scalar2=None, op0=ALU.mult)
                        v.wait_self()
                        v.mark(M, ("v", "p3bss2v", ti))
                        v.wait(M, ("s", "p3bsqrt2", ti))
                        v.op("tensor_scalar", out=sc("n1"), in0=sc("n1"),
                             scalar1=EPS_RMS, scalar2=None, op0=ALU.max)
                        v.wait_self()
                        v.op("reciprocal", out=sc("rc"), in_=sc("n1"))
                        v.wait_self()
                        v.op("tensor_scalar", out=sc("rr"), in0=sc("rc"),
                             scalar1=SQD, scalar2=None, op0=ALU.mult)
                        v.wait_self()
                        v.op("tensor_tensor", out=sc("mx"), in0=sc("mab"),
                             in1=sc("rr"), op=ALU.mult)
                        v.wait_self()
                        v.op("tensor_scalar", out=sc("mxc"), in0=sc("mx"),
                             scalar1=EPS_Q, scalar2=None, op0=ALU.max)
                        v.wait_self()
                        v.op("reciprocal", out=sc("si"), in_=sc("mxc"))
                        v.wait_self()
                        v.op("scalar_tensor_tensor",
                             out=st_[:, MU1 + ti:MU1 + ti + 1], in0=sc("rr"),
                             scalar=127.0, in1=sc("si"), op0=ALU.mult,
                             op1=ALU.mult)
                        v.op("scalar_tensor_tensor",
                             out=st_[:, CS1 + ti:CS1 + ti + 1], in0=sc("mxc"),
                             scalar=-1.0 / 127.0, in1=sc("sw1m"), op0=ALU.mult,
                             op1=ALU.mult)
                        v.wait_self()
                        v.mark(M, ("v", "p3bmu", ti))
                        for hh in range(2):
                            v.wait(M, ("s", "p3bymul", ti, hh))
                            emit_round(v, xqb[:, hh * 1024:(hh + 1) * 1024],
                                       ybuf[:], tA[:], tB[:], tS[:])
                            v.wait_self()
                            v.mark(M, ("v", "p3bround", ti, hh))
                        for k in range(16):
                            v.wait(M, ("t", "p3btr", ti, k))
                            v.op("tensor_copy",
                                 out=xqT[:, k * TQ + ti * 128:
                                         k * TQ + (ti + 1) * 128],
                                 in_=pbf[k % 2][:, 0:128])
                            v.wait_self()
                            v.mark(M, ("v", "p3bxqev", ti, k))
                sec("v", p3b_v)

                def p3b_t(t):
                    for ti in range(8):
                        t.wait(M, ("v", "p3bround", ti, 1))
                        for k in range(16):
                            if k >= 2:
                                t.wait(M, ("v", "p3bxqev", ti, k - 2))
                            elif ti >= 1:
                                t.wait(M, ("v", "p3bxqev", ti - 1, 14 + k))
                            t.op("transpose", out=pbf[k % 2][:, 0:128],
                                 in_=xqb[:, k * 128:(k + 1) * 128],
                                 identity=sidb[:])
                            t.mark(M, ("t", "p3btr", ti, k))
                sec("t", p3b_t)

                def p4a_q(q):
                    q.wait(M, ("g", "p0done"))
                    q.wait(M, ("t", "p3mm", 7, 7))
                    q.wait(M, ("g", "p3st", 7, 7))
                    q.wait(M, ("s", "p3bymul", 7, 1))
                    for icg in range(8):
                        if icg >= 1:
                            q.wait(M, ("t", "p4amm", icg - 1, 1, 7))
                        for c in range(16):
                            q.dma(w1sb[:, c * 1024:(c + 1) * 1024],
                                  w1q_d[icg, c])
                        q.mark(M, ("q", "p4aw", icg))
                sec("q", p4a_q)

                def p4a_t(t):
                    t.wait(M, ("v", "p3bxqev", 7, 15))
                    for icg in range(8):
                        t.wait(M, ("q", "p4aw", icg))
                        for sub in range(2):
                            for ti in range(8):
                                u = icg * 16 + sub * 8 + ti
                                bank = ps[u % 6]
                                if u >= 6:
                                    pu = u - 6
                                    t.wait(M, ("s", "p4agel1",
                                               pu // 16, (pu // 8) % 2, pu % 8))
                                for c in range(16):
                                    t.op("matmul", out=bank[:],
                                         lhsT=xqT[:, c * TQ + ti * 128:
                                                  c * TQ + (ti + 1) * 128],
                                         rhs=w1sb[:, c * 1024 + sub * 512:
                                                  c * 1024 + sub * 512 + 512],
                                         start=(c == 0), stop=(c == 15))
                                t.mark(M, ("t", "p4amm", icg, sub, ti))
                sec("t", p4a_t)

                def p4a_s(s):
                    for icg in range(8):
                        for sub in range(2):
                            for ti in range(8):
                                u = icg * 16 + sub * 8 + ti
                                bank = ps[u % 6]
                                slot = (u % 2) * 2
                                s.wait(M, ("t", "p4amm", icg, sub, ti))
                                if u >= 2:
                                    pu = u - 2
                                    s.wait(M, ("g", "p4ast",
                                               pu // 16, (pu // 8) % 2, pu % 8))
                                s.op("activation",
                                     out=hb[:, slot * 512:slot * 512 + 512],
                                     in_=bank[:], func=AF.Gelu,
                                     scale=st_[:, CS1 + ti:CS1 + ti + 1],
                                     bias=0.0)
                                s.mark(M, ("s", "p4agel1", icg, sub, ti))
                                s.wait_self()
                                s.op("activation",
                                     out=hb[:, (slot + 1) * 512:
                                             (slot + 1) * 512 + 512],
                                     in_=hb[:, slot * 512:slot * 512 + 512],
                                     func=AF.Gelu, bias=0.0, scale=1.0)
                                s.wait_self()
                                if u >= 1:
                                    pv = u - 1
                                    s.wait(M, ("v", "p4av", pv // 16,
                                               (pv // 8) % 2, pv % 8))
                                s.op("activation",
                                     out=hb[:, slot * 512:slot * 512 + 512],
                                     in_=hb[:, (slot + 1) * 512:
                                             (slot + 1) * 512 + 512],
                                     func=AF.Square, bias=0.0, scale=1.0,
                                     accum_out=sc("ssq"))
                                s.mark(M, ("s", "p4asq", icg, sub, ti))
                sec("s", p4a_s)

                def p4a_v(v):
                    v.op("memset", st_[:, SS2:SS2 + 8], 0.0)
                    v.op("memset", st_[:, MX2:MX2 + 8], 0.0)
                    v.wait_self()
                    for icg in range(8):
                        for sub in range(2):
                            for ti in range(8):
                                u = icg * 16 + sub * 8 + ti
                                slot = (u % 2) * 2
                                v.wait(M, ("s", "p4asq", icg, sub, ti))
                                v.op("tensor_tensor",
                                     out=st_[:, SS2 + ti:SS2 + ti + 1],
                                     in0=st_[:, SS2 + ti:SS2 + ti + 1],
                                     in1=sc("ssq"), op=ALU.add)
                                v.op("reduce_max", out=sc("mab"),
                                     in_=hb[:, (slot + 1) * 512:
                                             (slot + 1) * 512 + 512],
                                     axis=AX.X, apply_absolute_value=True)
                                v.wait_self()
                                v.op("tensor_tensor",
                                     out=st_[:, MX2 + ti:MX2 + ti + 1],
                                     in0=st_[:, MX2 + ti:MX2 + ti + 1],
                                     in1=sc("mab"), op=ALU.max)
                                v.wait_self()
                                v.mark(M, ("v", "p4av", icg, sub, ti))
                sec("v", p4a_v)

                def p4a_g(g):
                    for icg in range(8):
                        for sub in range(2):
                            ic = icg * 2 + sub
                            for ti in range(8):
                                u = icg * 16 + sub * 8 + ti
                                slot = (u % 2) * 2
                                g.wait(M, ("v", "p4av", icg, sub, ti))
                                g.dma(h_d[ti][:, ic * 512:(ic + 1) * 512],
                                      hb[:, (slot + 1) * 512:
                                         (slot + 1) * 512 + 512])
                                g.mark(M, ("g", "p4ast", icg, sub, ti))
                sec("g", p4a_g)

                def p4b_q(q):
                    q.wait(M, ("g", "p4ast", 7, 1, 7))
                    q.wait(M, ("t", "p4amm", 7, 1, 7))
                    q.wait(M, ("s", "p4asq", 7, 1, 7))
                    q.wait(M, ("v", "p4av", 7, 1, 7))
                    for th in range(2):
                        for tl in range(4):
                            ti = th * 4 + tl
                            for ch in range(8):
                                u = (th * 4 + tl) * 8 + ch
                                if u >= 2:
                                    pu = u - 2
                                    q.wait(M, ("s", "p4bymul",
                                               pu // 32, (pu // 8) % 4, pu % 8))
                                q.wait(M, ("g", "p4ast", 7, 1, ti))
                                q.dma(hrow[:, (ch % 2) * 1024:
                                           (ch % 2) * 1024 + 1024],
                                      h_d[ti][:, ch * 1024:(ch + 1) * 1024])
                                q.mark(M, ("q", "p4bh", th, tl, ch))
                        for d in range(4):
                            if (th, d) != (0, 0):
                                pth, pd = (th, d - 1) if d else (th - 1, 3)
                                q.wait(M, ("t", "p4bmm", pth, pd, 3))
                            for c in range(64):
                                q.dma(w2sb[:, c * 512:(c + 1) * 512],
                                      w2q_d[d, c])
                            q.mark(M, ("q", "p4bw", th, d))
                            for tl in range(4):
                                ti = th * 4 + tl
                                u = d * 4 + tl
                                if u >= 2 or th > 0:
                                    pu = u - 2 if u >= 2 else 14 + u
                                    pth = th if u >= 2 else th - 1
                                    q.wait(M, ("v", "p4badd",
                                               pth, pu // 4, pu % 4))
                                q.dma(xr2[:, (tl % 2) * 512:(tl % 2) * 512 + 512],
                                      x1_d[ti][:, d * 512:(d + 1) * 512])
                                q.mark(M, ("q", "p4bxr", th, d, tl))
                sec("q", p4b_q)

                def p4b_s(s):
                    for th in range(2):
                        for tl in range(4):
                            ti = th * 4 + tl
                            s.wait(M, ("v", "p4av", 7, 1, 7))
                            s.op("activation", out=sc("sd"),
                                 in_=st_[:, SS2 + ti:SS2 + ti + 1],
                                 func=AF.Sqrt, bias=0.0, scale=1.0)
                            s.mark(M, ("s", "p4bsqrt", th, tl))
                            for ch in range(8):
                                s.wait(M, ("q", "p4bh", th, tl, ch))
                                s.wait(M, ("v", "p4bmu", th, tl))
                                u = (th * 4 + tl) * 8 + ch
                                if u >= 1:
                                    pu = u - 1
                                    s.wait(M, ("v", "p4bround",
                                               pu // 32, (pu // 8) % 4, pu % 8))
                                s.op("activation", out=ybuf[:],
                                     in_=hrow[:, (ch % 2) * 1024:
                                             (ch % 2) * 1024 + 1024],
                                     func=AF.Copy,
                                     scale=st_[:, MU2 + ti:MU2 + ti + 1],
                                     bias=0.0)
                                s.mark(M, ("s", "p4bymul", th, tl, ch))
                        for d in range(4):
                            for tl in range(4):
                                ti = th * 4 + tl
                                bank = ps[(d * 4 + tl) % 6]
                                s.wait(M, ("t", "p4bmm", th, d, tl))
                                u = d * 4 + tl
                                if u >= 1 or th > 0:
                                    pu = u - 1 if u >= 1 else 15
                                    pth = th if u >= 1 else th - 1
                                    s.wait(M, ("v", "p4badd",
                                               pth, pu // 4, pu % 4))
                                s.op("activation", out=tmp5[:], in_=bank[:],
                                     func=AF.Copy,
                                     scale=st_[:, CS2 + ti:CS2 + ti + 1],
                                     bias=0.0)
                                s.mark(M, ("s", "p4bcp", th, d, tl))
                sec("s", p4b_s)

                def p4b_v(v):
                    for th in range(2):
                        for tl in range(4):
                            ti = th * 4 + tl
                            v.wait(M, ("s", "p4bsqrt", th, tl))
                            v.op("tensor_scalar", out=sc("n1"), in0=sc("sd"),
                                 scalar1=EPS_RMS, scalar2=None, op0=ALU.max)
                            v.wait_self()
                            v.op("reciprocal", out=sc("rc"), in_=sc("n1"))
                            v.wait_self()
                            v.op("tensor_scalar", out=sc("rr"), in0=sc("rc"),
                                 scalar1=SQI, scalar2=None, op0=ALU.mult)
                            v.wait_self()
                            v.op("tensor_tensor", out=sc("mx"),
                                 in0=st_[:, MX2 + ti:MX2 + ti + 1],
                                 in1=sc("rr"), op=ALU.mult)
                            v.wait_self()
                            v.op("tensor_scalar", out=sc("mxc"), in0=sc("mx"),
                                 scalar1=EPS_Q, scalar2=None, op0=ALU.max)
                            v.wait_self()
                            v.op("reciprocal", out=sc("si"), in_=sc("mxc"))
                            v.wait_self()
                            v.op("scalar_tensor_tensor",
                                 out=st_[:, MU2 + ti:MU2 + ti + 1], in0=sc("rr"),
                                 scalar=127.0, in1=sc("si"), op0=ALU.mult,
                                 op1=ALU.mult)
                            v.op("scalar_tensor_tensor",
                                 out=st_[:, CS2 + ti:CS2 + ti + 1],
                                 in0=sc("mxc"), scalar=-1.0 / 127.0,
                                 in1=sc("sw2m"), op0=ALU.mult, op1=ALU.mult)
                            v.wait_self()
                            v.mark(M, ("v", "p4bmu", th, tl))
                            for ch in range(8):
                                v.wait(M, ("s", "p4bymul", th, tl, ch))
                                if th == 1:
                                    v.wait(M, ("t", "p4bmm", 0, 3, 3))
                                emit_round(v, xq2b[:], ybuf[:], tA[:], tB[:],
                                           tS[:])
                                v.wait_self()
                                v.mark(M, ("v", "p4bround", th, tl, ch))
                                for k in range(8):
                                    v.wait(M, ("t", "p4btr", th, tl, ch, k))
                                    kk = ch * 8 + k
                                    v.op("tensor_copy",
                                         out=xq2T[:, kk * 512 + tl * 128:
                                                 kk * 512 + (tl + 1) * 128],
                                         in_=pbf[k % 2][:, 128:256])
                                    v.wait_self()
                                    v.mark(M, ("v", "p4bxqev", th, tl, ch, k))
                        for d in range(4):
                            for tl in range(4):
                                v.wait(M, ("s", "p4bcp", th, d, tl))
                                v.wait(M, ("q", "p4bxr", th, d, tl))
                                u = d * 4 + tl
                                if u >= 2 or th > 0:
                                    pu = u - 2 if u >= 2 else 14 + u
                                    pth = th if u >= 2 else th - 1
                                    v.wait(M, ("g", "p4by",
                                               pth, pu // 4, pu % 4))
                                v.op("tensor_tensor",
                                     out=yo[:, (tl % 2) * 512:(tl % 2) * 512 + 512],
                                     in0=tmp5[:],
                                     in1=xr2[:, (tl % 2) * 512:(tl % 2) * 512 + 512],
                                     op=ALU.add)
                                v.wait_self()
                                v.mark(M, ("v", "p4badd", th, d, tl))
                sec("v", p4b_v)

                def p4b_t(t):
                    for th in range(2):
                        for tl in range(4):
                            for ch in range(8):
                                t.wait(M, ("v", "p4bround", th, tl, ch))
                                for k in range(8):
                                    u = ((th * 4 + tl) * 8 + ch) * 8 + k
                                    if u >= 2:
                                        pu = u - 2
                                        t.wait(M, ("v", "p4bxqev",
                                                   pu // 256, (pu // 64) % 4,
                                                   (pu // 8) % 8, pu % 8))
                                    t.op("transpose", out=pbf[k % 2][:, 128:256],
                                         in_=xq2b[:, k * 128:(k + 1) * 128],
                                         identity=sidb[:])
                                    t.mark(M, ("t", "p4btr", th, tl, ch, k))
                        for d in range(4):
                            t.wait(M, ("q", "p4bw", th, d))
                            for tl in range(4):
                                bank = ps[(d * 4 + tl) % 6]
                                u = d * 4 + tl
                                if u >= 6 or th > 0:
                                    pu = u - 6 if u >= 6 else 10 + u
                                    pth = th if u >= 6 else th - 1
                                    t.wait(M, ("s", "p4bcp", pth, pu // 4, pu % 4))
                                t.wait(M, ("v", "p4bxqev", th, 3, 7, 7))
                                for c in range(64):
                                    t.op("matmul", out=bank[:],
                                         lhsT=xq2T[:, c * 512 + tl * 128:
                                                  c * 512 + (tl + 1) * 128],
                                         rhs=w2sb[:, c * 512:(c + 1) * 512],
                                         start=(c == 0), stop=(c == 63))
                                t.mark(M, ("t", "p4bmm", th, d, tl))
                sec("t", p4b_t)

                def p4b_g(g):
                    for th in range(2):
                        for d in range(4):
                            for tl in range(4):
                                ti = th * 4 + tl
                                g.wait(M, ("v", "p4badd", th, d, tl))
                                g.dma(y_out[ti * 128:(ti + 1) * 128,
                                            d * 512:(d + 1) * 512],
                                      yo[:, (tl % 2) * 512:(tl % 2) * 512 + 512])
                                g.mark(M, ("g", "p4by", th, d, tl))
                    if mode == "emit":
                        g.wait_total()
                sec("g", p4b_g)

            if part == "A":
                def pa_fin_g(g):
                    if mode == "emit":
                        g.wait_total()
                sec("g", pa_fin_g)

        M = {}
        CE = {k: CEng(16 if k in ("q", "g", "p") else 1, k)
              for k in ("q", "g", "p", "t", "v", "s", "cc")}
        block = None
        program(CE, M, "plan")

        sems = {}
        for k in ("q", "g", "p"):
            sems[k] = [ctx.enter_context(nc.semaphore(f"sem{k}{i}"))
                       for i in (0, 1)]
        for k in ("t", "v", "s", "cc"):
            sems[k] = [ctx.enter_context(nc.semaphore(f"sem{k}"))]
        RE = {k: REng(16 if k in ("q", "g", "p") else 1, sems[k], k)
              for k in ("q", "g", "p", "t", "v", "s", "cc")}
        M["_sems"] = sems
        block = ctx.enter_context(nc.Block())
        program(RE, M, "emit")
    return nc


def kernel(L0_attn_x, L0_attn_Wq, L0_attn_Wk, L0_attn_Wv, L0_attn_Wo,
           L0_ffn_w1, L0_ffn_w2):
    x = np.ascontiguousarray(np.asarray(L0_attn_x, dtype=np.float32))
    WqT = np.asarray(L0_attn_Wq, np.float32).T
    WkT = np.asarray(L0_attn_Wk, np.float32).T
    WvT = np.asarray(L0_attn_Wv, np.float32).T
    WoT = np.ascontiguousarray(np.asarray(L0_attn_Wo, np.float32).T)
    w1T = np.ascontiguousarray(np.asarray(L0_ffn_w1, np.float32).T)
    w2T = np.ascontiguousarray(np.asarray(L0_ffn_w2, np.float32).T)
    identf = np.eye(128, dtype=np.float32)
    maskd = np.triu(np.full((128, 128), NEG, dtype=np.float32), 1)
    mask01 = np.triu(np.ones((128, 128), dtype=np.float32))

    out = np.empty((B, T, D), dtype=np.float32)
    if FUSED:
        nc = build("ALL")
        in_maps = []
        for c in range(8):
            b, gq = c // 4, c % 4
            bm = np.zeros((128, 2), np.float32)
            bm[:, b] = 1.0
            in_maps.append({
                "x": np.ascontiguousarray(x[b]),
                "identf": identf, "maskd": maskd, "mask01": mask01, "bmask": bm,
                "wqT": np.ascontiguousarray(WqT[:, gq * DPC:(gq + 1) * DPC]),
                "wkT": np.ascontiguousarray(WkT[:, gq * DPC:(gq + 1) * DPC]),
                "wvT": np.ascontiguousarray(WvT[:, gq * DPC:(gq + 1) * DPC]),
                "xres": np.ascontiguousarray(x[b, gq * TQ:(gq + 1) * TQ]),
                "woT": WoT, "w1T": w1T, "w2T": w2T,
            })
        res = run_bass_kernel_spmd(nc, in_maps, core_ids=list(range(8)))
        for c in range(8):
            b, tq = c // 4, c % 4
            out[b, tq * TQ:(tq + 1) * TQ] = res.results[c]["y"]
    else:
        ncA = build("A")
        in_mapsA = []
        for c in range(8):
            b, gq = c // 4, c % 4
            in_mapsA.append({
                "x": np.ascontiguousarray(x[b]),
                "identf": identf, "maskd": maskd, "mask01": mask01,
                "wqT": np.ascontiguousarray(WqT[:, gq * DPC:(gq + 1) * DPC]),
                "wkT": np.ascontiguousarray(WkT[:, gq * DPC:(gq + 1) * DPC]),
                "wvT": np.ascontiguousarray(WvT[:, gq * DPC:(gq + 1) * DPC]),
            })
        resA = run_bass_kernel_spmd(ncA, in_mapsA, core_ids=list(range(8)))
        ncB = build("B")
        in_mapsB = []
        for c in range(8):
            b, tq = c // 4, c % 4
            bm = np.zeros((128, 2), np.float32)
            bm[:, b] = 1.0
            zq = np.stack([np.asarray(resA.results[(s // 4) * 4 + s % 4]
                                      ["zparts"])[c]
                           for s in range(8)])
            in_mapsB.append({
                "x": np.ascontiguousarray(x[b]),
                "identf": identf, "bmask": bm,
                "xres": np.ascontiguousarray(x[b, tq * TQ:(tq + 1) * TQ]),
                "woT": WoT, "w1T": w1T, "w2T": w2T,
                "zq_in": np.ascontiguousarray(zq),
            })
        resB = run_bass_kernel_spmd(ncB, in_mapsB, core_ids=list(range(8)))
        for c in range(8):
            b, tq = c // 4, c % 4
            out[b, tq * TQ:(tq + 1) * TQ] = resB.results[c]["y"]
    return out


# version 11 (final)
# speedup vs baseline: 1.1388x; 1.0379x over previous
import numpy as np
import os
os.environ.setdefault("TRN_ALLOW_ICI_ALL_TO_ALL", "1")
import concourse.bass as bass
import concourse.mybir as mybir
from concourse.bass_utils import run_bass_kernel_spmd
from contextlib import ExitStack

F32 = mybir.dt.float32
BF16 = mybir.dt.bfloat16
AF = mybir.ActivationFunctionType
ALU = mybir.AluOpType
AX = mybir.AxisListType

B, T, D, NH, HD, INNER = 2, 4096, 2048, 16, 128, 8192
HPC, DPC, TQ = 4, 512, 1024
ATT_SCALE = float(1.0 / np.sqrt(np.float64(HD)))
NEG = float(np.finfo(np.float32).min)
EPS_LN = 1e-5
EPS_Q = 1e-5
EPS_W = 1e-5
EPS_RMS = 1e-12
C32 = 3.0 * 2.0**22
C64 = 3.0 * 2.0**51
SQD = float(np.sqrt(np.float64(D)))
SQI = float(np.sqrt(np.float64(INNER)))
N_W1 = float(INNER * D)
N_W2 = float(D * INNER)

FUSED = True


class CEng:

    def __init__(self, step, key):
        self.step, self.key = step, key
        self.is_q = step == 16
        self.cnt = [0, 0]
        self.cur = 0
        self.e = None

    @property
    def n(self):
        return self.cnt[0] + self.cnt[1]

    def op(self, name, *a, **k):
        self.cnt[self.cur] += self.step
        return self.n

    def dma(self, out, in_, **kw):
        self.cnt[self.cur] += self.step
        return self.n

    def wait(self, M, key):
        pass

    def wait_n(self, sem_eng, n):
        pass

    def wait_self(self):
        pass

    def mark(self, M, key):
        if self.is_q:
            M[key] = ("Q", self.key, self.cur, self.cnt[self.cur],
                      self.cnt[self.cur ^ 1])
            self.cur ^= 1
        else:
            M[key] = self.cnt[0]


class REng:

    def __init__(self, step, sems, key):
        self.step, self.key = step, key
        self.is_q = step == 16
        self.sems = sems
        self.cnt = [0, 0]
        self.cur = 0
        self._need_drain = False
        self.e = None

    @property
    def n(self):
        return self.cnt[0] + self.cnt[1]

    @property
    def sem(self):
        return self.sems[0]

    def op(self, name, *a, **k):
        inst = getattr(self.e, name)(*a, **k)
        inst.then_inc(self.sems[self.cur] if self.is_q else self.sems[0],
                      self.step)
        self.cnt[self.cur] += self.step
        return self.n

    def dma(self, out, in_, **kw):
        if self.is_q and self._need_drain:
            self.e.wait_ge(self.sems[self.cur], self.cnt[self.cur])
            self._need_drain = False
        inst = self.e.dma_start(out=out, in_=in_, **kw)
        inst.then_inc(self.sems[self.cur] if self.is_q else self.sems[0],
                      self.step)
        self.cnt[self.cur] += self.step
        return self.n

    def wait(self, M, key):
        val = M[key]
        if isinstance(val, tuple) and val[0] == "Q":
            _, qkey, which, count, other = val
            self.e.wait_ge(M["_sems"][qkey][which], count)
            if other:
                self.e.wait_ge(M["_sems"][qkey][which ^ 1], other)
        else:
            self.e.wait_ge(M["_sems"][key[0]][0], val)

    def wait_n(self, sem_eng, n):
        self.e.wait_ge(sem_eng.sems[0], n)

    def wait_self(self):
        self.e.wait_ge(self.sems[0], self.cnt[0])

    def wait_total(self):
        for i in (0, 1):
            self.e.wait_ge(self.sems[i], self.cnt[i])

    def mark(self, M, key):
        if self.is_q:
            exp = M[key]
            got = ("Q", self.key, self.cur, self.cnt[self.cur],
                   self.cnt[self.cur ^ 1])
            assert exp == got, f"mark {key}: planned {exp} emitted {got}"
            self.cur ^= 1
            self._need_drain = True
        else:
            assert M[key] == self.cnt[0], (
                f"mark {key}: planned {M[key]} emitted {self.cnt[0]}")


def emit_round(v, out_bf, y, tA, tB, tS):
    v.op("tensor_scalar", out=tS, in0=y, scalar1=1e-30, scalar2=1e-30,
         op0=ALU.mult, op1=ALU.mult)
    v.wait_self()
    v.op("tensor_scalar", out=tS, in0=tS, scalar1=0.0, scalar2=None,
         op0=ALU.not_equal)
    v.op("tensor_scalar", out=tA, in0=y, scalar1=C32, scalar2=C32,
         op0=ALU.add, op1=ALU.subtract)
    v.wait_self()
    v.op("tensor_scalar", out=tB, in0=tA, scalar1=C64, scalar2=C64,
         op0=ALU.add, op1=ALU.subtract)
    v.wait_self()
    v.op("tensor_tensor", out=tB, in0=tB, in1=tA, op=ALU.subtract)
    v.wait_self()
    v.op("tensor_tensor", out=tB, in0=tB, in1=tS, op=ALU.mult)
    v.wait_self()
    v.op("tensor_tensor", out=out_bf, in0=tB, in1=tA, op=ALU.add)


def build(part):
    wantA = part in ("A", "ALL")
    wantB = part in ("B", "ALL")
    nc = bass.Bass("TRN2", target_bir_lowering=False, debug=False)

    x = nc.dram_tensor("x", [T, D], F32, kind="ExternalInput").ap()
    identf = nc.dram_tensor("identf", [128, 128], F32, kind="ExternalInput").ap()
    if wantA:
        wqT = nc.dram_tensor("wqT", [D, DPC], F32, kind="ExternalInput").ap()
        wkT = nc.dram_tensor("wkT", [D, DPC], F32, kind="ExternalInput").ap()
        wvT = nc.dram_tensor("wvT", [D, DPC], F32, kind="ExternalInput").ap()
        maskd = nc.dram_tensor("maskd", [128, 128], F32, kind="ExternalInput").ap()
        mask01 = nc.dram_tensor("mask01", [128, 128], F32, kind="ExternalInput").ap()
        qT_d = nc.dram_tensor("qT_d", [16, 128, 1024], F32, kind="Internal").ap()
        kT_d = nc.dram_tensor("kT_d", [16, 128, 1024], F32, kind="Internal").ap()
        v_d = nc.dram_tensor("v_d", [T, DPC], F32, kind="Internal").ap()
        zpk = "Internal" if part == "ALL" else "ExternalOutput"
        zparts = nc.dram_tensor("zparts", [8, DPC, TQ], F32, kind=zpk).ap()
    if wantB:
        xres = nc.dram_tensor("xres", [TQ, D], F32, kind="ExternalInput").ap()
        bmask = nc.dram_tensor("bmask", [128, 2], F32, kind="ExternalInput").ap()
        woT = nc.dram_tensor("woT", [8, 128, 16 * 256], F32, kind="ExternalInput").ap()
        w1T = nc.dram_tensor("w1T", [D, INNER], F32, kind="ExternalInput").ap()
        w2T = nc.dram_tensor("w2T", [INNER, D], F32, kind="ExternalInput").ap()
        w1q_d = nc.dram_tensor("w1q_d", [8, 128, 16 * 1024], BF16, kind="Internal").ap()
        w2q_d = nc.dram_tensor("w2q_d", [4, 128, 64 * 512], BF16, kind="Internal").ap()
        h_d = nc.dram_tensor("h_d", [8, 128, INNER], F32, kind="Internal").ap()
        x1_d = nc.dram_tensor("x1_d", [8, 128, D], F32, kind="Internal").ap()
        y_out = nc.dram_tensor("y", [TQ, D], F32, kind="ExternalOutput").ap()
    if part == "ALL":
        za_out = nc.dram_tensor("za_out", [8, DPC, TQ], F32, kind="Internal").ap()
    if part == "B":
        zq_in = nc.dram_tensor("zq_in", [8, DPC, TQ], F32, kind="ExternalInput").ap()
        za_out = zq_in

    ctx = ExitStack()
    with ctx:
        sidf = ctx.enter_context(nc.sbuf_tensor("sidf", [128, 128], F32))
        sidb = ctx.enter_context(nc.sbuf_tensor("sidb", [128, 128], BF16))
        smk = ctx.enter_context(nc.sbuf_tensor("smk", [128, 128], F32))
        tA = ctx.enter_context(nc.sbuf_tensor("tA", [128, 1024], F32))
        tB = ctx.enter_context(nc.sbuf_tensor("tB", [128, 1024], F32))
        tS = ctx.enter_context(nc.sbuf_tensor("tS", [128, 1024], F32))
        ybuf = ctx.enter_context(nc.sbuf_tensor("ybuf", [128, 1024], F32))
        tq1 = ctx.enter_context(nc.sbuf_tensor("tq1", [128, 1024], BF16))
        tq2 = ctx.enter_context(nc.sbuf_tensor("tq2", [128, 1024], BF16))
        st_ = ctx.enter_context(nc.sbuf_tensor("st_", [128, 128], F32))
        smb = ctx.enter_context(nc.sbuf_tensor("smb", [128, 2], F32))
        sones = ctx.enter_context(nc.sbuf_tensor("sones", [128, 1], F32))
        sones_r = ctx.enter_context(nc.sbuf_tensor("sones_r", [1, 128], F32))
        ps = [ctx.enter_context(nc.psum_tensor(f"ps{i}", [128, 512], F32))
              for i in range(6)]
        pbf = [ctx.enter_context(nc.psum_tensor(f"pbf{i}", [128, 512], BF16))
               for i in range(2)]

        SC = {k: i for i, k in enumerate(
            ["rsum", "ssq", "m", "ex2", "m2", "var", "sd", "rs", "nb",
             "nmax", "nmaxs", "psum_r", "prec",
             "sw1m", "thr1", "nthr1", "sw2m", "thr2", "nthr2",
             "n1", "rc", "rr", "mab", "mx", "mxc", "si"])}

        def sc(k):
            return st_[:, SC[k]:SC[k] + 1]

        MU1, CS1, SS2, MX2, MU2, CS2 = 26, 34, 42, 50, 58, 66

        sA1 = ExitStack()
        wq_sb = sA1.enter_context(nc.sbuf_tensor("wq_sb", [128, 16 * 512], F32))
        wk_sb = sA1.enter_context(nc.sbuf_tensor("wk_sb", [128, 16 * 512], F32))
        wv_sb = sA1.enter_context(nc.sbuf_tensor("wv_sb", [128, 16 * 512], F32))
        xc = sA1.enter_context(nc.sbuf_tensor("xc", [128, 2 * D], F32))
        xnc = sA1.enter_context(nc.sbuf_tensor("xnc", [128, 2 * D], F32))
        xnT = sA1.enter_context(nc.sbuf_tensor("xnT", [128, 16 * 256], F32))
        qev = sA1.enter_context(nc.sbuf_tensor("qev", [128, 1024], F32))
        kev = sA1.enter_context(nc.sbuf_tensor("kev", [128, 1024], F32))
        vev = sA1.enter_context(nc.sbuf_tensor("vev", [128, 1024], F32))
        sA1.close()

        sA2 = ExitStack()
        ksb = sA2.enter_context(nc.sbuf_tensor("ksb", [128, HPC * T], F32))
        vsb = sA2.enter_context(nc.sbuf_tensor("vsb", [128, 32 * 512], F32))
        qblk = sA2.enter_context(nc.sbuf_tensor("qblk", [128, 2 * 4 * 512], F32))
        ptring = sA2.enter_context(nc.sbuf_tensor("ptring", [128, 4 * 512], F32))
        sexpb = sA2.enter_context(nc.sbuf_tensor("sexpb", [128, 512], F32))
        rsums = sA2.enter_context(nc.sbuf_tensor("rsums", [128, 128], F32))
        rrow = sA2.enter_context(nc.sbuf_tensor("rrow", [1, 512], F32))
        rssb = sA2.enter_context(nc.sbuf_tensor("rssb", [128, 512], F32))
        zev = sA2.enter_context(nc.sbuf_tensor("zev", [128, 512], F32))
        sm01 = sA2.enter_context(nc.sbuf_tensor("sm01", [128, 128], F32))
        sA2.close()

        sB1 = ExitStack()
        xqT = sB1.enter_context(nc.sbuf_tensor("xqT", [128, 16 * TQ], BF16))
        sB1a = ExitStack()
        zqsb = sB1a.enter_context(nc.sbuf_tensor("zqsb", [128, 16 * TQ], F32))
        ztmp = sB1a.enter_context(nc.sbuf_tensor("ztmp", [128, 2 * TQ], F32))
        wosb = sB1a.enter_context(nc.sbuf_tensor("wosb", [128, 16 * 256], F32))
        xrb = sB1a.enter_context(nc.sbuf_tensor("xrb", [128, 2 * 256], F32))
        x1sb = sB1a.enter_context(nc.sbuf_tensor("x1sb", [128, 2 * 256], F32))
        x1row = sB1a.enter_context(nc.sbuf_tensor("x1row", [128, 2 * D], F32))
        xn2 = sB1a.enter_context(nc.sbuf_tensor("xn2", [128, D], F32))
        xqb = sB1a.enter_context(nc.sbuf_tensor("xqb", [128, D], BF16))
        sB1a.close()
        w1sb = sB1.enter_context(nc.sbuf_tensor("w1sb", [128, 16 * 1024], BF16))
        hb = sB1.enter_context(nc.sbuf_tensor("hb", [128, 4 * 512], F32))
        sB1.close()

        sB2 = ExitStack()
        xq2T = sB2.enter_context(nc.sbuf_tensor("xq2T", [128, 64 * 512], BF16))
        w2sb = sB2.enter_context(nc.sbuf_tensor("w2sb", [128, 64 * 512], BF16))
        hrow = sB2.enter_context(nc.sbuf_tensor("hrow", [128, 2 * 1024], F32))
        xq2b = sB2.enter_context(nc.sbuf_tensor("xq2b", [128, 1024], BF16))
        xr2 = sB2.enter_context(nc.sbuf_tensor("xr2", [128, 2 * 512], F32))
        yo = sB2.enter_context(nc.sbuf_tensor("yo", [128, 2 * 512], F32))
        tmp5 = sB2.enter_context(nc.sbuf_tensor("tmp5", [128, 512], F32))
        sB2.close()

        p0A, p0B = [], []
        if wantB:
            for ci in range(16):
                for cg in range(8):
                    p0A.append(("w1", w1T[ci * 128:(ci + 1) * 128,
                                          cg * 1024:(cg + 1) * 1024]))
            for ci in range(64):
                for cg in range(2):
                    p0A.append(("w2", w2T[ci * 128:(ci + 1) * 128,
                                          cg * 1024:(cg + 1) * 1024]))
            for ci in range(16):
                for cg in range(8):
                    p0B.append(("w1", w1T[ci * 128:(ci + 1) * 128,
                                          cg * 1024:(cg + 1) * 1024],
                                ("w1", cg, ci)))
            for ci in range(64):
                for cg in range(2):
                    p0B.append(("w2", w2T[ci * 128:(ci + 1) * 128,
                                          cg * 1024:(cg + 1) * 1024],
                                ("w2", ci, cg)))

        def program(EN, M, mode):
            q, g, t, v, s = EN["q"], EN["g"], EN["t"], EN["v"], EN["s"]

            p0_state = {"gA": 0, "vA": 0, "gB": 0, "vB": 0}

            def p0_g_passA(qeng, nunits):
                for _ in range(nunits):
                    i = p0_state["gA"]
                    if i >= len(p0A):
                        return
                    p0_state["gA"] += 1
                    if i >= 4:
                        qeng.wait(M, ("v", "p0A", i - 4))
                    qeng.dma((tA, tB, tS, ybuf)[i % 4][:], p0A[i][1])
                    qeng.mark(M, ("g", "p0A", i))

            def p0_v_passA(nunits):
                for _ in range(nunits):
                    i = p0_state["vA"]
                    if i >= len(p0A):
                        return
                    p0_state["vA"] += 1
                    col = 74 if p0A[i][0] == "w1" else 75
                    v.wait(M, ("g", "p0A", i))
                    v.op("reduce_sum", out=sc("rsum"),
                         in_=(tA, tB, tS, ybuf)[i % 4][:],
                         axis=AX.X, apply_absolute_value=True)
                    v.wait_self()
                    v.op("tensor_tensor", out=st_[:, col:col + 1],
                         in0=st_[:, col:col + 1], in1=sc("rsum"), op=ALU.add)
                    v.wait_self()
                    v.mark(M, ("v", "p0A", i))

            def p0_g_passB(nunits):
                p_ = EN["p"]
                for _ in range(nunits):
                    i = p0_state["gB"]
                    if i >= len(p0B):
                        return
                    p0_state["gB"] += 1
                    if i >= 2:
                        p_.wait(M, ("v", "p0B", i - 2))
                    else:
                        p_.wait(M, ("v", "p0A", len(p0A) - 1))
                    p_.dma((tA, tB)[i % 2][:], p0B[i][1])
                    p_.mark(M, ("g", "p0Bld", i))
                    j = i - 1
                    if j >= 0:
                        p0_g_store(j)

            def p0_g_store(j):
                p_ = EN["p"]
                which, _, dst = p0B[j]
                p_.wait(M, ("v", "p0B", j))
                tb = (tq1, tq2)[j % 2]
                if which == "w1":
                    _, cg, ci = dst
                    p_.dma(w1q_d[cg][:, ci * 1024:(ci + 1) * 1024], tb[:])
                else:
                    _, ci, cg = dst
                    for dd in range(2):
                        p_.dma(w2q_d[cg * 2 + dd][:, ci * 512:(ci + 1) * 512],
                               tb[:, dd * 512:(dd + 1) * 512])
                p_.mark(M, ("g", "p0Bst", j))

            def p0_v_passB(nunits):
                for _ in range(nunits):
                    i = p0_state["vB"]
                    if i >= len(p0B):
                        return
                    p0_state["vB"] += 1
                    which = p0B[i][0]
                    thr = sc("thr1") if which == "w1" else sc("thr2")
                    nthr = sc("nthr1") if which == "w1" else sc("nthr2")
                    if i == 0:
                        v.wait(M, ("v", "p0A", len(p0A) - 1))
                    v.wait(M, ("g", "p0Bld", i))
                    if i >= 2:
                        v.wait(M, ("g", "p0Bst", i - 2))
                    buf = (tA, tB)[i % 2]
                    v.op("tensor_scalar", out=ybuf[:], in0=buf[:], scalar1=thr,
                         scalar2=None, op0=ALU.is_gt)
                    v.op("tensor_scalar", out=tS[:], in0=buf[:], scalar1=nthr,
                         scalar2=None, op0=ALU.is_lt)
                    v.wait_self()
                    v.op("tensor_tensor", out=(tq1, tq2)[i % 2][:], in0=tS[:],
                         in1=ybuf[:], op=ALU.subtract)
                    v.wait_self()
                    v.mark(M, ("v", "p0B", i))

            def sec(engkey, fn):
                if mode == "plan":
                    fn(EN[engkey])
                else:
                    getattr(block, {"q": "sync", "g": "gpsimd", "t": "tensor",
                                    "v": "vector", "s": "scalar"}[engkey])(
                        lambda e: fn(EN[engkey].__setattr__("e", e) or EN[engkey]))

            def init_q(q):
                q.dma(sidf[:], identf)
                if wantA:
                    q.dma(smk[:], maskd)
                if wantB:
                    q.dma(smb[:], bmask)
                q.mark(M, ("q", "init"))
            sec("q", init_q)

            def init_v(v):
                v.wait(M, ("q", "init"))
                v.op("tensor_copy", out=sidb[:], in_=sidf[:])
                v.op("memset", sones[:], 1.0)
                v.op("memset", sones_r[:], 1.0)
                if wantB:
                    v.op("memset", st_[:, 74:76], 0.0)
                v.wait_self()
                v.mark(M, ("v", "init"))
            sec("v", init_v)

            NCH = 16
            if wantA:
                def p1_q(q):
                    for dc in range(16):
                        q.dma(wq_sb[:, dc * 512:(dc + 1) * 512],
                              wqT[dc * 128:(dc + 1) * 128, :])
                        q.dma(wk_sb[:, dc * 512:(dc + 1) * 512],
                              wkT[dc * 128:(dc + 1) * 128, :])
                        q.dma(wv_sb[:, dc * 512:(dc + 1) * 512],
                              wvT[dc * 128:(dc + 1) * 128, :])
                    q.mark(M, ("q", "ldw"))
                    for ci in range(NCH):
                        if ci >= 1:
                            q.wait(M, ("s", "p1apply", ci - 1))
                        for tt in range(2):
                            q.dma(xc[:, tt * D:(tt + 1) * D],
                                  x[ci * 256 + tt * 128: ci * 256 + (tt + 1) * 128])
                            q.mark(M, ("q", "p1x", ci, tt))
                sec("q", p1_q)

                def p1_s(s):
                    for ci in range(NCH):
                        for tt in range(2):
                            xt = xc[:, tt * D:(tt + 1) * D]
                            s.wait(M, ("v", "p1var", ci, tt))
                            s.op("activation", out=sc("sd"), in_=sc("var"),
                                 func=AF.Sqrt, bias=0.0, scale=1.0)
                            s.mark(M, ("s", "p1sd", ci, tt))
                            s.wait(M, ("v", "p1nb", ci, tt))
                            if ci >= 1 and tt == 0:
                                s.wait(M, ("t", "p1tr", ci - 1, 7))
                            s.op("activation", out=xnc[:, tt * D:(tt + 1) * D],
                                 in_=xt, func=AF.Identity, scale=sc("rs"),
                                 bias=sc("nb"))
                            s.mark(M, ("s", "p1applyt", ci, tt))
                        s.mark(M, ("s", "p1apply", ci))
                        s.wait(M, ("t", "p1mm", ci))
                        if ci >= 1:
                            s.wait(M, ("g", "p1sp", ci - 1))
                        for h in range(4):
                            bank, half = ps[h // 2], (h % 2) * 256
                            s.op("activation",
                                 out=qev[:, h * 256:(h + 1) * 256],
                                 in_=bank[:, half:half + 256], func=AF.Copy,
                                 bias=0.0, scale=ATT_SCALE)
                        for h in range(4):
                            bank, half = ps[2 + h // 2], (h % 2) * 256
                            s.op("activation", out=kev[:, h * 256:(h + 1) * 256],
                                 in_=bank[:, half:half + 256], func=AF.Copy,
                                 bias=0.0, scale=1.0)
                        for tt in range(2):
                            s.op("activation", out=vev[:, tt * 512:(tt + 1) * 512],
                                 in_=ps[4 + tt][:], func=AF.Copy,
                                 bias=0.0, scale=1.0)
                        s.wait_self()
                        s.mark(M, ("s", "p1evs", ci))
                sec("s", p1_s)

                def p1_v(v):
                    v.wait(M, ("v", "init"))
                    for ci in range(NCH):
                        for tt in range(2):
                            v.wait(M, ("q", "p1x", ci, tt))
                            xt = xc[:, tt * D:(tt + 1) * D]
                            for bq in range(4):
                                v.op("bn_stats", out=st_[:, 96 + 6 * bq:
                                                         96 + 6 * bq + 6],
                                     in_=xt[:, bq * 512:(bq + 1) * 512])
                            v.wait_self()
                            v.op("bn_aggr", out=st_[:, 86:88],
                                 in_=st_[:, 96:120])
                            v.wait_self()
                            v.op("tensor_scalar", out=sc("var"),
                                 in0=st_[:, 87:88], scalar1=EPS_LN,
                                 scalar2=None, op0=ALU.add)
                            v.op("tensor_copy", out=sc("m"), in_=st_[:, 86:87])
                            v.wait_self()
                            v.mark(M, ("v", "p1var", ci, tt))
                            v.wait(M, ("s", "p1sd", ci, tt))
                            v.op("reciprocal", out=sc("rs"), in_=sc("sd"))
                            v.wait_self()
                            v.op("scalar_tensor_tensor", out=sc("nb"), in0=sc("m"),
                                 scalar=-1.0, in1=sc("rs"), op0=ALU.mult,
                                 op1=ALU.mult)
                            v.wait_self()
                            v.mark(M, ("v", "p1nb", ci, tt))
                        if ci >= 1:
                            v.wait(M, ("t", "p1mm", ci - 1))
                        for grp in range(8):
                            v.wait(M, ("t", "p1tr", ci, grp))
                            tt0 = grp // 4
                            for k in range(4):
                                dc = (grp % 4) * 4 + k
                                v.op("tensor_copy",
                                     out=xnT[:, dc * 256 + tt0 * 128:
                                             dc * 256 + tt0 * 128 + 128],
                                     in_=ps[4][:, k * 128:(k + 1) * 128])
                            v.mark(M, ("v", "p1trev", ci, grp))
                        if wantB:
                            p0_v_passA(16)
                sec("v", p1_v)

                def p1_t(t):
                    t.wait(M, ("q", "ldw"))
                    for ci in range(NCH):
                        for grp in range(8):
                            tt0 = grp // 4
                            t.wait(M, ("s", "p1applyt", ci, tt0))
                            if (ci, grp) != (0, 0):
                                pci, pgrp = (ci, grp - 1) if grp else (ci - 1, 7)
                                t.wait(M, ("v", "p1trev", pci, pgrp))
                            if grp == 0 and ci >= 1:
                                t.wait(M, ("s", "p1evs", ci - 1))
                            for k in range(4):
                                dc = (grp % 4) * 4 + k
                                t.op("transpose",
                                     out=ps[4][:, k * 128:(k + 1) * 128],
                                     in_=xnc[:, tt0 * D + dc * 128:
                                             tt0 * D + (dc + 1) * 128],
                                     identity=sidf[:])
                            t.mark(M, ("t", "p1tr", ci, grp))
                        t.wait(M, ("v", "p1trev", ci, 7))
                        if ci >= 1:
                            t.wait(M, ("s", "p1evs", ci - 1))
                        for wsb, b0 in ((wq_sb, 0), (wk_sb, 2)):
                            for h in range(4):
                                bank, half = ps[b0 + h // 2], (h % 2) * 256
                                for dc in range(16):
                                    t.op("matmul", out=bank[:, half:half + 256],
                                         lhsT=wsb[:, dc * 512 + h * 128:
                                                  dc * 512 + (h + 1) * 128],
                                         rhs=xnT[:, dc * 256:(dc + 1) * 256],
                                         start=(dc == 0), stop=(dc == 15))
                        for tt in range(2):
                            for dc in range(16):
                                t.op("matmul", out=ps[4 + tt][:],
                                     lhsT=xnT[:, dc * 256 + tt * 128:
                                              dc * 256 + tt * 128 + 128],
                                     rhs=wv_sb[:, dc * 512:(dc + 1) * 512],
                                     start=(dc == 0), stop=(dc == 15))
                        t.mark(M, ("t", "p1mm", ci))
                sec("t", p1_t)

                def p1_g(g):
                    for ci in range(NCH):
                        if wantB:
                            p0_g_passA(g, 16)
                        g.wait(M, ("s", "p1evs", ci))
                        g.dma(qT_d[ci], qev[:])
                        g.dma(kT_d[ci], kev[:])
                        for tt in range(2):
                            g.dma(v_d[ci * 256 + tt * 128:ci * 256 + (tt + 1) * 128, :],
                                  vev[:, tt * 512:(tt + 1) * 512])
                        g.mark(M, ("g", "p1sp", ci))
                sec("g", p1_g)

                def units2():
                    for qb in range(8):
                        for h in range(4):
                            for i in range(4):
                                yield qb, h, i

                def p2_q(q):
                    q.wait(M, ("g", "p1sp", NCH - 1))
                    q.dma(sm01[:], mask01)
                    for ci in range(16):
                        for h in range(4):
                            q.dma(ksb[:, h * T + ci * 256:h * T + (ci + 1) * 256],
                                  kT_d[ci][:, h * 256:(h + 1) * 256])
                    for j in range(32):
                        q.dma(vsb[:, j * 512:(j + 1) * 512],
                              v_d[j * 128:(j + 1) * 128, :])
                    q.mark(M, ("q", "p2kv"))
                    for qb in range(8):
                        if qb >= 2:
                            q.wait(M, ("t", "p2send", qb - 2))
                        qo = (qb % 2) * 2048
                        for h in range(4):
                            for hf in range(2):
                                q.dma(qblk[:, qo + h * 512 + hf * 256:
                                           qo + h * 512 + hf * 256 + 256],
                                      qT_d[2 * qb + hf][:, h * 256:(h + 1) * 256])
                        q.mark(M, ("q", "p2q", qb))
                sec("q", p2_q)

                if wantB:
                    def p2r_t(t):
                        t.wait(M, ("v", "p0A", len(p0A) - 1))
                        t.wait(M, ("s", "p1evs", NCH - 1))
                        t.op("matmul", out=ps[2][0:1, 0:2], lhsT=sones[:],
                             rhs=st_[:, 74:76], start=True, stop=True)
                        t.mark(M, ("t", "redA1"))
                        t.wait(M, ("v", "redAsc"))
                        t.op("matmul", out=ps[2][:, 4:6], lhsT=sones_r[:],
                             rhs=st_[0:1, 76:78], start=True, stop=True)
                        t.mark(M, ("t", "redA2"))
                    sec("t", p2r_t)

                    def p2r_v(v):
                        v.wait(M, ("t", "redA1"))
                        v.op("tensor_scalar", out=st_[0:1, 76:77],
                             in0=ps[2][0:1, 0:1], scalar1=1.0 / N_W1,
                             scalar2=EPS_W, op0=ALU.mult, op1=ALU.max)
                        v.op("tensor_scalar", out=st_[0:1, 77:78],
                             in0=ps[2][0:1, 1:2], scalar1=1.0 / N_W2,
                             scalar2=EPS_W, op0=ALU.mult, op1=ALU.max)
                        v.wait_self()
                        v.mark(M, ("v", "redAsc"))
                        v.wait(M, ("t", "redA2"))
                        v.op("tensor_copy", out=sc("sw1m"), in_=ps[2][:, 4:5])
                        v.op("tensor_copy", out=sc("sw2m"), in_=ps[2][:, 5:6])
                        v.wait_self()
                        v.op("tensor_scalar", out=sc("thr1"), in0=sc("sw1m"),
                             scalar1=0.5, scalar2=None, op0=ALU.mult)
                        v.op("tensor_scalar", out=sc("thr2"), in0=sc("sw2m"),
                             scalar1=0.5, scalar2=None, op0=ALU.mult)
                        v.wait_self()
                        v.op("tensor_scalar", out=sc("nthr1"), in0=sc("thr1"),
                             scalar1=-1.0, scalar2=None, op0=ALU.mult)
                        v.op("tensor_scalar", out=sc("nthr2"), in0=sc("thr2"),
                             scalar1=-1.0, scalar2=None, op0=ALU.mult)
                        v.wait_self()
                        v.mark(M, ("v", "redA"))
                    sec("v", p2r_v)

                rlist = M.setdefault("_rlist", [])

                def _prev_rchunk(qb, h, i, j):
                    idx = rindex.get((qb, h, i, j), 0)
                    return rlist[idx - 1] if idx >= 1 else None

                rindex = M.setdefault("_rindex", {})

                def p2_t(t):
                    t.wait(M, ("q", "p2kv"))
                    rch = 0
                    for qb in range(8):
                        t.wait(M, ("q", "p2q", qb))
                        NT = 4 * qb + 4
                        for h in range(4):
                            u = qb * 4 + h
                            zbank = ps[4 + (u % 2)]
                            if u >= 1:
                                t.wait(M, ("v", "r2T", (u - 1) // 4, (u - 1) % 4))
                                t.wait(M, ("v", "r2B", (u - 1) // 4, (u - 1) % 4))
                            for i in range(4):
                                qt = qb * 4 + i
                                L = qt + 1
                                nch = (L + 3) // 4
                                for j in range(nch):
                                    wdt = min(512, L * 128 - j * 512)
                                    if rch >= 2:
                                        t.wait(M, ("s", "r2e") + rlist[rch - 2])
                                    t.op("matmul", out=ps[rch % 2][:, 0:wdt],
                                         lhsT=qblk[:, (qb % 2) * 2048 + h * 512
                                                   + i * 128:
                                                   (qb % 2) * 2048 + h * 512
                                                   + (i + 1) * 128],
                                         rhs=ksb[:, h * T + j * 512:
                                                 h * T + j * 512 + wdt],
                                         start=True, stop=True)
                                    t.mark(M, ("t", "r2s", qb, h, i, j))
                                    if mode == "plan":
                                        rlist.append((qb, h, i, j))
                                        rindex[(qb, h, i, j)] = len(rlist) - 1
                                    rch += 1
                            if u >= 2:
                                t.wait(M, ("v", "p2zev", (u - 2) // 4, (u - 2) % 4))

                            def _pv(j):
                                if j >= 4 * qb:
                                    t.wait(M, ("v", "x2m", qb, h, j))
                                else:
                                    t.wait(M, ("s", "x2e", qb, h, j))
                                t.op("matmul", out=zbank[:, 0:512],
                                     lhsT=vsb[:, j * 512 + h * 128:
                                             j * 512 + (h + 1) * 128],
                                     rhs=ptring[:, (j % 4) * 512:
                                               (j % 4) * 512 + 512],
                                     start=(j == 0), stop=(j == NT - 1))
                                t.mark(M, ("t", "x2pv", qb, h, j))

                            for j in range(NT):
                                if j >= 2:
                                    t.wait(M, ("s", "x2e", qb, h, j - 2))
                                t.op("matmul", out=ps[2 + j % 2][:, 0:512],
                                     lhsT=ksb[:, h * T + j * 128:
                                             h * T + (j + 1) * 128],
                                     rhs=qblk[:, (qb % 2) * 2048 + h * 512:
                                             (qb % 2) * 2048 + (h + 1) * 512],
                                     start=True, stop=True)
                                t.mark(M, ("t", "x2s", qb, h, j))
                                if j >= 1:
                                    _pv(j - 1)
                            _pv(NT - 1)
                            t.wait(M, ("v", "r2r", qb, h))
                            t.wait(M, ("s", "r2e") + rlist[rch - 1])
                            for i in range(4):
                                t.op("matmul",
                                     out=ps[0][0:1, i * 128:(i + 1) * 128],
                                     lhsT=rsums[:, 8 + i:9 + i], rhs=sidf[:],
                                     start=True, stop=True)
                            t.mark(M, ("t", "r2t", qb, h))
                            t.wait(M, ("v", "r2T", qb, h))
                            t.op("matmul", out=ps[1][:, 0:512],
                                 lhsT=sones_r[:], rhs=rrow[0:1, 0:512],
                                 start=True, stop=True)
                            t.mark(M, ("t", "r2b", qb, h))
                        t.mark(M, ("t", "p2send", qb))
                sec("t", p2_t)

                def p2_v(v):
                    v.wait(M, ("g", "p1sp", NCH - 1))
                    v.op("memset", rsums[:], 0.0)
                    v.wait_self()
                    for qb in range(8):
                        NT = 4 * qb + 4
                        for h in range(4):
                            u = qb * 4 + h
                            for i in range(4):
                                qt = qb * 4 + i
                                L = qt + 1
                                nch = (L + 3) // 4
                                wdt = L * 128 - (nch - 1) * 512
                                gp = rindex[(qb, h, i, nch - 1)] % 2
                                for j in range(nch):
                                    if j == nch - 1:
                                        v.wait(M, ("t", "r2s", qb, h, i, j))
                                        v.op("tensor_tensor",
                                             out=ps[gp][:, wdt - 128:wdt],
                                             in0=ps[gp][:, wdt - 128:wdt],
                                             in1=smk[:], op=ALU.add)
                                        v.wait_self()
                                        v.mark(M, ("v", "r2m", qb, h, i))
                                    v.wait(M, ("s", "r2e", qb, h, i, j))
                                    if j == 0:
                                        v.op("tensor_copy",
                                             out=rsums[:, i:i + 1],
                                             in_=sc("psum_r"))
                                    else:
                                        v.op("tensor_tensor",
                                             out=rsums[:, i:i + 1],
                                             in0=rsums[:, i:i + 1],
                                             in1=sc("psum_r"), op=ALU.add)
                                    v.wait_self()
                                    v.mark(M, ("v", "r2a", qb, h, i, j))
                            for j in range(4 * qb, NT):
                                ji = j - 4 * qb
                                v.wait(M, ("s", "x2e", qb, h, j))
                                for ci in range(4):
                                    dst = ptring[:, (j % 4) * 512 + ci * 128:
                                                 (j % 4) * 512 + (ci + 1) * 128]
                                    if ci < ji:
                                        v.op("memset", dst, 0.0)
                                    elif ci == ji:
                                        v.op("tensor_tensor", out=dst, in0=dst,
                                             in1=sm01[:], op=ALU.mult)
                                v.wait_self()
                                v.mark(M, ("v", "x2m", qb, h, j))
                            if u >= 1:
                                v.wait(M, ("t", "r2t", (u - 1) // 4, (u - 1) % 4))
                            v.op("reciprocal", out=rsums[:, 8:12],
                                 in_=rsums[:, 0:4])
                            v.wait_self()
                            v.mark(M, ("v", "r2r", qb, h))
                            v.wait(M, ("t", "r2t", qb, h))
                            v.op("tensor_copy", out=rrow[0:1, 0:512],
                                 in_=ps[0][0:1, 0:512])
                            v.wait_self()
                            v.mark(M, ("v", "r2T", qb, h))
                            v.wait(M, ("t", "r2b", qb, h))
                            v.op("tensor_copy", out=rssb[:], in_=ps[1][:, 0:512])
                            v.wait_self()
                            v.mark(M, ("v", "r2B", qb, h))
                            v.wait(M, ("t", "x2pv", qb, h, NT - 1))
                            if u >= 1:
                                pu = u - 1
                                v.wait(M, ("g", "p2z", pu // 4, pu % 4))
                            v.op("tensor_tensor", out=zev[:],
                                 in0=ps[4 + (u % 2)][:, 0:512], in1=rssb[:],
                                 op=ALU.mult)
                            v.wait_self()
                            v.mark(M, ("v", "p2zev", qb, h))
                            if wantB:
                                p0_v_passB(8)
                sec("v", p2_v)

                def p2_s(s):
                    for qb in range(8):
                        NT = 4 * qb + 4
                        for h in range(4):
                            u = qb * 4 + h
                            for i in range(4):
                                qt = qb * 4 + i
                                L = qt + 1
                                nch = (L + 3) // 4
                                for j in range(nch):
                                    wdt = min(512, L * 128 - j * 512)
                                    t_key = ("t", "r2s", qb, h, i, j)
                                    s.wait(M, t_key)
                                    if j == nch - 1:
                                        s.wait(M, ("v", "r2m", qb, h, i))
                                    pk = _prev_rchunk(qb, h, i, j)
                                    if pk is not None:
                                        s.wait(M, ("v", "r2a", *pk))
                                    gp = rindex[(qb, h, i, j)] % 2
                                    s.op("activation", out=sexpb[:, 0:wdt],
                                         in_=ps[gp][:, 0:wdt], func=AF.Exp,
                                         bias=0.0, scale=1.0,
                                         accum_out=sc("psum_r"))
                                    s.mark(M, ("s", "r2e", qb, h, i, j))
                            for j in range(NT):
                                s.wait(M, ("t", "x2s", qb, h, j))
                                if j >= 4:
                                    s.wait(M, ("t", "x2pv", qb, h, j - 4))
                                elif u >= 1:
                                    pu = u - 1
                                    pNT = 4 * (pu // 4) + 4
                                    s.wait(M, ("t", "x2pv", pu // 4, pu % 4,
                                               pNT - 1))
                                s.op("activation",
                                     out=ptring[:, (j % 4) * 512:
                                               (j % 4) * 512 + 512],
                                     in_=ps[2 + j % 2][:, 0:512], func=AF.Exp,
                                     bias=0.0, scale=1.0)
                                s.mark(M, ("s", "x2e", qb, h, j))
                            if wantB:
                                EN["p"].e = s.e
                                p0_g_passB(8)
                sec("s", p2_s)

                def p2_g(g):
                    for qb in range(8):
                        d1, cofs = qb // 2, (qb % 2) * 512
                        for h in range(4):
                            g.wait(M, ("v", "p2zev", qb, h))
                            g.dma(zparts[d1][h * 128:(h + 1) * 128,
                                             cofs:cofs + 512], zev[:])
                            g.dma(zparts[4 + d1][h * 128:(h + 1) * 128,
                                                 cofs:cofs + 512], zev[:])
                            g.mark(M, ("g", "p2z", qb, h))
                sec("g", p2_g)

                if wantB:
                    def p0_drain_v(v):
                        p0_v_passB(len(p0B))
                    def p0_drain_p(s):
                        EN["p"].e = s.e
                        p0_g_passB(len(p0B))
                        p0_g_store(len(p0B) - 1)
                        EN["p"].mark(M, ("g", "p0done"))
                    sec("v", p0_drain_v)
                    sec("s", p0_drain_p)
            elif wantB:
                def p0_all_g(s):
                    EN["p"].e = s.e
                    EN["p"].wait(M, ("v", "init"))
                    p0_g_passA(EN["p"], len(p0A))
                sec("s", p0_all_g)

                def p0_all_v(v):
                    p0_v_passA(len(p0A))
                sec("v", p0_all_v)

                def pBr_t(t):
                    t.wait(M, ("v", "p0A", len(p0A) - 1))
                    t.op("matmul", out=ps[2][0:1, 0:2], lhsT=sones[:],
                         rhs=st_[:, 74:76], start=True, stop=True)
                    t.mark(M, ("t", "redA1"))
                    t.wait(M, ("v", "redAsc"))
                    t.op("matmul", out=ps[2][:, 4:6], lhsT=sones_r[:],
                         rhs=st_[0:1, 76:78], start=True, stop=True)
                    t.mark(M, ("t", "redA2"))
                sec("t", pBr_t)

                def pBr_v(v):
                    v.wait(M, ("t", "redA1"))
                    v.op("tensor_scalar", out=st_[0:1, 76:77],
                         in0=ps[2][0:1, 0:1], scalar1=1.0 / N_W1,
                         scalar2=EPS_W, op0=ALU.mult, op1=ALU.max)
                    v.op("tensor_scalar", out=st_[0:1, 77:78],
                         in0=ps[2][0:1, 1:2], scalar1=1.0 / N_W2,
                         scalar2=EPS_W, op0=ALU.mult, op1=ALU.max)
                    v.wait_self()
                    v.mark(M, ("v", "redAsc"))
                    v.wait(M, ("t", "redA2"))
                    v.op("tensor_copy", out=sc("sw1m"), in_=ps[2][:, 4:5])
                    v.op("tensor_copy", out=sc("sw2m"), in_=ps[2][:, 5:6])
                    v.wait_self()
                    v.op("tensor_scalar", out=sc("thr1"), in0=sc("sw1m"),
                         scalar1=0.5, scalar2=None, op0=ALU.mult)
                    v.op("tensor_scalar", out=sc("thr2"), in0=sc("sw2m"),
                         scalar1=0.5, scalar2=None, op0=ALU.mult)
                    v.wait_self()
                    v.op("tensor_scalar", out=sc("nthr1"), in0=sc("thr1"),
                         scalar1=-1.0, scalar2=None, op0=ALU.mult)
                    v.op("tensor_scalar", out=sc("nthr2"), in0=sc("thr2"),
                         scalar1=-1.0, scalar2=None, op0=ALU.mult)
                    v.wait_self()
                    v.mark(M, ("v", "redA"))
                    p0_v_passB(len(p0B))
                sec("v", pBr_v)

                def p0B_g(s):
                    EN["p"].e = s.e
                    p0_g_passB(len(p0B))
                    p0_g_store(len(p0B) - 1)
                    EN["p"].mark(M, ("g", "p0done"))
                sec("s", p0B_g)

            if part == "ALL":
                def ag_g(g):
                    g.wait(M, ("g", "p2z", 7, 3))
                    cc_ = EN["cc"]
                    if mode == "plan":
                        cc_.cnt[0] += 1
                    else:
                        cc_.e = g.e
                        inst = g.e.collective_compute(
                            "AllToAll", ALU.bypass,
                            replica_groups=[list(range(8))],
                            ins=[zparts.opt()], outs=[za_out.opt()])
                        inst.then_inc(cc_.sems[0], 1)
                        cc_.cnt[0] += 1
                    M[("cc", "a2a")] = EN["cc"].cnt[0]
                sec("g", ag_g)

            if wantB:
                def p3_q(q):
                    if wantA:
                        q.wait(M, ("t", "p2send", 7))
                        q.wait(M, ("v", "p2zev", 7, 3))
                        q.wait(M, ("s", "x2e", 7, 3, 31))
                    q.dma(wosb[:], woT[0])
                    q.mark(M, ("q", "p3wo", 0))
                    if part == "ALL":
                        q.wait_n(EN["cc"], M[("cc", "a2a")])
                    for rb in range(4):
                        for pt in range(4):
                            c = rb * 4 + pt
                            if c >= 1:
                                q.wait(M, ("v", "p3cmb", c - 1))
                            q.dma(ztmp[:, 0:TQ],
                                  za_out[rb][pt * 128:(pt + 1) * 128, :])
                            q.dma(ztmp[:, TQ:2 * TQ],
                                  za_out[4 + rb][pt * 128:(pt + 1) * 128, :])
                            q.mark(M, ("q", "p3z", c))
                    for d in range(8):
                        if d >= 1:
                            q.wait(M, ("t", "p3mm", d - 1, 7))
                            q.dma(wosb[:], woT[d])
                            q.mark(M, ("q", "p3wo", d))
                        for ti in range(8):
                            u = d * 8 + ti
                            if u >= 2:
                                q.wait(M, ("v", "p3ev", (u - 2) // 8, (u - 2) % 8))
                            q.dma(xrb[:, (ti % 2) * 256:(ti % 2) * 256 + 256],
                                  xres[ti * 128:(ti + 1) * 128,
                                       d * 256:(d + 1) * 256])
                            q.mark(M, ("q", "p3xr", d, ti))
                sec("q", p3_q)

                def p3comb_v(v):
                    for rb in range(4):
                        for pt in range(4):
                            c = rb * 4 + pt
                            v.wait(M, ("q", "p3z", c))
                            v.op("tensor_scalar", out=zqsb[:, c * TQ:(c + 1) * TQ],
                                 in0=ztmp[:, 0:TQ], scalar1=smb[:, 0:1],
                                 scalar2=None, op0=ALU.mult)
                            v.op("tensor_scalar", out=ztmp[:, TQ:2 * TQ],
                                 in0=ztmp[:, TQ:2 * TQ], scalar1=smb[:, 1:2],
                                 scalar2=None, op0=ALU.mult)
                            v.wait_self()
                            v.op("tensor_tensor",
                                 out=zqsb[:, c * TQ:(c + 1) * TQ],
                                 in0=zqsb[:, c * TQ:(c + 1) * TQ],
                                 in1=ztmp[:, TQ:2 * TQ], op=ALU.add)
                            v.wait_self()
                            v.mark(M, ("v", "p3cmb", c))
                sec("v", p3comb_v)

                def p3_t(t):
                    for d in range(8):
                        t.wait(M, ("q", "p3wo", d))
                        for ti in range(8):
                            u = d * 8 + ti
                            bank = ps[u % 6]
                            if u >= 6:
                                pu = u - 6
                                t.wait(M, ("v", "p3ev", pu // 8, pu % 8))
                            for c in range(16):
                                if d == 0 and ti == 0:
                                    t.wait(M, ("v", "p3cmb", c))
                                t.op("matmul", out=bank[:, 0:256],
                                     lhsT=zqsb[:, c * TQ + ti * 128:
                                              c * TQ + (ti + 1) * 128],
                                     rhs=wosb[:, c * 256:(c + 1) * 256],
                                     start=(c == 0), stop=(c == 15))
                            t.mark(M, ("t", "p3mm", d, ti))
                sec("t", p3_t)

                def p3_v(v):
                    for d in range(8):
                        for ti in range(8):
                            u = d * 8 + ti
                            bank = ps[u % 6]
                            v.wait(M, ("t", "p3mm", d, ti))
                            v.wait(M, ("q", "p3xr", d, ti))
                            if u >= 2:
                                v.wait(M, ("g", "p3st", (u - 2) // 8, (u - 2) % 8))
                            v.op("tensor_tensor",
                                 out=x1sb[:, (ti % 2) * 256:(ti % 2) * 256 + 256],
                                 in0=bank[:, 0:256],
                                 in1=xrb[:, (ti % 2) * 256:(ti % 2) * 256 + 256],
                                 op=ALU.add)
                            v.wait_self()
                            v.mark(M, ("v", "p3ev", d, ti))
                sec("v", p3_v)

                def p3_g(g):
                    for d in range(8):
                        for ti in range(8):
                            g.wait(M, ("v", "p3ev", d, ti))
                            g.dma(x1_d[ti][:, d * 256:(d + 1) * 256],
                                  x1sb[:, (ti % 2) * 256:(ti % 2) * 256 + 256])
                            g.mark(M, ("g", "p3st", d, ti))
                sec("g", p3_g)

                def p3b_q(q):
                    for ti in range(8):
                        q.wait(M, ("g", "p3st", 7, ti))
                        if ti >= 2:
                            q.wait(M, ("s", "p3bymul", ti - 2, 1))
                        q.dma(x1row[:, (ti % 2) * D:(ti % 2) * D + D], x1_d[ti])
                        q.mark(M, ("q", "p3bx", ti))
                sec("q", p3b_q)

                def p3b_s(s):
                    s.wait(M, ("v", "p0B", len(p0B) - 1))
                    s.wait(M, ("g", "p0Bst", len(p0B) - 1))
                    for ti in range(8):
                        xt = x1row[:, (ti % 2) * D:(ti % 2) * D + D]
                        s.wait(M, ("v", "p3bvar", ti))
                        s.op("activation", out=sc("sd"), in_=sc("var"),
                             func=AF.Sqrt, bias=0.0, scale=1.0)
                        s.mark(M, ("s", "p3bsd", ti))
                        s.wait(M, ("v", "p3bnb", ti))
                        if ti >= 1:
                            s.wait(M, ("v", "p3bround", ti - 1, 1))
                        s.op("activation", out=xn2[:], in_=xt, func=AF.Identity,
                             scale=sc("rs"), bias=sc("nb"))
                        s.mark(M, ("s", "p3bapply", ti))
                        s.wait(M, ("v", "p3bss2v", ti))
                        s.op("activation", out=sc("n1"), in_=sc("n1"),
                             func=AF.Sqrt, bias=0.0, scale=1.0)
                        s.mark(M, ("s", "p3bsqrt2", ti))
                        s.wait(M, ("v", "p3bmu", ti))
                        for hh in range(2):
                            s.op("activation", out=ybuf[:],
                                 in_=xn2[:, hh * 1024:(hh + 1) * 1024],
                                 func=AF.Copy,
                                 scale=st_[:, MU1 + ti:MU1 + ti + 1], bias=0.0)
                            s.mark(M, ("s", "p3bymul", ti, hh))
                            if hh == 0:
                                s.wait(M, ("v", "p3bround", ti, 0))
                sec("s", p3b_s)

                def p3b_v(v):
                    v.wait(M, ("g", "p0Bst", len(p0B) - 1))
                    for ti in range(8):
                        xt = x1row[:, (ti % 2) * D:(ti % 2) * D + D]
                        v.wait(M, ("q", "p3bx", ti))
                        for bq in range(4):
                            v.op("bn_stats", out=st_[:, 96 + 6 * bq:
                                                     96 + 6 * bq + 6],
                                 in_=xt[:, bq * 512:(bq + 1) * 512])
                        v.wait_self()
                        v.op("bn_aggr", out=st_[:, 86:88],
                             in_=st_[:, 96:120])
                        v.wait_self()
                        v.op("tensor_scalar", out=sc("var"),
                             in0=st_[:, 87:88], scalar1=EPS_LN,
                             scalar2=None, op0=ALU.add)
                        v.op("tensor_copy", out=sc("m"), in_=st_[:, 86:87])
                        v.wait_self()
                        v.mark(M, ("v", "p3bvar", ti))
                        v.wait(M, ("s", "p3bsd", ti))
                        v.op("reciprocal", out=sc("rs"), in_=sc("sd"))
                        v.wait_self()
                        v.op("scalar_tensor_tensor", out=sc("nb"), in0=sc("m"),
                             scalar=-1.0, in1=sc("rs"), op0=ALU.mult,
                             op1=ALU.mult)
                        v.wait_self()
                        v.mark(M, ("v", "p3bnb", ti))
                        v.wait(M, ("s", "p3bapply", ti))
                        v.op("reduce_max", out=sc("mab"), in_=xn2[:], axis=AX.X,
                             apply_absolute_value=True)
                        v.wait_self()
                        v.op("tensor_tensor", out=sc("n1"), in0=st_[:, 87:88],
                             in1=sc("rs"), op=ALU.mult)
                        v.wait_self()
                        v.op("tensor_tensor", out=sc("n1"), in0=sc("n1"),
                             in1=sc("rs"), op=ALU.mult)
                        v.wait_self()
                        v.op("tensor_scalar", out=sc("n1"), in0=sc("n1"),
                             scalar1=float(D), scalar2=None, op0=ALU.mult)
                        v.wait_self()
                        v.mark(M, ("v", "p3bss2v", ti))
                        v.wait(M, ("s", "p3bsqrt2", ti))
                        v.op("tensor_scalar", out=sc("n1"), in0=sc("n1"),
                             scalar1=EPS_RMS, scalar2=None, op0=ALU.max)
                        v.wait_self()
                        v.op("reciprocal", out=sc("rc"), in_=sc("n1"))
                        v.wait_self()
                        v.op("tensor_scalar", out=sc("rr"), in0=sc("rc"),
                             scalar1=SQD, scalar2=None, op0=ALU.mult)
                        v.wait_self()
                        v.op("tensor_tensor", out=sc("mx"), in0=sc("mab"),
                             in1=sc("rr"), op=ALU.mult)
                        v.wait_self()
                        v.op("tensor_scalar", out=sc("mxc"), in0=sc("mx"),
                             scalar1=EPS_Q, scalar2=None, op0=ALU.max)
                        v.wait_self()
                        v.op("reciprocal", out=sc("si"), in_=sc("mxc"))
                        v.wait_self()
                        v.op("scalar_tensor_tensor",
                             out=st_[:, MU1 + ti:MU1 + ti + 1], in0=sc("rr"),
                             scalar=127.0, in1=sc("si"), op0=ALU.mult,
                             op1=ALU.mult)
                        v.op("scalar_tensor_tensor",
                             out=st_[:, CS1 + ti:CS1 + ti + 1], in0=sc("mxc"),
                             scalar=-1.0 / 127.0, in1=sc("sw1m"), op0=ALU.mult,
                             op1=ALU.mult)
                        v.wait_self()
                        v.mark(M, ("v", "p3bmu", ti))
                        for hh in range(2):
                            v.wait(M, ("s", "p3bymul", ti, hh))
                            emit_round(v, xqb[:, hh * 1024:(hh + 1) * 1024],
                                       ybuf[:], tA[:], tB[:], tS[:])
                            v.wait_self()
                            v.mark(M, ("v", "p3bround", ti, hh))
                        for k in range(16):
                            v.wait(M, ("t", "p3btr", ti, k))
                            v.op("tensor_copy",
                                 out=xqT[:, k * TQ + ti * 128:
                                         k * TQ + (ti + 1) * 128],
                                 in_=pbf[k % 2][:, 0:128])
                            v.wait_self()
                            v.mark(M, ("v", "p3bxqev", ti, k))
                sec("v", p3b_v)

                def p3b_t(t):
                    for ti in range(8):
                        t.wait(M, ("v", "p3bround", ti, 1))
                        for k in range(16):
                            if k >= 2:
                                t.wait(M, ("v", "p3bxqev", ti, k - 2))
                            elif ti >= 1:
                                t.wait(M, ("v", "p3bxqev", ti - 1, 14 + k))
                            t.op("transpose", out=pbf[k % 2][:, 0:128],
                                 in_=xqb[:, k * 128:(k + 1) * 128],
                                 identity=sidb[:])
                            t.mark(M, ("t", "p3btr", ti, k))
                sec("t", p3b_t)

                def p4a_q(q):
                    q.wait(M, ("g", "p0done"))
                    q.wait(M, ("t", "p3mm", 7, 7))
                    q.wait(M, ("g", "p3st", 7, 7))
                    q.wait(M, ("s", "p3bymul", 7, 1))
                    for icg in range(8):
                        if icg >= 1:
                            q.wait(M, ("t", "p4amm", icg - 1, 1, 7))
                        q.dma(w1sb[:, 0:8 * 1024],
                              w1q_d[icg][:, 0:8 * 1024])
                        q.dma(w1sb[:, 8 * 1024:16 * 1024],
                              w1q_d[icg][:, 8 * 1024:16 * 1024])
                        q.mark(M, ("q", "p4aw", icg))
                sec("q", p4a_q)

                def p4a_t(t):
                    t.wait(M, ("v", "p3bxqev", 7, 15))
                    for icg in range(8):
                        t.wait(M, ("q", "p4aw", icg))
                        for sub in range(2):
                            for ti in range(8):
                                u = icg * 16 + sub * 8 + ti
                                bank = ps[u % 6]
                                if u >= 6:
                                    pu = u - 6
                                    t.wait(M, ("s", "p4agel1",
                                               pu // 16, (pu // 8) % 2, pu % 8))
                                for c in range(16):
                                    t.op("matmul", out=bank[:],
                                         lhsT=xqT[:, c * TQ + ti * 128:
                                                  c * TQ + (ti + 1) * 128],
                                         rhs=w1sb[:, c * 1024 + sub * 512:
                                                  c * 1024 + sub * 512 + 512],
                                         start=(c == 0), stop=(c == 15))
                                t.mark(M, ("t", "p4amm", icg, sub, ti))
                sec("t", p4a_t)

                def p4a_s(s):
                    for icg in range(8):
                        for sub in range(2):
                            for ti in range(8):
                                u = icg * 16 + sub * 8 + ti
                                bank = ps[u % 6]
                                slot = (u % 2) * 2
                                s.wait(M, ("t", "p4amm", icg, sub, ti))
                                if u >= 2:
                                    pu = u - 2
                                    s.wait(M, ("g", "p4ast",
                                               pu // 16, (pu // 8) % 2, pu % 8))
                                s.op("activation",
                                     out=hb[:, slot * 512:slot * 512 + 512],
                                     in_=bank[:], func=AF.Gelu,
                                     scale=st_[:, CS1 + ti:CS1 + ti + 1],
                                     bias=0.0)
                                s.mark(M, ("s", "p4agel1", icg, sub, ti))
                                s.wait_self()
                                s.op("activation",
                                     out=hb[:, (slot + 1) * 512:
                                             (slot + 1) * 512 + 512],
                                     in_=hb[:, slot * 512:slot * 512 + 512],
                                     func=AF.Gelu, bias=0.0, scale=1.0)
                                s.wait_self()
                                if u >= 1:
                                    pv = u - 1
                                    s.wait(M, ("v", "p4av", pv // 16,
                                               (pv // 8) % 2, pv % 8))
                                s.op("activation",
                                     out=hb[:, slot * 512:slot * 512 + 512],
                                     in_=hb[:, (slot + 1) * 512:
                                             (slot + 1) * 512 + 512],
                                     func=AF.Square, bias=0.0, scale=1.0,
                                     accum_out=sc("ssq"))
                                s.mark(M, ("s", "p4asq", icg, sub, ti))
                sec("s", p4a_s)

                def p4a_v(v):
                    v.op("memset", st_[:, SS2:SS2 + 8], 0.0)
                    v.op("memset", st_[:, MX2:MX2 + 8], 0.0)
                    v.wait_self()
                    for icg in range(8):
                        for sub in range(2):
                            for ti in range(8):
                                u = icg * 16 + sub * 8 + ti
                                slot = (u % 2) * 2
                                v.wait(M, ("s", "p4asq", icg, sub, ti))
                                v.op("tensor_tensor",
                                     out=st_[:, SS2 + ti:SS2 + ti + 1],
                                     in0=st_[:, SS2 + ti:SS2 + ti + 1],
                                     in1=sc("ssq"), op=ALU.add)
                                v.op("reduce_max", out=sc("mab"),
                                     in_=hb[:, (slot + 1) * 512:
                                             (slot + 1) * 512 + 512],
                                     axis=AX.X, apply_absolute_value=True)
                                v.wait_self()
                                v.op("tensor_tensor",
                                     out=st_[:, MX2 + ti:MX2 + ti + 1],
                                     in0=st_[:, MX2 + ti:MX2 + ti + 1],
                                     in1=sc("mab"), op=ALU.max)
                                v.wait_self()
                                v.mark(M, ("v", "p4av", icg, sub, ti))
                sec("v", p4a_v)

                def p4a_g(g):
                    for icg in range(8):
                        for sub in range(2):
                            ic = icg * 2 + sub
                            for ti in range(8):
                                u = icg * 16 + sub * 8 + ti
                                slot = (u % 2) * 2
                                g.wait(M, ("v", "p4av", icg, sub, ti))
                                g.dma(h_d[ti][:, ic * 512:(ic + 1) * 512],
                                      hb[:, (slot + 1) * 512:
                                         (slot + 1) * 512 + 512])
                                g.mark(M, ("g", "p4ast", icg, sub, ti))
                sec("g", p4a_g)

                def p4b_q(q):
                    q.wait(M, ("g", "p4ast", 7, 1, 7))
                    q.wait(M, ("t", "p4amm", 7, 1, 7))
                    q.wait(M, ("s", "p4asq", 7, 1, 7))
                    q.wait(M, ("v", "p4av", 7, 1, 7))
                    for th in range(2):
                        for tl in range(4):
                            ti = th * 4 + tl
                            for ch in range(8):
                                u = (th * 4 + tl) * 8 + ch
                                if u >= 2:
                                    pu = u - 2
                                    q.wait(M, ("s", "p4bymul",
                                               pu // 32, (pu // 8) % 4, pu % 8))
                                q.wait(M, ("g", "p4ast", 7, 1, ti))
                                q.dma(hrow[:, (ch % 2) * 1024:
                                           (ch % 2) * 1024 + 1024],
                                      h_d[ti][:, ch * 1024:(ch + 1) * 1024])
                                q.mark(M, ("q", "p4bh", th, tl, ch))
                        for d in range(4):
                            if (th, d) != (0, 0):
                                pth, pd = (th, d - 1) if d else (th - 1, 3)
                                q.wait(M, ("t", "p4bmm", pth, pd, 3))
                            for cq in range(4):
                                q.dma(w2sb[:, cq * 16 * 512:(cq + 1) * 16 * 512],
                                      w2q_d[d][:, cq * 16 * 512:
                                               (cq + 1) * 16 * 512])
                                q.mark(M, ("q", "p4bwq", th, d, cq))
                            q.mark(M, ("q", "p4bw", th, d))
                            for tl in range(4):
                                ti = th * 4 + tl
                                u = d * 4 + tl
                                if u >= 2 or th > 0:
                                    pu = u - 2 if u >= 2 else 14 + u
                                    pth = th if u >= 2 else th - 1
                                    q.wait(M, ("v", "p4badd",
                                               pth, pu // 4, pu % 4))
                                q.dma(xr2[:, (tl % 2) * 512:(tl % 2) * 512 + 512],
                                      x1_d[ti][:, d * 512:(d + 1) * 512])
                                q.mark(M, ("q", "p4bxr", th, d, tl))
                sec("q", p4b_q)

                def p4b_s(s):
                    for th in range(2):
                        for tl in range(4):
                            ti = th * 4 + tl
                            s.wait(M, ("v", "p4av", 7, 1, 7))
                            s.op("activation", out=sc("sd"),
                                 in_=st_[:, SS2 + ti:SS2 + ti + 1],
                                 func=AF.Sqrt, bias=0.0, scale=1.0)
                            s.mark(M, ("s", "p4bsqrt", th, tl))
                            for ch in range(8):
                                s.wait(M, ("q", "p4bh", th, tl, ch))
                                s.wait(M, ("v", "p4bmu", th, tl))
                                u = (th * 4 + tl) * 8 + ch
                                if u >= 1:
                                    pu = u - 1
                                    s.wait(M, ("v", "p4bround",
                                               pu // 32, (pu // 8) % 4, pu % 8))
                                s.op("activation", out=ybuf[:],
                                     in_=hrow[:, (ch % 2) * 1024:
                                             (ch % 2) * 1024 + 1024],
                                     func=AF.Copy,
                                     scale=st_[:, MU2 + ti:MU2 + ti + 1],
                                     bias=0.0)
                                s.mark(M, ("s", "p4bymul", th, tl, ch))
                        for d in range(4):
                            for tl in range(4):
                                ti = th * 4 + tl
                                bank = ps[(d * 4 + tl) % 6]
                                s.wait(M, ("t", "p4bmm", th, d, tl))
                                u = d * 4 + tl
                                if u >= 1 or th > 0:
                                    pu = u - 1 if u >= 1 else 15
                                    pth = th if u >= 1 else th - 1
                                    s.wait(M, ("v", "p4badd",
                                               pth, pu // 4, pu % 4))
                                s.op("activation", out=tmp5[:], in_=bank[:],
                                     func=AF.Copy,
                                     scale=st_[:, CS2 + ti:CS2 + ti + 1],
                                     bias=0.0)
                                s.mark(M, ("s", "p4bcp", th, d, tl))
                sec("s", p4b_s)

                def p4b_v(v):
                    for th in range(2):
                        for tl in range(4):
                            ti = th * 4 + tl
                            v.wait(M, ("s", "p4bsqrt", th, tl))
                            v.op("tensor_scalar", out=sc("n1"), in0=sc("sd"),
                                 scalar1=EPS_RMS, scalar2=None, op0=ALU.max)
                            v.wait_self()
                            v.op("reciprocal", out=sc("rc"), in_=sc("n1"))
                            v.wait_self()
                            v.op("tensor_scalar", out=sc("rr"), in0=sc("rc"),
                                 scalar1=SQI, scalar2=None, op0=ALU.mult)
                            v.wait_self()
                            v.op("tensor_tensor", out=sc("mx"),
                                 in0=st_[:, MX2 + ti:MX2 + ti + 1],
                                 in1=sc("rr"), op=ALU.mult)
                            v.wait_self()
                            v.op("tensor_scalar", out=sc("mxc"), in0=sc("mx"),
                                 scalar1=EPS_Q, scalar2=None, op0=ALU.max)
                            v.wait_self()
                            v.op("reciprocal", out=sc("si"), in_=sc("mxc"))
                            v.wait_self()
                            v.op("scalar_tensor_tensor",
                                 out=st_[:, MU2 + ti:MU2 + ti + 1], in0=sc("rr"),
                                 scalar=127.0, in1=sc("si"), op0=ALU.mult,
                                 op1=ALU.mult)
                            v.op("scalar_tensor_tensor",
                                 out=st_[:, CS2 + ti:CS2 + ti + 1],
                                 in0=sc("mxc"), scalar=-1.0 / 127.0,
                                 in1=sc("sw2m"), op0=ALU.mult, op1=ALU.mult)
                            v.wait_self()
                            v.mark(M, ("v", "p4bmu", th, tl))
                            for ch in range(8):
                                v.wait(M, ("s", "p4bymul", th, tl, ch))
                                if th == 1:
                                    v.wait(M, ("t", "p4bmm", 0, 3, 3))
                                emit_round(v, xq2b[:], ybuf[:], tA[:], tB[:],
                                           tS[:])
                                v.wait_self()
                                v.mark(M, ("v", "p4bround", th, tl, ch))
                                for k in range(8):
                                    v.wait(M, ("t", "p4btr", th, tl, ch, k))
                                    kk = ch * 8 + k
                                    v.op("tensor_copy",
                                         out=xq2T[:, kk * 512 + tl * 128:
                                                 kk * 512 + (tl + 1) * 128],
                                         in_=pbf[k % 2][:, 128:256])
                                    v.wait_self()
                                    v.mark(M, ("v", "p4bxqev", th, tl, ch, k))
                        for d in range(4):
                            for tl in range(4):
                                v.wait(M, ("s", "p4bcp", th, d, tl))
                                v.wait(M, ("q", "p4bxr", th, d, tl))
                                u = d * 4 + tl
                                if u >= 2 or th > 0:
                                    pu = u - 2 if u >= 2 else 14 + u
                                    pth = th if u >= 2 else th - 1
                                    v.wait(M, ("g", "p4by",
                                               pth, pu // 4, pu % 4))
                                v.op("tensor_tensor",
                                     out=yo[:, (tl % 2) * 512:(tl % 2) * 512 + 512],
                                     in0=tmp5[:],
                                     in1=xr2[:, (tl % 2) * 512:(tl % 2) * 512 + 512],
                                     op=ALU.add)
                                v.wait_self()
                                v.mark(M, ("v", "p4badd", th, d, tl))
                sec("v", p4b_v)

                def p4b_t(t):
                    for th in range(2):
                        for tl in range(4):
                            for ch in range(8):
                                t.wait(M, ("v", "p4bround", th, tl, ch))
                                for k in range(8):
                                    u = ((th * 4 + tl) * 8 + ch) * 8 + k
                                    if u >= 2:
                                        pu = u - 2
                                        t.wait(M, ("v", "p4bxqev",
                                                   pu // 256, (pu // 64) % 4,
                                                   (pu // 8) % 8, pu % 8))
                                    t.op("transpose", out=pbf[k % 2][:, 128:256],
                                         in_=xq2b[:, k * 128:(k + 1) * 128],
                                         identity=sidb[:])
                                    t.mark(M, ("t", "p4btr", th, tl, ch, k))
                        for d in range(4):
                            for tl in range(4):
                                bank = ps[(d * 4 + tl) % 6]
                                u = d * 4 + tl
                                if u >= 6 or th > 0:
                                    pu = u - 6 if u >= 6 else 10 + u
                                    pth = th if u >= 6 else th - 1
                                    t.wait(M, ("s", "p4bcp", pth, pu // 4, pu % 4))
                                t.wait(M, ("v", "p4bxqev", th, 3, 7, 7))
                                for c in range(64):
                                    if tl == 0 and c % 16 == 0:
                                        t.wait(M, ("q", "p4bwq", th, d, c // 16))
                                    t.op("matmul", out=bank[:],
                                         lhsT=xq2T[:, c * 512 + tl * 128:
                                                  c * 512 + (tl + 1) * 128],
                                         rhs=w2sb[:, c * 512:(c + 1) * 512],
                                         start=(c == 0), stop=(c == 63))
                                t.mark(M, ("t", "p4bmm", th, d, tl))
                sec("t", p4b_t)

                def p4b_g(g):
                    for th in range(2):
                        for d in range(4):
                            for tl in range(4):
                                ti = th * 4 + tl
                                g.wait(M, ("v", "p4badd", th, d, tl))
                                g.dma(y_out[ti * 128:(ti + 1) * 128,
                                            d * 512:(d + 1) * 512],
                                      yo[:, (tl % 2) * 512:(tl % 2) * 512 + 512])
                                g.mark(M, ("g", "p4by", th, d, tl))
                    if mode == "emit":
                        g.wait_total()
                sec("g", p4b_g)

            if part == "A":
                def pa_fin_g(g):
                    if mode == "emit":
                        g.wait_total()
                sec("g", pa_fin_g)

        M = {}
        CE = {k: CEng(16 if k in ("q", "g", "p") else 1, k)
              for k in ("q", "g", "p", "t", "v", "s", "cc")}
        block = None
        program(CE, M, "plan")

        sems = {}
        for k in ("q", "g", "p"):
            sems[k] = [ctx.enter_context(nc.semaphore(f"sem{k}{i}"))
                       for i in (0, 1)]
        for k in ("t", "v", "s", "cc"):
            sems[k] = [ctx.enter_context(nc.semaphore(f"sem{k}"))]
        RE = {k: REng(16 if k in ("q", "g", "p") else 1, sems[k], k)
              for k in ("q", "g", "p", "t", "v", "s", "cc")}
        M["_sems"] = sems
        block = ctx.enter_context(nc.Block())
        program(RE, M, "emit")
    return nc


def kernel(L0_attn_x, L0_attn_Wq, L0_attn_Wk, L0_attn_Wv, L0_attn_Wo,
           L0_ffn_w1, L0_ffn_w2):
    x = np.ascontiguousarray(np.asarray(L0_attn_x, dtype=np.float32))
    WqT = np.asarray(L0_attn_Wq, np.float32).T
    WkT = np.asarray(L0_attn_Wk, np.float32).T
    WvT = np.asarray(L0_attn_Wv, np.float32).T
    WoT_f = np.asarray(L0_attn_Wo, np.float32).T
    WoT = np.ascontiguousarray(
        WoT_f.reshape(16, 128, 8, 256).transpose(2, 1, 0, 3).reshape(
            8, 128, 16 * 256))
    w1T = np.ascontiguousarray(np.asarray(L0_ffn_w1, np.float32).T)
    w2T = np.ascontiguousarray(np.asarray(L0_ffn_w2, np.float32).T)
    identf = np.eye(128, dtype=np.float32)
    maskd = np.triu(np.full((128, 128), NEG, dtype=np.float32), 1)
    mask01 = np.triu(np.ones((128, 128), dtype=np.float32))

    out = np.empty((B, T, D), dtype=np.float32)
    if FUSED:
        nc = build("ALL")
        in_maps = []
        for c in range(8):
            b, gq = c // 4, c % 4
            bm = np.zeros((128, 2), np.float32)
            bm[:, b] = 1.0
            in_maps.append({
                "x": np.ascontiguousarray(x[b]),
                "identf": identf, "maskd": maskd, "mask01": mask01, "bmask": bm,
                "wqT": np.ascontiguousarray(WqT[:, gq * DPC:(gq + 1) * DPC]),
                "wkT": np.ascontiguousarray(WkT[:, gq * DPC:(gq + 1) * DPC]),
                "wvT": np.ascontiguousarray(WvT[:, gq * DPC:(gq + 1) * DPC]),
                "xres": np.ascontiguousarray(x[b, gq * TQ:(gq + 1) * TQ]),
                "woT": WoT, "w1T": w1T, "w2T": w2T,
            })
        res = run_bass_kernel_spmd(nc, in_maps, core_ids=list(range(8)))
        for c in range(8):
            b, tq = c // 4, c % 4
            out[b, tq * TQ:(tq + 1) * TQ] = res.results[c]["y"]
    else:
        ncA = build("A")
        in_mapsA = []
        for c in range(8):
            b, gq = c // 4, c % 4
            in_mapsA.append({
                "x": np.ascontiguousarray(x[b]),
                "identf": identf, "maskd": maskd, "mask01": mask01,
                "wqT": np.ascontiguousarray(WqT[:, gq * DPC:(gq + 1) * DPC]),
                "wkT": np.ascontiguousarray(WkT[:, gq * DPC:(gq + 1) * DPC]),
                "wvT": np.ascontiguousarray(WvT[:, gq * DPC:(gq + 1) * DPC]),
            })
        resA = run_bass_kernel_spmd(ncA, in_mapsA, core_ids=list(range(8)))
        ncB = build("B")
        in_mapsB = []
        for c in range(8):
            b, tq = c // 4, c % 4
            bm = np.zeros((128, 2), np.float32)
            bm[:, b] = 1.0
            zq = np.stack([np.asarray(resA.results[(s // 4) * 4 + s % 4]
                                      ["zparts"])[c]
                           for s in range(8)])
            in_mapsB.append({
                "x": np.ascontiguousarray(x[b]),
                "identf": identf, "bmask": bm,
                "xres": np.ascontiguousarray(x[b, tq * TQ:(tq + 1) * TQ]),
                "woT": WoT, "w1T": w1T, "w2T": w2T,
                "zq_in": np.ascontiguousarray(zq),
            })
        resB = run_bass_kernel_spmd(ncB, in_mapsB, core_ids=list(range(8)))
        for c in range(8):
            b, tq = c // 4, c % 4
            out[b, tq * TQ:(tq + 1) * TQ] = resB.results[c]["y"]
    return out
